# version 02
# baseline (speedup 1.0000x reference)
import numpy as np
from contextlib import ExitStack

import concourse.bass as bass
import concourse.mybir as mybir
from concourse.bass import IndirectOffsetOnAxis
from concourse.bass_utils import run_bass_kernel_spmd

F32 = mybir.dt.float32
BF16 = mybir.dt.bfloat16
I32 = mybir.dt.int32
U32 = mybir.dt.uint32
AF = mybir.ActivationFunctionType
ALU = mybir.AluOpType
AX = mybir.AxisListType

S = 4096
D = 1024
FF = 4096
NT = S // 128
NC = D // 128
NF = FF // 128

ENG_CH = {"sync": "ld", "tensor": "pe", "vector": "ve",
          "scalar": "ac", "gpsimd": "gp"}

_UID = [0]


def _u(base):
    _UID[0] += 1
    return f"{base}_{_UID[0]}"


class Prog:
    def __init__(self):
        self.nc = bass.Bass("TRN2", target_bir_lowering=False)
        self.ctx = ExitStack()
        self.sems = {}
        self.counts = {}
        self.blk = None
        for ch in ("ld", "pe", "ve", "ac", "gp", "st"):
            self.sem(ch)

    def sem(self, name):
        if name not in self.sems:
            self.sems[name] = self.ctx.enter_context(self.nc.semaphore(name))
            self.counts[name] = 0
        return self.sems[name]

    def inc(self, name, instr, k=1):
        instr.then_inc(self.sem(name), k)
        self.counts[name] += k
        return instr

    def body(self, eng, fn):
        vals = {n: v for n, v in self.counts.items() if v > 0}
        ch = ENG_CH[eng]

        def run(e):
            for n, v in vals.items():
                e.wait_ge(self.sems[n], v)
            if eng in ("vector", "scalar", "gpsimd"):
                e.drain()
            last = fn(e)
            if last is not None:
                self.inc(ch, last)
        getattr(self.blk, eng)(run)

    def finish(self):
        vals = {n: v for n, v in self.counts.items() if v > 0}

        def run(e):
            for n, v in vals.items():
                e.wait_ge(self.sems[n], v)
        self.blk.sync(run)

    def sb(self, name, shape, dtype, **kw):
        return self.ctx.enter_context(self.nc.sbuf_tensor(_u(name), shape, dtype, **kw))

    def ps(self, name, shape, dtype=F32):
        return self.ctx.enter_context(self.nc.psum_tensor(_u(name), shape, dtype))


def load_wT(p, wT_sb, w_bf, rows, chunks):
    def fn(e):
        for c in range(chunks):
            p.inc("ld", e.dma_start(out=wT_sb[:, c * rows:(c + 1) * rows],
                                    in_=w_bf[:, c * 128:(c + 1) * 128],
                                    transpose=True), 16)
    p.body("sync", fn)


def emit_consts(p):
    ones = p.sb("ones", [1, 512], BF16)
    onesf = p.sb("onesf", [1, 128], F32)
    onecolf = p.sb("onecolf", [128, 1], F32)
    identity = p.sb("identity", [128, 128], BF16)
    constf = p.sb("constf", [128, 4], F32)
    p.constf = constf

    def fn(e):
        e.memset(ones[:], 1.0)
        e.memset(onesf[:], 1.0)
        e.memset(onecolf[:], 1.0)
        e.memset(constf[:, 0:1], 1e-5)
        e.memset(constf[:, 1:2], float(1.0 / np.sqrt(512.0)))
        e.memset(constf[:, 2:3], float(1.0 / np.sqrt(64.0)))
        e.memset(identity[:], 0.0)
        return e.affine_select(out=identity[:], in_=identity[:],
                               compare_op=ALU.not_equal, fill=1.0, base=0,
                               pattern=[[-1, 128]], channel_multiplier=1)
    p.body("gpsimd", fn)
    return ones, onesf, onecolf, identity



def emit_ln_transpose(p, x_d, xnfm_d, identity, xnrow_d=None):
    nc = p.nc
    with ExitStack() as ctx:
        e_ = ctx.enter_context
        xt = [e_(nc.sbuf_tensor(_u(f"ln_x{i}"), [128, D], F32)) for i in range(2)]
        xnb = [e_(nc.sbuf_tensor(_u(f"ln_n{i}"), [128, D], BF16)) for i in range(2)]
        xnT = [e_(nc.sbuf_tensor(_u(f"ln_t{i}"), [128, D], BF16)) for i in range(2)]
        sq = e_(nc.sbuf_tensor(_u("ln_sq"), [128, D], F32))
        st = e_(nc.sbuf_tensor(_u("ln_st"), [128, 8], F32))
        ptT = e_(nc.psum_tensor(_u("ln_pt"), [128, D], BF16))

        for i in range(NT):
            par = i % 2

            def ld_fn(e, i=i, par=par):
                p.inc("ld", e.dma_start(out=xt[par][:],
                                        in_=x_d[i * 128:(i + 1) * 128, :]), 16)
            p.body("sync", ld_fn)

            def sq_fn(e, par=par):
                return e.activation(sq[:], xt[par][:], AF.Square,
                                    accum_out=st[:, 0:1])
            p.body("scalar", sq_fn)

            def ve_fn(e, par=par):
                e.reduce_sum(st[:, 2:3], xt[par][:], axis=AX.X)
                e.drain()
                e.tensor_scalar(st[:, 3:4], st[:, 2:3], 1.0 / D, None, ALU.mult)
                e.tensor_scalar(st[:, 4:5], st[:, 0:1], 1.0 / D, None, ALU.mult)
                e.drain()
                e.tensor_tensor(st[:, 6:7], st[:, 3:4], st[:, 3:4], ALU.mult)
                e.drain()
                return e.tensor_sub(st[:, 4:5], st[:, 4:5], st[:, 6:7])
            p.body("vector", ve_fn)

            def sqrt_fn(e):
                return e.activation(st[:, 5:6], st[:, 4:5], AF.Sqrt,
                                    bias=p.constf[:, 0:1])
            p.body("scalar", sqrt_fn)

            def norm_fn(e, par=par):
                e.reciprocal(st[:, 6:7], st[:, 5:6])
                e.drain()
                return e.tensor_scalar(xnb[par][:], xt[par][:], st[:, 3:4],
                                       st[:, 6:7], ALU.subtract, ALU.mult)
            p.body("vector", norm_fn)

            def tr_fn(e, par=par):
                ins = None
                for c in range(NC):
                    ins = e.transpose(ptT[:, c * 128:(c + 1) * 128],
                                      xnb[par][:, c * 128:(c + 1) * 128],
                                      identity[:])
                return ins
            p.body("tensor", tr_fn)

            def ev_fn(e, par=par):
                return e.activation(xnT[par][:], ptT[:], AF.Copy)
            p.body("scalar", ev_fn)

            def stx_fn(e, i=i, par=par):
                p.inc("st", e.dma_start(
                    out=xnfm_d[:, :, i * 128:(i + 1) * 128],
                    in_=xnT[par][:].rearrange("q (c t) -> q c t", c=NC)), 16)
                if xnrow_d is not None:
                    p.inc("st", e.dma_start(
                        out=xnrow_d[i * 128:(i + 1) * 128, :],
                        in_=xnb[par][:]), 16)
            p.body("gpsimd", stx_fn)


def emit_proj_fm(p, wT_sb, xnfm_sb, dest_d, b_row, ones, ncol, scale=None):
    nc = p.nc
    nd = ncol // 128
    with ExitStack() as ctx:
        e_ = ctx.enter_context
        ot = [e_(nc.sbuf_tensor(_u(f"pj_o{i}"), [128, 512], BF16)) for i in range(2)]
        psu = [e_(nc.psum_tensor(_u(f"pj_p{i}"), [128, 512], F32)) for i in range(2)]
        k = 0
        for dc in range(nd):
            for tp in range(S // 512):
                par = k % 2

                def pe_fn(e, dc=dc, tp=tp, par=par):
                    for c in range(NC):
                        ins = e.matmul(
                            psu[par][:, 0:512],
                            wT_sb[:, c * ncol + dc * 128:c * ncol + (dc + 1) * 128],
                            xnfm_sb[:, c * S + tp * 512:c * S + (tp + 1) * 512],
                            start=(c == 0), stop=(b_row is None and c == NC - 1))
                    if b_row is not None:
                        ins = e.matmul(psu[par][:, 0:512],
                                       b_row[:, dc * 128:(dc + 1) * 128],
                                       ones[:, 0:512], start=False, stop=True)
                    return ins
                p.body("tensor", pe_fn)

                def ac_fn(e, par=par):
                    return e.activation(ot[par][:], psu[par][:, 0:512], AF.Copy,
                                        scale=(scale if scale else 1.0))
                p.body("scalar", ac_fn)

                def st_fn(e, dc=dc, tp=tp, par=par):
                    p.inc("st", e.dma_start(
                        out=dest_d[:, dc, tp * 512:(tp + 1) * 512],
                        in_=ot[par][:]), 16)
                p.body("gpsimd", st_fn)
                k += 1


def emit_proj_tm(p, wT_sb, xnfm_sb, dest_d, b_row, ones, ncol):
    nc = p.nc
    with ExitStack() as ctx:
        e_ = ctx.enter_context
        ot = [e_(nc.sbuf_tensor(_u(f"pv_o{i}"), [128, ncol], BF16)) for i in range(2)]
        psu = [e_(nc.psum_tensor(_u(f"pv_p{i}"), [128, 512], F32)) for i in range(2)]
        for i in range(NT):
            par = i % 2

            def pe_fn(e, i=i, par=par):
                for half in range(ncol // 512):
                    for c in range(NC):
                        ins = e.matmul(
                            psu[half][:, 0:512],
                            xnfm_sb[:, c * S + i * 128:c * S + (i + 1) * 128],
                            wT_sb[:, c * ncol + half * 512:c * ncol + (half + 1) * 512],
                            start=(c == 0), stop=(b_row is None and c == NC - 1))
                    if b_row is not None:
                        ins = e.matmul(psu[half][:, 0:512], ones[0:1, 0:128],
                                       b_row[:, half * 512:(half + 1) * 512],
                                       start=False, stop=True)
                return ins
            p.body("tensor", pe_fn)

            def ac_fn(e, par=par):
                for half in range(ncol // 512):
                    e.drain()
                    ins = e.activation(ot[par][:, half * 512:(half + 1) * 512],
                                       psu[half][:, 0:512], AF.Copy)
                return ins
            p.body("scalar", ac_fn)

            def st_fn(e, i=i, par=par):
                p.inc("st", e.dma_start(out=dest_d[i * 128:(i + 1) * 128, :],
                                        in_=ot[par][:]), 16)
            p.body("gpsimd", st_fn)


def emit_quant(p, w_f32, w_q_bf, rows, cols, onesf, onecolf, tag):
    nc = p.nc
    ntile = rows // 128
    with ExitStack() as ctx:
        e_ = ctx.enter_context
        wt = [e_(nc.sbuf_tensor(_u(f"qz_w{tag}{i}"), [128, cols], F32))
              for i in range(2)]
        wq = e_(nc.sbuf_tensor(_u(f"qz_q{tag}"), [128, cols], BF16))
        q1 = e_(nc.sbuf_tensor(_u(f"qz_1{tag}"), [128, cols], F32))
        q2 = e_(nc.sbuf_tensor(_u(f"qz_2{tag}"), [128, cols], F32))
        absn = e_(nc.sbuf_tensor(_u(f"qz_a{tag}"), [128, ntile], F32))
        gcol = e_(nc.sbuf_tensor(_u(f"qz_g{tag}"), [128, 4], F32))
        gps = e_(nc.psum_tensor(_u(f"qz_p{tag}"), [128, 2], F32))

        for i in range(ntile):
            def ld_fn(e, i=i):
                p.inc("ld", e.dma_start(out=wt[i % 2][:],
                                        in_=w_f32[i * 128:(i + 1) * 128, :]), 16)
            p.body("sync", ld_fn)

            def ve_fn(e, i=i):
                return e.reduce_sum(absn[:, i:i + 1], wt[i % 2][:], axis=AX.X,
                                    apply_absolute_value=True)
            p.body("vector", ve_fn)

        def ve_tot(e):
            return e.reduce_sum(gcol[:, 0:1], absn[:, 0:ntile], axis=AX.X)
        p.body("vector", ve_tot)

        def pe_tot(e):
            return e.matmul(gps[0:1, 0:1], gcol[:, 0:1], onecolf[:],
                            start=True, stop=True)
        p.body("tensor", pe_tot)

        def ve_g(e):
            e.tensor_scalar(gcol[0:1, 1:2], gps[0:1, 0:1], 1.0 / (rows * cols),
                            1e-5, ALU.mult, ALU.add)
            e.drain()
            return e.reciprocal(gcol[0:1, 2:3], gcol[0:1, 1:2])
        p.body("vector", ve_g)

        def pe_b(e):
            return e.matmul(gps[:, 1:2], onesf[:], gcol[0:1, 2:3],
                            start=True, stop=True)
        p.body("tensor", pe_b)

        def ve_bc(e):
            return e.tensor_copy(gcol[:, 3:4], gps[:, 1:2])
        p.body("vector", ve_bc)

        for i in range(ntile):
            def ld_fn(e, i=i):
                p.inc("ld", e.dma_start(out=wt[i % 2][:],
                                        in_=w_f32[i * 128:(i + 1) * 128, :]), 16)
            p.body("sync", ld_fn)

            def ve_fn(e, i=i):
                e.tensor_scalar(q1[:], wt[i % 2][:], gcol[:, 3:4], None,
                                ALU.mult)
                e.drain()
                e.tensor_scalar(wq[:], q1[:], 0.5, None, ALU.is_gt)
                e.tensor_scalar(q2[:], q1[:], -0.5, None, ALU.is_lt)
                e.drain()
                return e.tensor_sub(wq[:], wq[:], q2[:])
            p.body("vector", ve_fn)

            def st_fn(e, i=i):
                p.inc("st", e.dma_start(out=w_q_bf[i * 128:(i + 1) * 128, :],
                                        in_=wq[:]), 16)
            p.body("gpsimd", st_fn)



def build_l0(p, io):
    nc = p.nc
    x = io["x"]
    r1 = io["r1"]
    xnfm_d = io["xnfm"]

    wq_bf = nc.dram_tensor("wq_bf", [D, D], BF16, kind="Internal").ap()
    wk_bf = nc.dram_tensor("wk_bf", [D, D], BF16, kind="Internal").ap()
    wv_bf = nc.dram_tensor("wv_bf", [D, D], BF16, kind="Internal").ap()
    wo_bf = nc.dram_tensor("wo_bf", [D, D], BF16, kind="Internal").ap()
    w1q_bf = nc.dram_tensor("w1q_bf", [FF, D], BF16, kind="Internal").ap()
    w2q_bf = nc.dram_tensor("w2q_bf", [D, FF], BF16, kind="Internal").ap()
    qfm_d = nc.dram_tensor("qfm_d", [128, NC, S], BF16, kind="Internal").ap()
    kfm_d = nc.dram_tensor("kfm_d", [128, NC, S], BF16, kind="Internal").ap()
    vtm_d = nc.dram_tensor("vtm_d", [S, D], BF16, kind="Internal").ap()
    racc_d = nc.dram_tensor("racc_d", [S, D], F32, kind="Internal").ap()

    ones, onesf, onecolf, identity = emit_consts(p)

    def cast_fn(e):
        for dst, src in ((wq_bf, io["Wq"]), (wk_bf, io["Wk"]),
                         (wv_bf, io["Wv"]), (wo_bf, io["Wo"])):
            p.inc("gp", e.dma_start(out=dst, in_=src), 16)
    p.body("gpsimd", cast_fn)

    brow = p.sb("brow", [1, 4 * D + FF + D], BF16)
    bq_r = brow[:, 0:D]
    bk_r = brow[:, D:2 * D]
    bv_r = brow[:, 2 * D:3 * D]
    bo_r = brow[:, 3 * D:4 * D]
    b1_r = brow[:, 4 * D:4 * D + FF]
    b2_r = brow[:, 4 * D + FF:4 * D + FF + D]

    def bias_fn(e):
        for dst, src in ((bq_r, io["bq"]), (bk_r, io["bk"]), (bv_r, io["bv"]),
                         (bo_r, io["bo"]), (b1_r, io["b1"]), (b2_r, io["b2"])):
            p.inc("gp", e.dma_start(out=dst, in_=src), 16)
    p.body("gpsimd", bias_fn)

    emit_ln_transpose(p, x, xnfm_d, identity)
    emit_quant(p, io["W1"], w1q_bf, FF, D, onesf, onecolf, "a")
    emit_quant(p, io["W2"], w2q_bf, D, FF, onesf, onecolf, "b")

    with ExitStack() as ctx:
        e_ = ctx.enter_context
        xnfm = e_(nc.sbuf_tensor(_u("xnfm_sb"), [128, NC * S], BF16))
        wT = e_(nc.sbuf_tensor(_u("wT"), [128, NC * D], BF16))

        def ld_xn(e):
            p.inc("ld", e.dma_start(out=xnfm[:],
                                    in_=xnfm_d.rearrange("q c t -> q (c t)")), 16)
        p.body("sync", ld_xn)

        load_wT(p, wT, wk_bf, D, NC)
        emit_proj_fm(p, wT, xnfm, kfm_d, bk_r, ones, D)
        load_wT(p, wT, wq_bf, D, NC)
        emit_proj_fm(p, wT, xnfm, qfm_d, bq_r, ones, D)
        load_wT(p, wT, wv_bf, D, NC)
        emit_proj_tm(p, wT, xnfm, vtm_d, bv_r, ones, D)

    with ExitStack() as ctx:
        e_ = ctx.enter_context
        kfm = e_(nc.sbuf_tensor(_u("kfm"), [128, NC * S], BF16))
        vtm = e_(nc.sbuf_tensor(_u("vtm"), [128, NT * D], BF16))
        woT = e_(nc.sbuf_tensor(_u("woT"), [128, NC * D], BF16))
        qt = [e_(nc.sbuf_tensor(_u(f"qt{i}"), [128, D], BF16)) for i in range(2)]
        s_sb = e_(nc.sbuf_tensor(_u("s_sb"), [128, S], BF16))
        p_sb = e_(nc.sbuf_tensor(_u("p_sb"), [128, S], BF16))
        pt_sb = [e_(nc.sbuf_tensor(_u(f"pt{i}"), [128, 128], BF16)) for i in range(2)]
        ao_sb = e_(nc.sbuf_tensor(_u("ao_sb"), [128, D], BF16))
        o_sb = [e_(nc.sbuf_tensor(_u(f"o_sb{i}"), [128, D], F32)) for i in range(2)]
        xres = [e_(nc.sbuf_tensor(_u(f"xres{i}"), [128, D], F32)) for i in range(2)]
        cstat = e_(nc.sbuf_tensor(_u("cstat"), [128, 4], F32))
        ps_s = e_(nc.psum_tensor(_u("ps_s"), [128, 512], F32))
        ps_pt = e_(nc.psum_tensor(_u("ps_pt"), [128, 256], BF16))
        ps_ao = e_(nc.psum_tensor(_u("ps_ao"), [128, 1024], F32))
        ps_o = e_(nc.psum_tensor(_u("ps_o"), [128, 1024], F32))

        def ld0(e):
            p.inc("ld", e.dma_start(out=kfm[:],
                                    in_=kfm_d.rearrange("q c t -> q (c t)")), 16)
            p.inc("ld", e.dma_start(
                out=vtm[:].rearrange("q (n d) -> q n d", n=NT),
                in_=vtm_d.rearrange("(n q) d -> q n d", q=128)), 16)
        p.body("sync", ld0)
        load_wT(p, woT, wo_bf, D, NC)

        for i in range(NT):
            par = i % 2

            def ld_fn(e, i=i, par=par):
                p.inc("ld", e.dma_start(
                    out=qt[par][:].rearrange("q (c t) -> q c t", c=NC),
                    in_=qfm_d[:, :, i * 128:(i + 1) * 128]), 16)
                p.inc("ld", e.dma_start(out=xres[par][:],
                                        in_=x[i * 128:(i + 1) * 128, :]), 16)
            p.body("sync", ld_fn)

            for kp in range(NC):
                def pe_s(e, par=par, kp=kp):
                    ins = None
                    for c in range(NC):
                        ins = e.matmul(
                            ps_s[:, 0:512],
                            qt[par][:, c * 128:(c + 1) * 128],
                            kfm[:, c * S + kp * 512:c * S + (kp + 1) * 512],
                            start=(c == 0), stop=(c == NC - 1))
                    return ins
                p.body("tensor", pe_s)

                def ac_s(e, kp=kp):
                    return e.activation(s_sb[:, kp * 512:(kp + 1) * 512],
                                        ps_s[:, 0:512], AF.Copy,
                                        scale=p.constf[:, 1:2])
                p.body("scalar", ac_s)

            def ve_sm(e):
                return e.reduce_max(cstat[:, 0:1], s_sb[:], axis=AX.X,
                                    negate=True)
            p.body("vector", ve_sm)

            def ac_sm(e):
                return e.activation(p_sb[:], s_sb[:], AF.Exp,
                                    bias=cstat[:, 0:1], accum_out=cstat[:, 1:2])
            p.body("scalar", ac_sm)

            def ve_sm2(e):
                e.reciprocal(cstat[:, 2:3], cstat[:, 1:2])
                e.drain()
                return e.tensor_scalar(s_sb[:], p_sb[:], cstat[:, 2:3], None,
                                       ALU.mult)
            p.body("vector", ve_sm2)

            for kc in range(NT):
                ptp = kc % 2

                def pe_pt(e, kc=kc, ptp=ptp):
                    return e.transpose(ps_pt[:, ptp * 128:(ptp + 1) * 128],
                                       s_sb[:, kc * 128:(kc + 1) * 128],
                                       identity[:])
                p.body("tensor", pe_pt)

                def ve_pt(e, ptp=ptp):
                    return e.tensor_copy(pt_sb[ptp][:],
                                         ps_pt[:, ptp * 128:(ptp + 1) * 128])
                p.body("vector", ve_pt)

                def pe_pv(e, kc=kc, ptp=ptp):
                    ins = None
                    for dc in range(NC):
                        ins = e.matmul(
                            ps_ao[:, dc * 128:(dc + 1) * 128],
                            vtm[:, kc * D + dc * 128:kc * D + (dc + 1) * 128],
                            pt_sb[ptp][:],
                            start=(kc == 0 and dc % 4 == 0),
                            stop=(kc == NT - 1))
                    return ins
                p.body("tensor", pe_pv)

            def ac_ao(e):
                return e.activation(ao_sb[:], ps_ao[:], AF.Copy)
            p.body("scalar", ac_ao)

            def pe_o(e):
                ins = None
                for half in range(2):
                    for c in range(NC):
                        e.matmul(ps_o[:, half * 512:(half + 1) * 512],
                                 ao_sb[:, c * 128:(c + 1) * 128],
                                 woT[:, c * D + half * 512:c * D + (half + 1) * 512],
                                 start=(c == 0), stop=False)
                    ins = e.matmul(ps_o[:, half * 512:(half + 1) * 512],
                                   ones[0:1, 0:128],
                                   bo_r[:, half * 512:(half + 1) * 512],
                                   start=False, stop=True)
                return ins
            p.body("tensor", pe_o)

            def ve_r(e, par=par):
                return e.tensor_add(o_sb[par][:], ps_o[:], xres[par][:])
            p.body("vector", ve_r)

            def st_r(e, i=i, par=par):
                p.inc("st", e.dma_start(out=racc_d[i * 128:(i + 1) * 128, :],
                                        in_=o_sb[par][:]), 16)
            p.body("gpsimd", st_r)

    with ExitStack() as ctx:
        e_ = ctx.enter_context
        w1T = e_(nc.sbuf_tensor(_u("w1T"), [128, NC * FF], BF16))
        w2T = e_(nc.sbuf_tensor(_u("w2T"), [128, NF * D], BF16))
        xnt = [e_(nc.sbuf_tensor(_u(f"fx{i}"), [128, D], BF16)) for i in range(2)]
        hfm = e_(nc.sbuf_tensor(_u("hfm"), [128, FF], BF16))
        rres = [e_(nc.sbuf_tensor(_u(f"fr{i}"), [128, D], F32)) for i in range(2)]
        fout = [e_(nc.sbuf_tensor(_u(f"fo{i}"), [128, D], F32)) for i in range(2)]
        ps_h = [e_(nc.psum_tensor(_u(f"ps_h{i}"), [128, 128], F32)) for i in range(2)]
        ps_f = e_(nc.psum_tensor(_u("ps_f"), [128, 1024], F32))

        load_wT(p, w1T, w1q_bf, FF, NC)
        load_wT(p, w2T, w2q_bf, D, NF)

        for i in range(NT):
            par = i % 2

            def ld_fn(e, i=i, par=par):
                p.inc("ld", e.dma_start(
                    out=xnt[par][:].rearrange("q (c t) -> q c t", c=NC),
                    in_=xnfm_d[:, :, i * 128:(i + 1) * 128]), 16)
                p.inc("ld", e.dma_start(out=rres[par][:],
                                        in_=racc_d[i * 128:(i + 1) * 128, :]), 16)
            p.body("sync", ld_fn)

            for fc in range(NF):
                fp = fc % 2

                def pe_h(e, par=par, fc=fc, fp=fp):
                    for c in range(NC):
                        e.matmul(ps_h[fp][:, 0:128],
                                 w1T[:, c * FF + fc * 128:c * FF + (fc + 1) * 128],
                                 xnt[par][:, c * 128:(c + 1) * 128],
                                 start=(c == 0), stop=False)
                    return e.matmul(ps_h[fp][:, 0:128],
                                    b1_r[:, fc * 128:(fc + 1) * 128],
                                    ones[:, 0:128], start=False, stop=True)
                p.body("tensor", pe_h)

                def ac_h(e, fc=fc, fp=fp):
                    return e.activation(hfm[:, fc * 128:(fc + 1) * 128],
                                        ps_h[fp][:, 0:128], AF.Relu)
                p.body("scalar", ac_h)

            def pe_f(e):
                ins = None
                for half in range(2):
                    for fc in range(NF):
                        e.matmul(ps_f[:, half * 512:(half + 1) * 512],
                                 hfm[:, fc * 128:(fc + 1) * 128],
                                 w2T[:, fc * D + half * 512:fc * D + (half + 1) * 512],
                                 start=(fc == 0), stop=False)
                    ins = e.matmul(ps_f[:, half * 512:(half + 1) * 512],
                                   ones[0:1, 0:128],
                                   b2_r[:, half * 512:(half + 1) * 512],
                                   start=False, stop=True)
                return ins
            p.body("tensor", pe_f)

            def ve_f(e, par=par):
                return e.tensor_add(fout[par][:], ps_f[:], rres[par][:])
            p.body("vector", ve_f)

            def st_f(e, i=i, par=par):
                p.inc("st", e.dma_start(out=r1[i * 128:(i + 1) * 128, :],
                                        in_=fout[par][:]), 16)
            p.body("gpsimd", st_f)




NH = 16
HD = 64


def build_l1a(p, io):
    nc = p.nc
    r1 = io["r1"]
    r2 = io["r2"]
    xn1fm_d = io["xn1fm"]
    xn1row_d = io["xn1row"]

    wq_bf = nc.dram_tensor("l1wq_bf", [D, D], BF16, kind="Internal").ap()
    wk_bf = nc.dram_tensor("l1wk_bf", [D, D], BF16, kind="Internal").ap()
    wv_bf = nc.dram_tensor("l1wv_bf", [D, D], BF16, kind="Internal").ap()
    wo_bf = nc.dram_tensor("l1wo_bf", [D, D], BF16, kind="Internal").ap()
    qfm_d = nc.dram_tensor("l1qfm_d", [128, NC, S], BF16, kind="Internal").ap()
    kfm_d = nc.dram_tensor("l1kfm_d", [128, NC, S], BF16, kind="Internal").ap()
    vtm_d = nc.dram_tensor("l1vtm_d", [S, D], BF16, kind="Internal").ap()

    ones, onesf, onecolf, identity = emit_consts(p)

    def cast_fn(e):
        for dst, src in ((wq_bf, io["Wq"]), (wk_bf, io["Wk"]),
                         (wv_bf, io["Wv"]), (wo_bf, io["Wo"])):
            p.inc("gp", e.dma_start(out=dst, in_=src), 16)
    p.body("gpsimd", cast_fn)

    mtriT = p.sb("mtriT", [128, 128], BF16)

    def mask_fn(e):
        e.memset(mtriT[:], 0.0)
        return e.affine_select(out=mtriT[:], in_=mtriT[:],
                               compare_op=ALU.is_ge, fill=-1e30, base=0,
                               pattern=[[1, 128]], channel_multiplier=-1)
    p.body("gpsimd", mask_fn)

    emit_ln_transpose(p, r1, xn1fm_d, identity, xnrow_d=xn1row_d)

    with ExitStack() as ctx:
        e_ = ctx.enter_context
        xnfm = e_(nc.sbuf_tensor(_u("l1xn_sb"), [128, NC * S], BF16))
        wT = e_(nc.sbuf_tensor(_u("l1wT"), [128, NC * D], BF16))

        def ld_xn(e):
            p.inc("ld", e.dma_start(out=xnfm[:],
                                    in_=xn1fm_d.rearrange("q c t -> q (c t)")), 16)
        p.body("sync", ld_xn)

        load_wT(p, wT, wk_bf, D, NC)
        emit_proj_fm(p, wT, xnfm, kfm_d, None, ones, D)
        load_wT(p, wT, wq_bf, D, NC)
        emit_proj_fm(p, wT, xnfm, qfm_d, None, ones, D)
        load_wT(p, wT, wv_bf, D, NC)
        emit_proj_tm(p, wT, xnfm, vtm_d, None, ones, D)

    with ExitStack() as ctx:
        e_ = ctx.enter_context
        kfm = e_(nc.sbuf_tensor(_u("l1kfm"), [128, NC * S], BF16))
        vpl = e_(nc.sbuf_tensor(_u("l1vpl"), [128, NT * D], BF16))
        onecb = e_(nc.sbuf_tensor(_u("l1oneb"), [128, 1], BF16))
        woT = e_(nc.sbuf_tensor(_u("l1woT"), [128, NC * D], BF16))
        qt = [e_(nc.sbuf_tensor(_u("l1qt"), [128, D], BF16)) for i in range(2)]
        ptt = e_(nc.sbuf_tensor(_u("l1ptt"), [128, NH * 128], BF16))
        dmask = e_(nc.sbuf_tensor(_u("l1dmask"), [128, NH * 128], F32))
        sums = e_(nc.sbuf_tensor(_u("l1sums"), [128, NH * 2], F32))
        ao_sb = e_(nc.sbuf_tensor(_u("l1ao"), [128, D], BF16))
        aofm = e_(nc.sbuf_tensor(_u("l1aofm"), [128, D], BF16))
        o_sb = [e_(nc.sbuf_tensor(_u("l1o"), [128, D], F32)) for i in range(2)]
        rres = [e_(nc.sbuf_tensor(_u("l1rr"), [128, D], F32)) for i in range(2)]
        ps_st = e_(nc.psum_tensor(_u("l1ps_st"), [128, NH * 128], F32))
        ps_ao = e_(nc.psum_tensor(_u("l1ps_ao"), [128, NH * 64], F32))
        ps_sum = e_(nc.psum_tensor(_u("l1ps_sum"), [128, 16], F32))
        ps_tr = e_(nc.psum_tensor(_u("l1ps_tr"), [128, D], BF16))

        def ld0(e):
            p.inc("ld", e.dma_start(out=kfm[:],
                                    in_=kfm_d.rearrange("q c t -> q (c t)")), 16)
            p.inc("ld", e.dma_start(
                out=vpl[:].rearrange("q (n d) -> q n d", n=NT),
                in_=vtm_d.rearrange("(n q) d -> q n d", q=128)), 16)
        p.body("sync", ld0)

        def ones_fn(e):
            return e.memset(onecb[:], 1.0)
        p.body("gpsimd", ones_fn)

        load_wT(p, woT, wo_bf, D, NC)

        for i in range(NT):
            par = i % 2

            def ld_fn(e, i=i, par=par):
                p.inc("ld", e.dma_start(
                    out=qt[par][:].rearrange("q (c t) -> q c t", c=NC),
                    in_=qfm_d[:, :, i * 128:(i + 1) * 128]), 16)
                p.inc("ld", e.dma_start(out=rres[par][:],
                                        in_=r1[i * 128:(i + 1) * 128, :]), 16)
            p.body("sync", ld_fn)

            def slot(h):
                return (h % 2) * 8 + h // 2

            for kc in range(i + 1):
                def pe_st(e, par=par, kc=kc):
                    ins = None
                    for h in sorted(range(NH), key=lambda h: h % 2):
                        c, b = h // 2, 64 * (h % 2)
                        ins = e.matmul(
                            ps_st[:, slot(h) * 128:(slot(h) + 1) * 128],
                            kfm[b:b + HD, c * S + kc * 128:c * S + (kc + 1) * 128],
                            qt[par][b:b + HD, c * 128:(c + 1) * 128],
                            start=True, stop=True)
                    return ins
                p.body("tensor", pe_st)

                if kc == i:
                    def m_fn(e, kc=kc):
                        ins = None
                        for h in range(NH):
                            ins = e.tensor_tensor(
                                dmask[:, h * 128:(h + 1) * 128],
                                ps_st[:, slot(h) * 128:(slot(h) + 1) * 128],
                                mtriT[:], ALU.add)
                        return ins
                    p.body("vector", m_fn)

                def ac_exp(e, kc=kc, diag=(kc == i)):
                    ins = None
                    for h in range(NH):
                        src = (dmask[:, h * 128:(h + 1) * 128] if diag else
                               ps_st[:, slot(h) * 128:(slot(h) + 1) * 128])
                        ins = e.activation(ptt[:, h * 128:(h + 1) * 128],
                                           src, AF.Exp, scale=p.constf[:, 2:3])
                    return ins
                p.body("scalar", ac_exp)

                def pe_pv(e, kc=kc, first=(kc == 0), last=(kc == i)):
                    ins = None
                    for h in range(NH):
                        e.matmul(
                            ps_ao[:, h * 64:(h + 1) * 64],
                            ptt[:, h * 128:(h + 1) * 128],
                            vpl[:, kc * D + h * 64:kc * D + (h + 1) * 64],
                            start=(first and h % 8 == 0), stop=last)
                        ins = e.matmul(
                            ps_sum[:, h:h + 1],
                            ptt[:, h * 128:(h + 1) * 128], onecb[:],
                            start=(first and h == 0), stop=last)
                    return ins
                p.body("tensor", pe_pv)

            def ve_ao(e):
                e.tensor_copy(sums[:, 0:NH], ps_sum[:])
                e.drain()
                e.reciprocal(sums[:, NH:2 * NH], sums[:, 0:NH])
                e.drain()
                ins = None
                for h in range(NH):
                    ins = e.tensor_scalar(ao_sb[:, h * HD:(h + 1) * HD],
                                          ps_ao[:, h * 64:(h + 1) * 64],
                                          sums[:, NH + h:NH + h + 1], None,
                                          ALU.mult)
                return ins
            p.body("vector", ve_ao)

            if i == 1 and "dbg_ao" in io:
                def dbg_fn(e):
                    p.inc("st", e.dma_start(out=io["dbg_ao"], in_=ao_sb[:]), 16)
                    p.inc("st", e.dma_start(out=io["dbg_sums"], in_=sums[:]), 16)
                p.body("gpsimd", dbg_fn)

            def pe_tr(e):
                ins = None
                for c in range(NC):
                    ins = e.transpose(ps_tr[:, c * 128:(c + 1) * 128],
                                      ao_sb[:, c * 128:(c + 1) * 128],
                                      identity[:])
                return ins
            p.body("tensor", pe_tr)

            def ac_tr(e):
                return e.activation(aofm[:], ps_tr[:], AF.Copy)
            p.body("scalar", ac_tr)

            def pe_o(e):
                ins = None
                for half in range(2):
                    for c in range(NC):
                        ins = e.matmul(
                            ps_st[:, half * 512:(half + 1) * 512],
                            aofm[:, c * 128:(c + 1) * 128],
                            woT[:, c * D + half * 512:c * D + (half + 1) * 512],
                            start=(c == 0), stop=(c == NC - 1))
                return ins
            p.body("tensor", pe_o)

            def ve_r(e, par=par):
                return e.tensor_add(o_sb[par][:], ps_st[:, 0:1024], rres[par][:])
            p.body("vector", ve_r)

            def st_r(e, i=i, par=par):
                p.inc("st", e.dma_start(out=r2[i * 128:(i + 1) * 128, :],
                                        in_=o_sb[par][:]), 16)
            p.body("gpsimd", st_r)


def build_l1a_program():
    p = Prog()
    nc = p.nc
    io = {}
    io["r1"] = nc.dram_tensor("r1", [S, D], F32, kind="ExternalInput").ap()
    for nm in ("Wq", "Wk", "Wv", "Wo"):
        io[nm] = nc.dram_tensor(nm, [D, D], F32, kind="ExternalInput").ap()
    io["r2"] = nc.dram_tensor("r2", [S, D], F32, kind="ExternalOutput").ap()
    io["xn1fm"] = nc.dram_tensor("xn1fm", [128, NC, S], BF16,
                                 kind="ExternalOutput").ap()
    io["xn1row"] = nc.dram_tensor("xn1row", [S, D], BF16,
                                  kind="ExternalOutput").ap()
    io["dbg_ao"] = nc.dram_tensor("dbg_ao", [128, D], BF16,
                                  kind="ExternalOutput").ap()
    io["dbg_sums"] = nc.dram_tensor("dbg_sums", [128, 32], F32,
                                    kind="ExternalOutput").ap()
    with p.ctx:
        p.blk = p.ctx.enter_context(nc.Block())
        build_l1a(p, io)
        p.finish()
    return p


def run_l1a(r1_full, inputs):
    p = build_l1a_program()
    in_maps = []
    for b in range(8):
        m = {"r1": np.ascontiguousarray(r1_full[b], dtype=np.float32)}
        for nm, key in (("Wq", "L1_attn_Wq"), ("Wk", "L1_attn_Wk"),
                        ("Wv", "L1_attn_Wv"), ("Wo", "L1_attn_Wo")):
            m[nm] = np.ascontiguousarray(inputs[key], dtype=np.float32)
        in_maps.append(m)
    res = run_bass_kernel_spmd(p.nc, in_maps, core_ids=list(range(8)))
    r2 = np.stack([r["r2"] for r in res.results], axis=0)
    xn1fm = np.stack([r["xn1fm"] for r in res.results], axis=0)
    xn1row = np.stack([r["xn1row"] for r in res.results], axis=0)
    return r2, xn1fm, xn1row





NE = 64
TK = 127
I16 = mybir.dt.int16


def moe_dram(p, names):
    nc = p.nc
    shapes = {
        "slot_up": ([S, 2], F32), "slot_dn": ([S, 2], F32),
        "park_up": ([128, NT * 6], F32), "park_dn": ([128, NT * 6], F32),
        "btl_up": ([TK, 64 * 128], BF16), "atl_up": ([TK, 64 * 128], BF16),
        "btl_dn": ([TK, 64 * 32], BF16), "atl_dn": ([TK, 64 * 32], BF16),
        "packed_up": ([TK * 128, 4], F32), "packed_dn": ([TK * 128, 4], F32),
        "xs_up": ([TK * 128, D], BF16), "xs_dn": ([TK * 128, FF], BF16),
        "ypl_up": ([2 * S, FF], BF16), "ypl_dn": ([2 * S, D], BF16),
        "hrow_d": ([S, FF], BF16),
    }
    out = {}
    for nm, kind in names.items():
        shp, dt = shapes[nm]
        out[nm] = p.nc.dram_tensor(nm, shp, dt, kind=kind).ap()
    return out


def emit_routing(p, io, consts, stage, slot_d, park_d, wexp):
    nc = p.nc
    ones, onesf, onecolf, identity = consts
    iota64 = p.sb("iota64", [128, NE], F32)
    iotaT = p.sb("iotaT", [64, TK], F32)
    u128 = p.sb("u128", [128, 128], F32)
    u64 = p.sb("u64", [64, NE], F32)
    onec64 = p.sb("onec64", [64, 1], F32)
    park = p.sb("park", [128, NT * 6], F32)
    lg_sb = p.sb("lg_sb", [128, NE], F32)
    m8 = p.sb("m8", [128, 8], F32)
    mi8 = p.sb("mi8", [128, 8], U32)
    efp = p.sb("efp", [128, 8], F32)
    mc = p.sb("mc", [128, 3 * NE], F32)
    offrow = p.sb("offrow", [1, NE], F32)
    rtmp = p.sb("rtmp", [128, NE], F32)
    grow = p.sb("grow", [64, 8], F32)
    thr64 = p.sb("thr64", [64, NE], F32)
    thrt = p.sb("thrt", [64, NE], F32)
    brow_t = p.sb("brow_t", [1, NE], F32)
    bb128 = p.sb("bb128", [128, NE], F32)
    tidw = p.sb("tidw", [128, 2 * TK], F32)
    wgi = p.sb("wgi", [TK, 1], I32)
    slotc = p.sb("slotc", [128, 2], F32)
    ps_r = p.ps("ps_r", [128, 512], F32)
    ps_c = p.ps("ps_c", [128, 512], F32)
    ps_c2 = p.ps("ps_c2", [128, 512], F32)

    def cst_fn(e):
        e.iota(iota64[:], pattern=[[1, NE]], base=0, channel_multiplier=0,
               allow_small_or_imprecise_dtypes=True)
        e.iota(iotaT[:], pattern=[[1, TK]], base=0, channel_multiplier=0,
               allow_small_or_imprecise_dtypes=True)
        e.memset(onec64[:], 1.0)
        e.memset(u128[:], 1.0)
        e.memset(u64[:], 1.0)
        e.iota(thr64[:], pattern=[[128, NE]], base=0, channel_multiplier=0,
               allow_small_or_imprecise_dtypes=True)
        e.drain()
        e.affine_select(out=u128[:], in_=u128[:], compare_op=ALU.is_gt,
                        fill=0.0, base=0, pattern=[[1, 128]],
                        channel_multiplier=-1)
        return e.affine_select(out=u64[:], in_=u64[:], compare_op=ALU.is_gt,
                               fill=0.0, base=0, pattern=[[1, NE]],
                               channel_multiplier=-1)
    p.body("gpsimd", cst_fn)

    def zoff_fn(e):
        return e.memset(offrow[:], 0.0)
    p.body("vector", zoff_fn)

    mk_logits = io["mk_logits"]
    for i in range(NT):
        mk_logits(i, ps_r, lg_sb)

        def top2_fn(e, i=i):
            e.max(out=m8[:], in_=lg_sb[:])
            e.drain()
            e.max_index(out=mi8[:], in_max=m8[:], in_values=lg_sb[:])
            e.drain()
            e.tensor_copy(efp[:, 0:2], mi8[:, 0:2])
            e.tensor_sub(efp[:, 2:3], m8[:, 0:1], m8[:, 1:2])
            e.drain()
            return e.tensor_scalar(efp[:, 3:4], efp[:, 2:3], -1.0, None,
                                   ALU.mult)
        p.body("vector", top2_fn)

        def sig_fn(e):
            return e.activation(efp[:, 4:6], efp[:, 2:4], AF.Sigmoid)
        p.body("scalar", sig_fn)

        def mc_fn(e, i=i):
            e.tensor_scalar(mc[:, 0:NE], iota64[:], efp[:, 0:1], None,
                            ALU.is_equal)
            e.tensor_scalar(mc[:, NE:2 * NE], iota64[:], efp[:, 1:2], None,
                            ALU.is_equal)
            e.drain()
            e.tensor_add(mc[:, 2 * NE:3 * NE], mc[:, 0:NE], mc[:, NE:2 * NE])
            e.drain()
            e.tensor_copy(park[:, i * 6:i * 6 + 2], efp[:, 0:2])
            return e.tensor_copy(park[:, i * 6 + 4:i * 6 + 6], efp[:, 4:6])
        p.body("vector", mc_fn)

        def rank_fn(e, i=i):
            e.matmul(ps_r[:, 64:128], u128[:], mc[:, 2 * NE:3 * NE],
                     start=True, stop=False)
            e.matmul(ps_r[:, 64:128], onesf[:], offrow[:], start=False,
                     stop=True)
            e.matmul(ps_c[0:1, 0:64], onecolf[:], mc[:, 2 * NE:3 * NE],
                     start=(i == 0), stop=(i == NT - 1))
            return e.matmul(ps_c2[0:64, 0:1], mc[:, 2 * NE:3 * NE],
                            onecolf[:], start=(i == 0), stop=(i == NT - 1))
        p.body("tensor", rank_fn)

        def ext_fn(e, i=i):
            e.tensor_copy(offrow[:], ps_c[0:1, 0:64])
            e.tensor_mul(rtmp[:], ps_r[:, 64:128], mc[:, 0:NE])
            e.drain()
            e.reduce_sum(park[:, i * 6 + 2:i * 6 + 3], rtmp[:], axis=AX.X)
            e.drain()
            e.tensor_mul(rtmp[:], ps_r[:, 64:128], mc[:, NE:2 * NE])
            e.drain()
            return e.reduce_sum(park[:, i * 6 + 3:i * 6 + 4], rtmp[:],
                                axis=AX.X)
        p.body("vector", ext_fn)

    def tt_fn(e):
        e.tensor_copy(grow[:, 0:1], ps_c2[0:64, 0:1])
        e.drain()
        e.tensor_scalar(thrt[:], thr64[:], grow[:, 0:1], None, ALU.is_lt)
        e.drain()
        return e.reduce_sum(grow[:, 4:5], thrt[:], axis=AX.X)
    p.body("vector", tt_fn)

    def bt_fn(e):
        e.matmul(ps_c[0:1, 128:192], grow[0:64, 4:5], u64[:], start=True,
                 stop=True)
        return e.matmul(ps_c[0:64, 192:193], u64[:], grow[0:64, 4:5],
                        start=True, stop=True)
    p.body("tensor", bt_fn)

    def tid_fn(e):
        e.tensor_copy(brow_t[:], ps_c[0:1, 128:192])
        e.tensor_copy(grow[:, 5:6], ps_c[0:64, 192:193])
        e.drain()
        return e.tensor_scalar(tidw[0:64, 0:TK], iotaT[:], grow[:, 5:6],
                               None, ALU.is_ge)
    p.body("vector", tid_fn)

    def tid2_fn(e):
        return e.matmul(ps_c[0:TK, 256:257], tidw[0:64, 0:TK], onec64[:],
                        start=True, stop=True)
    p.body("tensor", tid2_fn)

    def tid3_fn(e):
        e.tensor_scalar(tidw[0:TK, TK:TK + 1], ps_c[0:TK, 256:257], 1.0,
                        None, ALU.subtract)
        e.drain()
        return e.tensor_copy(wgi[:], tidw[0:TK, TK:TK + 1])
    p.body("vector", tid3_fn)

    wexp_sb = p.sb("wexp_sb", [128, 64 * 128], BF16)
    for (wsrc, wdst) in wexp:
        rowlen = wsrc.shape[1] * wsrc.shape[2]

        def wexp_fn(e, wsrc=wsrc, rowlen=rowlen):
            p.inc("gp", e.indirect_dma_start(
                out=wexp_sb[0:TK, 0:rowlen], out_offset=None,
                in_=wsrc.rearrange("e r c -> e (r c)"),
                in_offset=IndirectOffsetOnAxis(wgi[:], 0),
                bounds_check=NE - 1, oob_is_err=False), 16)
        p.body("gpsimd", wexp_fn)

        def wst_fn(e, wdst=wdst, rowlen=rowlen):
            p.inc("st", e.dma_start(out=wdst, in_=wexp_sb[0:TK, 0:rowlen]), 16)
        p.body("gpsimd", wst_fn)

    def bb_fn(e):
        return e.matmul(ps_c[:, 384:448], onesf[:], brow_t[:], start=True,
                        stop=True)
    p.body("tensor", bb_fn)

    def bb_ev(e):
        return e.tensor_copy(bb128[:], ps_c[:, 384:448])
    p.body("vector", bb_ev)

    for i in range(NT):
        def sl_fn(e, i=i):
            ins = None
            for k in range(2):
                e.tensor_scalar(mc[:, 0:NE], iota64[:],
                                park[:, i * 6 + k:i * 6 + k + 1], None,
                                ALU.is_equal)
                e.drain()
                e.tensor_mul(rtmp[:], bb128[:], mc[:, 0:NE])
                e.drain()
                e.reduce_sum(grow[0:128, 6:7] if False else efp[:, 6:7],
                             rtmp[:], axis=AX.X)
                e.drain()
                ins = e.tensor_scalar(slotc[:, k:k + 1], efp[:, 6:7], 128.0,
                                      park[:, i * 6 + 2 + k:i * 6 + 3 + k],
                                      ALU.mult, ALU.add)
            return ins
        p.body("vector", sl_fn)

        def sst_fn(e, i=i):
            p.inc("st", e.dma_start(out=slot_d[i * 128:(i + 1) * 128, :],
                                    in_=slotc[:]), 16)
        p.body("gpsimd", sst_fn)

    def pst_fn(e):
        p.inc("st", e.dma_start(out=park_d, in_=park[:]), 16)
    p.body("gpsimd", pst_fn)


def emit_wprep_up(p, io, consts):
    nc = p.nc
    ones, onesf, onecolf, identity = consts
    aup_bf = nc.dram_tensor("aup_bf", [NE, 64, 32], BF16, kind="Internal").ap()
    bup_bf = nc.dram_tensor("bup_bf", [NE, 64, 32], BF16, kind="Internal").ap()
    btz_d = nc.dram_tensor("btz_d", [NE, 64, 128], BF16, kind="Internal").ap()
    atz_d = nc.dram_tensor("atz_d", [NE, 64, 128], BF16, kind="Internal").ap()

    def cast_fn(e):
        p.inc("gp", e.dma_start(out=aup_bf, in_=io["A_up"]), 16)
        p.inc("gp", e.dma_start(out=bup_bf, in_=io["B_up"]), 16)
    p.body("gpsimd", cast_fn)

    with ExitStack() as ctx:
        e_ = ctx.enter_context
        wld = e_(nc.sbuf_tensor(_u("mw_ld"), [128, 64], BF16))
        flat32 = e_(nc.sbuf_tensor(_u("mw_f32"), [32, NE * 64], BF16))
        zz = e_(nc.sbuf_tensor(_u("mw_zz"), [128, 128], BF16))
        ps_w = e_(nc.psum_tensor(_u("mw_ps"), [128, 1024], BF16))

        def z_fn(e):
            return e.memset(zz[:], 0.0)
        p.body("gpsimd", z_fn)

        def zfill_fn(e):
            for tgt in (btz_d, atz_d):
                flat = tgt.rearrange("e r c -> (e r) c")
                for r0 in range(0, NE * 64, 128):
                    p.inc("st", e.dma_start(out=flat[r0:r0 + 128, :],
                                            in_=zz[:]), 16)
        p.body("gpsimd", zfill_fn)

        for src, tgt in ((bup_bf, btz_d), (aup_bf, atz_d)):
            for pair in range(32):
                def ld_fn(e, src=src, pair=pair):
                    p.inc("ld", e.dma_start(
                        out=wld[:, 0:32],
                        in_=src[2 * pair:2 * pair + 2]
                            .rearrange("e r c -> (e r) c")), 16)
                p.body("sync", ld_fn)

                def tr_fn(e, pair=pair):
                    return e.transpose(
                        ps_w[0:32, (pair % 8) * 128:(pair % 8 + 1) * 128],
                        wld[:, 0:32], identity[:])
                p.body("tensor", tr_fn)

                def ev_fn(e, pair=pair):
                    return e.activation(
                        flat32[0:32, pair * 128:(pair + 1) * 128],
                        ps_w[0:32, (pair % 8) * 128:(pair % 8 + 1) * 128],
                        AF.Copy)
                p.body("scalar", ev_fn)

            def st_fn(e, tgt=tgt):
                p.inc("st", e.dma_start(
                    out=tgt[:, 0:32, 0:64].rearrange("e j q -> j e q"),
                    in_=flat32[0:32, :].rearrange("j (e q) -> j e q", e=NE)), 16)
                p.inc("st", e.dma_start(
                    out=tgt[:, 32:64, 64:128].rearrange("e j q -> j e q"),
                    in_=flat32[0:32, :].rearrange("j (e q) -> j e q", e=NE)), 16)
            p.body("gpsimd", st_fn)
    return btz_d, atz_d


def emit_wprep_dn(p, io, consts):
    nc = p.nc
    ones, onesf, onecolf, identity = consts
    adn_bf = nc.dram_tensor("adn_bf", [NE, 32, 64], BF16, kind="Internal").ap()
    bdn_bf = nc.dram_tensor("bdn_bf", [NE, 32, 64], BF16, kind="Internal").ap()
    bdt_d = nc.dram_tensor("bdt_d", [NE, 64, 32], BF16, kind="Internal").ap()
    adt_d = nc.dram_tensor("adt_d", [NE, 64, 32], BF16, kind="Internal").ap()

    def cast_fn(e):
        p.inc("gp", e.dma_start(out=adn_bf, in_=io["A_down"]), 16)
        p.inc("gp", e.dma_start(out=bdn_bf, in_=io["B_down"]), 16)
    p.body("gpsimd", cast_fn)

    with ExitStack() as ctx:
        e_ = ctx.enter_context
        wld = e_(nc.sbuf_tensor(_u("mw_ld"), [128, 64], BF16))
        flat64 = e_(nc.sbuf_tensor(_u("mw_f64"), [64, NE * 32], BF16))
        ps_w = e_(nc.psum_tensor(_u("mw_ps"), [128, 1024], BF16))
        for src, tgt in ((bdn_bf, bdt_d), (adn_bf, adt_d)):
            for quad in range(16):
                def ld_fn(e, src=src, quad=quad):
                    p.inc("ld", e.dma_start(
                        out=wld[:, 0:64],
                        in_=src[4 * quad:4 * quad + 4]
                            .rearrange("e r c -> (e r) c")), 16)
                p.body("sync", ld_fn)

                def tr_fn(e, quad=quad):
                    return e.transpose(
                        ps_w[0:64, (quad % 8) * 128:(quad % 8 + 1) * 128],
                        wld[:, 0:64], identity[:])
                p.body("tensor", tr_fn)

                def ev_fn(e, quad=quad):
                    return e.activation(
                        flat64[0:64, quad * 128:(quad + 1) * 128],
                        ps_w[0:64, (quad % 8) * 128:(quad % 8 + 1) * 128],
                        AF.Copy)
                p.body("scalar", ev_fn)

            def st_fn(e, tgt=tgt):
                p.inc("st", e.dma_start(
                    out=tgt.rearrange("e j q -> j e q"),
                    in_=flat64[0:64, :].rearrange("j (e q) -> j e q", e=NE)), 16)
            p.body("gpsimd", st_fn)
    return bdt_d, adt_d


def build_moe_route_up(p, io):
    consts = emit_consts(p)
    nc = p.nc
    wup_bf = nc.dram_tensor("wup_bf", [NE, D], BF16, kind="Internal").ap()

    def cast_fn(e):
        p.inc("gp", e.dma_start(out=wup_bf, in_=io["W_up"]), 16)
    p.body("gpsimd", cast_fn)

    wupT = p.sb("wupT", [128, NC * NE], BF16)
    load_wT(p, wupT, wup_bf, NE, NC)
    btz_d, atz_d = emit_wprep_up(p, io, consts)

    xnt = p.sb("mr_x", [128, D], BF16)

    def mk_logits(i, ps_r, lg_sb):
        def ld_fn(e, i=i):
            p.inc("ld", e.dma_start(
                out=xnt[:].rearrange("q (c t) -> q c t", c=NC),
                in_=io["xn1fm"][:, :, i * 128:(i + 1) * 128]), 16)
        p.body("sync", ld_fn)

        def lg_fn(e):
            for c in range(NC):
                ins = e.matmul(ps_r[:, 0:64], xnt[:, c * 128:(c + 1) * 128],
                               wupT[:, c * NE:(c + 1) * NE],
                               start=(c == 0), stop=(c == NC - 1))
            return ins
        p.body("tensor", lg_fn)

        def lg_ev(e):
            return e.tensor_copy(lg_sb[:], ps_r[:, 0:64])
        p.body("vector", lg_ev)

    io2 = dict(io)
    io2["mk_logits"] = mk_logits
    emit_routing(p, io2, consts, "up", io["slot_up"], io["park_up"],
                 [(btz_d, io["btl_up"]), (atz_d, io["atl_up"])])


def build_moe_sort(p, io, xw, c0, c1, fill_packed):
    nc = p.nc
    slot_d = io["slot_d"]
    park_d = io["park_d"]
    xrow = io["xrow"]
    xs_d = io["xs_d"]
    packed_d = io["packed_d"]
    if not fill_packed:
        def pt_fn(e):
            p.inc("st", e.dma_start(out=packed_d, in_=io["packed_in"]), 16)
            nrow = TK * 128
            step = nrow // 4
            for r0 in range(0, nrow, step):
                p.inc("st", e.dma_start(out=xs_d[r0:r0 + step, :],
                                        in_=io["xs_in"][r0:r0 + step, :]), 16)
        p.body("gpsimd", pt_fn)
    park = p.sb("park", [128, NT * 6], F32)
    tokio = p.sb("tokio", [128, 1], F32)
    pkq = p.sb("pkq", [128, 4], F32)
    pki = p.sb("pki", [128, 1], I32)
    slotc = p.sb("slotc", [128, 2], F32)
    xtile = p.sb("sx_x", [128, xw], BF16)
    xptile = p.sb("sx_xp", [128, xw], BF16)
    zf = p.sb("zf", [128, 508], F32)

    def cst_fn(e):
        e.iota(tokio[:], pattern=[[0, 1]], base=0, channel_multiplier=1,
               allow_small_or_imprecise_dtypes=True)
        if fill_packed:
            return e.memset(zf[:], 1e9)
        return e.memset(zf[0:1, 0:1], 0.0)
    p.body("gpsimd", cst_fn)

    if fill_packed:
        def fp_fn(e):
            p.inc("st", e.dma_start(
                out=packed_d.rearrange("(q r) c -> q (r c)", q=128),
                in_=zf[:]), 16)
        p.body("gpsimd", fp_fn)

    def ldp_fn(e):
        p.inc("ld", e.dma_start(out=park[:], in_=park_d), 16)
    p.body("sync", ldp_fn)

    for i in range(c0, c1):
        def ldx_fn(e, i=i):
            p.inc("ld", e.dma_start(out=xtile[:],
                                    in_=xrow[i * 128:(i + 1) * 128, :]), 16)
            p.inc("ld", e.dma_start(out=slotc[:],
                                    in_=slot_d[i * 128:(i + 1) * 128, :]), 16)
        p.body("sync", ldx_fn)

        for k in range(2):
            def pk_fn(e, i=i, k=k):
                e.tensor_scalar(pkq[:, 0:1], tokio[:], float(i * 128), None,
                                ALU.add)
                e.tensor_scalar(pkq[:, 1:2], tokio[:],
                                float(i * 128 + k * S), None, ALU.add)
                e.tensor_copy(pkq[:, 2:3],
                              park[:, i * 6 + 4 + k:i * 6 + 5 + k])
                e.tensor_scalar(pkq[:, 3:4], tokio[:], 0.0, None, ALU.mult)
                e.tensor_copy(pki[:], slotc[:, k:k + 1])
                e.drain()
                return e.tensor_scalar(
                    xptile[:], xtile[:],
                    park[:, i * 6 + 4 + k:i * 6 + 5 + k], None, ALU.mult)
            p.body("vector", pk_fn)

            def sc_fn(e, i=i, k=k):
                p.inc("gp", e.indirect_dma_start(
                    out=packed_d, out_offset=IndirectOffsetOnAxis(pki[:], 0),
                    in_=pkq[:], in_offset=None,
                    bounds_check=TK * 128 - 1, oob_is_err=False), 16)
                p.inc("gp", e.indirect_dma_start(
                    out=xs_d, out_offset=IndirectOffsetOnAxis(pki[:], 0),
                    in_=xptile[:], in_offset=None,
                    bounds_check=TK * 128 - 1, oob_is_err=False), 16)
            p.body("gpsimd", sc_fn)


def build_moe_tiles_seg(p, io, stage, j0, j1, prezero):
    nc = p.nc
    consts = emit_consts(p)
    ones, onesf, onecolf, identity = consts
    up = (stage == "up")
    XW = D if up else FF
    YW = FF if up else D
    packed_d = io["packed_d"]
    xs_d = io["xs_d"]
    btl_d = io["btl_d"]
    atl_d = io["atl_d"]
    ypl_d = io["ypl_d"]
    zzb = p.sb("zzb", [128, 8192], BF16)

    if not prezero:
        def pt_fn(e):
            nrow = 2 * S
            step = nrow // 4
            for r0 in range(0, nrow, step):
                p.inc("st", e.dma_start(out=ypl_d[r0:r0 + step, :],
                                        in_=io["ypl_in"][r0:r0 + step, :]), 16)
        p.body("gpsimd", pt_fn)

    if prezero:
        def z0_fn(e):
            return e.memset(zzb[:], 0.0)
        p.body("gpsimd", z0_fn)
        def zp_fn(e):
            step = 8192 // YW * 128
            for r0 in range(0, 2 * S, 128):
                p.inc("st", e.dma_start(out=ypl_d[r0:r0 + 128, :],
                                        in_=zzb[:, 0:YW]), 16)
        p.body("gpsimd", zp_fn)

    with ExitStack() as ctx:
        e_ = ctx.enter_context
        pk = e_(nc.sbuf_tensor(_u("mt_pk"), [128, 4], F32))
        pki = e_(nc.sbuf_tensor(_u("mt_pki"), [128, 1], I32))
        xg = e_(nc.sbuf_tensor(_u("mt_xg"), [128, XW], BF16))
        xt = e_(nc.sbuf_tensor(_u("mt_xt"), [128, 2 * XW if up else XW], BF16))
        wb = e_(nc.sbuf_tensor(_u("mt_wb"), [128, 128], BF16))
        wa = e_(nc.sbuf_tensor(_u("mt_wa"), [128, 128], BF16))
        tsb = e_(nc.sbuf_tensor(_u("mt_tsb"), [128, 2048], BF16))
        ttb = e_(nc.sbuf_tensor(_u("mt_ttb"), [128, 2048], BF16))
        yrow = e_(nc.sbuf_tensor(_u("mt_yr"), [128, YW], BF16))
        ps_t1 = e_(nc.psum_tensor(_u("mt_p1"), [128, 1024], BF16))
        ps_ta = e_(nc.psum_tensor(_u("mt_pa"), [128, 512], F32))
        ps_tb = e_(nc.psum_tensor(_u("mt_pb"), [128, 512], F32))
        ps_t2 = e_(nc.psum_tensor(_u("mt_p2"), [128, 1024], BF16))
        ps_ya = e_(nc.psum_tensor(_u("mt_ya"), [128, 512], F32))
        ps_yb = e_(nc.psum_tensor(_u("mt_yb"), [128, 512], F32))

        for j in range(j0, j1):
            def ld_fn(e, j=j):
                p.inc("ld", e.dma_start(
                    out=pk[:], in_=packed_d[j * 128:(j + 1) * 128, :]), 16)
                p.inc("ld", e.dma_start(
                    out=xg[:], in_=xs_d[j * 128:(j + 1) * 128, :]), 16)
                if up:
                    p.inc("ld", e.dma_start(
                        out=wb[0:64, :],
                        in_=btl_d[j].rearrange("(r c) -> r c", r=64)), 16)
                    for hb in range(2):
                        p.inc("ld", e.dma_start(
                            out=wa[hb * 64:(hb + 1) * 64, :],
                            in_=atl_d[j].rearrange("(r c) -> r c", r=64)), 16)
                else:
                    for hb in range(2):
                        p.inc("ld", e.dma_start(
                            out=wb[hb * 64:(hb + 1) * 64, 0:32],
                            in_=btl_d[j].rearrange("(r c) -> r c", r=64)), 16)
                        p.inc("ld", e.dma_start(
                            out=wa[hb * 64:(hb + 1) * 64, 0:32],
                            in_=atl_d[j].rearrange("(r c) -> r c", r=64)), 16)
            p.body("sync", ld_fn)

            def cvt_fn(e):
                return e.tensor_copy(pki[:], pk[:, 1:2])
            p.body("vector", cvt_fn)

            if up:
                for w in range(2):
                    def t1_fn(e, w=w):
                        ins = None
                        for q in range(8):
                            i2 = w * 8 + q
                            ins = e.transpose(
                                ps_t1[0:64, q * 128:(q + 1) * 128],
                                xg[:, i2 * 64:(i2 + 1) * 64], identity[:])
                        return ins
                    p.body("tensor", t1_fn)

                    def t1_ev(e, w=w):
                        return e.activation(xt[0:64, w * 1024:(w + 1) * 1024],
                                            ps_t1[0:64, :], AF.Copy)
                    p.body("scalar", t1_ev)
            else:
                for w in range(4):
                    def t1_fn(e, w=w):
                        ins = None
                        for q in range(8):
                            c = w * 8 + q
                            ins = e.transpose(
                                ps_t1[:, q * 128:(q + 1) * 128],
                                xg[:, c * 128:(c + 1) * 128], identity[:])
                        return ins
                    p.body("tensor", t1_fn)

                    def t1_ev(e, w=w):
                        return e.activation(xt[:, w * 1024:(w + 1) * 1024],
                                            ps_t1[:], AF.Copy)
                    p.body("scalar", t1_ev)

            if up:
                for w in range(4):
                    def s1_fn(e, w=w):
                        ins = None
                        for q in range(8):
                            ii = w * 8 + q
                            i2, v = ii // 2, ii % 2
                            ins = e.matmul(
                                ps_ta[:, q * 64:(q + 1) * 64],
                                xt[0:64, i2 * 128:(i2 + 1) * 128],
                                wb[0:64, v * 64:(v + 1) * 64],
                                start=True, stop=True)
                        return ins
                    p.body("tensor", s1_fn)

                    def s1_ev(e, w=w):
                        return e.tensor_copy(
                            tsb[:].rearrange("q (o i) -> q o i", i=32)
                                  [:, :, w * 8:(w + 1) * 8],
                            ps_ta[:, 0:512].rearrange("q (i o) -> q o i", i=8))
                    p.body("vector", s1_ev)
            else:
                for w in range(4):
                    par2 = w % 2
                    psu = ps_ta if par2 == 0 else ps_tb
                    base = 64 * par2

                    def s1_fn(e, w=w, psu=psu, base=base):
                        ins = None
                        for q in range(16):
                            c = (w // 2) * 16 + q
                            ins = e.matmul(
                                psu[:, q * 32:(q + 1) * 32],
                                xt[base:base + 64, c * 128:(c + 1) * 128],
                                wb[base:base + 64, 0:32],
                                start=True, stop=True)
                        return ins
                    p.body("tensor", s1_fn)

                    def s1_ev(e, w=w, psu=psu):
                        return e.tensor_copy(
                            tsb[:].rearrange("q (o i2 r) -> q o i2 r",
                                             i2=32, r=2)
                                  [:, :, (w // 2) * 16:(w // 2) * 16 + 16,
                                   (w % 2):(w % 2) + 1]
                                  .rearrange("q o i2 r -> q o (i2 r)"),
                            psu[:, 0:512].rearrange("q (i o) -> q o i", i=16))
                    p.body("vector", s1_ev)

            for w in range(2):
                def t2_fn(e, w=w):
                    ins = None
                    for q in range(8):
                        c = w * 8 + q
                        ins = e.transpose(ps_t2[:, q * 128:(q + 1) * 128],
                                          tsb[:, c * 128:(c + 1) * 128],
                                          identity[:])
                    return ins
                p.body("tensor", t2_fn)

                def t2_ev(e, w=w):
                    return e.activation(ttb[:, w * 1024:(w + 1) * 1024],
                                        ps_t2[:], AF.Copy)
                p.body("scalar", t2_ev)

            if up:
                for g in range(8):
                    def s2_fn(e, g=g):
                        ins = None
                        for b2 in range(2):
                            blk = g * 2 + b2
                            for v in range(2):
                                sl = b2 * 2 + v
                                ins = e.matmul(
                                    ps_ya[:, sl * 64:(sl + 1) * 64],
                                    ttb[0:64, blk * 128:(blk + 1) * 128],
                                    wa[0:64, v * 64:(v + 1) * 64],
                                    start=True, stop=True)
                        for b2 in range(2):
                            blk = g * 2 + b2
                            for v in range(2):
                                sl = b2 * 2 + v
                                ins = e.matmul(
                                    ps_yb[:, sl * 64:(sl + 1) * 64],
                                    ttb[64:128, blk * 128:(blk + 1) * 128],
                                    wa[64:128, v * 64:(v + 1) * 64],
                                    start=True, stop=True)
                        return ins
                    p.body("tensor", s2_fn)

                    def s2_ev(e, g=g):
                        e.tensor_copy(
                            yrow[:].rearrange("q (o pp) -> q o pp", pp=64)
                                   [:, :, 8 * g:8 * g + 8]
                                   .rearrange("q o (b2 vv) -> q o b2 vv", b2=2)
                                   [:, :, :, 0:2],
                            ps_ya[:, 0:256]
                                .rearrange("q (b2 v o) -> q o b2 v", b2=2, v=2))
                        e.drain()
                        return e.tensor_copy(
                            yrow[:].rearrange("q (o pp) -> q o pp", pp=64)
                                   [:, :, 8 * g:8 * g + 8]
                                   .rearrange("q o (b2 vv) -> q o b2 vv", b2=2)
                                   [:, :, :, 2:4],
                            ps_yb[:, 0:256]
                                .rearrange("q (b2 v o) -> q o b2 v", b2=2, v=2))
                    p.body("vector", s2_ev)
            else:
                for w in range(2):
                    psu = ps_ya if w == 0 else ps_yb
                    base = 64 * w

                    def s2_fn(e, w=w, psu=psu, base=base):
                        ins = None
                        for q in range(16):
                            ins = e.matmul(
                                psu[:, q * 32:(q + 1) * 32],
                                ttb[base:base + 64, q * 128:(q + 1) * 128],
                                wa[base:base + 64, 0:32],
                                start=True, stop=True)
                        return ins
                    p.body("tensor", s2_fn)

                    def s2_ev(e, w=w, psu=psu):
                        return e.tensor_copy(
                            yrow[:].rearrange("q (o p2 r) -> q o p2 r",
                                              p2=16, r=2)
                                   [:, :, :, w:w + 1]
                                   .rearrange("q o p2 r -> q o (p2 r)"),
                            psu[:, 0:512].rearrange("q (i o) -> q o i", i=16))
                    p.body("vector", s2_ev)

            def sc_fn(e, j=j):
                p.inc("gp", e.indirect_dma_start(
                    out=ypl_d, out_offset=IndirectOffsetOnAxis(pki[:], 0),
                    in_=yrow[:], in_offset=None,
                    bounds_check=2 * S - 1, oob_is_err=False), 16)
            p.body("gpsimd", sc_fn)


def build_moe_combine_up(p, io):
    nc = p.nc
    consts = emit_consts(p)
    ones, onesf, onecolf, identity = consts
    wdn_bf = nc.dram_tensor("wdn_bf", [NE, FF], BF16, kind="Internal").ap()

    def cast_fn(e):
        p.inc("gp", e.dma_start(out=wdn_bf, in_=io["W_down"]), 16)
    p.body("gpsimd", cast_fn)

    wdnT = p.sb("wdnT", [128, NF * NE], BF16)
    load_wT(p, wdnT, wdn_bf, NE, NF)
    bdt_d, adt_d = emit_wprep_dn(p, io, consts)

    sclcol = p.sb("sclcol", [128, 1], F32)
    biasup = p.sb("biasup", [128, FF], F32)
    with ExitStack() as ctx:
        e_ = ctx.enter_context
        srow = e_(nc.sbuf_tensor(_u("mb_s"), [1, 1], F32))
        burow = e_(nc.sbuf_tensor(_u("mb_u"), [1, FF], F32))
        ps_bc = e_(nc.psum_tensor(_u("mb_ps"), [128, 512], F32))

        def ld_sb(e):
            p.inc("ld", e.dma_start(out=srow[:], in_=io["scale_up"]), 16)
            p.inc("ld", e.dma_start(out=burow[:], in_=io["bias_up"]), 16)
        p.body("sync", ld_sb)

        def bc_fn(e):
            return e.matmul(ps_bc[:, 0:1], onesf[:], srow[:], start=True,
                            stop=True)
        p.body("tensor", bc_fn)

        def bc_ev(e):
            return e.tensor_copy(sclcol[:], ps_bc[:, 0:1])
        p.body("vector", bc_ev)

        for j in range(FF // 512):
            def bb_fn(e, j=j):
                return e.matmul(ps_bc[:, 0:512], onesf[:],
                                burow[:, j * 512:(j + 1) * 512],
                                start=True, stop=True)
            p.body("tensor", bb_fn)

            def bb_ev(e, j=j):
                return e.tensor_copy(biasup[:, j * 512:(j + 1) * 512],
                                     ps_bc[:, 0:512])
            p.body("vector", bb_ev)

    y0 = p.sb("cu_y0", [128, FF], BF16)
    y1 = p.sb("cu_y1", [128, FF], BF16)
    hsum = p.sb("cu_s", [128, FF], F32)
    hbf = p.sb("cu_h", [128, FF], BF16)
    hfm_t = p.sb("cu_hf", [128, FF], BF16)
    ps_tr2 = p.ps("cu_ps", [128, 1024], BF16)

    def mk_logits(i, ps_r, lg_sb):
        def ld_fn(e, i=i):
            p.inc("ld", e.dma_start(
                out=y0[:], in_=io["ypl_up"][i * 128:(i + 1) * 128, :]), 16)
            p.inc("ld", e.dma_start(
                out=y1[:], in_=io["ypl_up"][S + i * 128:S + (i + 1) * 128, :]), 16)
        p.body("sync", ld_fn)

        def cmb_fn(e):
            e.tensor_add(hsum[:], y0[:], y1[:])
            e.drain()
            return e.scalar_tensor_tensor(hsum[:], hsum[:], sclcol[:],
                                          biasup[:], ALU.mult, ALU.add)
        p.body("vector", cmb_fn)

        def gel_fn(e):
            return e.activation(hbf[:], hsum[:], AF.Gelu)
        p.body("scalar", gel_fn)

        def sth_fn(e, i=i):
            p.inc("st", e.dma_start(out=io["hrow_d"][i * 128:(i + 1) * 128, :],
                                    in_=hbf[:]), 16)
        p.body("gpsimd", sth_fn)

        for half in range(4):
            def tr_fn(e, half=half):
                ins = None
                for c in range(8):
                    cc = half * 8 + c
                    ins = e.transpose(ps_tr2[:, c * 128:(c + 1) * 128],
                                      hbf[:, cc * 128:(cc + 1) * 128],
                                      identity[:])
                return ins
            p.body("tensor", tr_fn)

            def tr_ev(e, half=half):
                return e.activation(hfm_t[:, half * 1024:(half + 1) * 1024],
                                    ps_tr2[:], AF.Copy)
            p.body("scalar", tr_ev)

        def lg_fn(e):
            for c in range(NF):
                ins = e.matmul(ps_r[:, 0:64], hfm_t[:, c * 128:(c + 1) * 128],
                               wdnT[:, c * NE:(c + 1) * NE],
                               start=(c == 0), stop=(c == NF - 1))
            return ins
        p.body("tensor", lg_fn)

        def lg_ev(e):
            return e.tensor_copy(lg_sb[:], ps_r[:, 0:64])
        p.body("vector", lg_ev)

    io2 = dict(io)
    io2["mk_logits"] = mk_logits
    emit_routing(p, io2, consts, "dn", io["slot_dn"], io["park_dn"],
                 [(bdt_d, io["btl_dn"]), (adt_d, io["atl_dn"])])


def build_moe_final(p, io):
    nc = p.nc
    consts = emit_consts(p)
    ones, onesf, onecolf, identity = consts
    sclcol = p.sb("sclcol", [128, 1], F32)
    biasdn = p.sb("biasdn", [128, D], F32)
    with ExitStack() as ctx:
        e_ = ctx.enter_context
        srow = e_(nc.sbuf_tensor(_u("mb_s"), [1, 1], F32))
        bdrow = e_(nc.sbuf_tensor(_u("mb_d"), [1, D], F32))
        ps_bc = e_(nc.psum_tensor(_u("mb_ps"), [128, 512], F32))

        def ld_sb(e):
            p.inc("ld", e.dma_start(out=srow[:], in_=io["scale_down"]), 16)
            p.inc("ld", e.dma_start(out=bdrow[:], in_=io["bias_down"]), 16)
        p.body("sync", ld_sb)

        def bc_fn(e):
            return e.matmul(ps_bc[:, 0:1], onesf[:], srow[:], start=True,
                            stop=True)
        p.body("tensor", bc_fn)

        def bc_ev(e):
            return e.tensor_copy(sclcol[:], ps_bc[:, 0:1])
        p.body("vector", bc_ev)

        for j in range(D // 512):
            def bb_fn(e, j=j):
                return e.matmul(ps_bc[:, 0:512], onesf[:],
                                bdrow[:, j * 512:(j + 1) * 512],
                                start=True, stop=True)
            p.body("tensor", bb_fn)

            def bb_ev(e, j=j):
                return e.tensor_copy(biasdn[:, j * 512:(j + 1) * 512],
                                     ps_bc[:, 0:512])
            p.body("vector", bb_ev)

    y0 = p.sb("cf_y0", [128, D], BF16)
    y1 = p.sb("cf_y1", [128, D], BF16)
    rr = p.sb("cf_r", [128, D], F32)
    acc = p.sb("cf_a", [128, D], F32)
    ob = [p.sb("cf_o", [128, D], F32) for _ in range(2)]
    for i in range(NT):
        par = i % 2

        def ld_fn(e, i=i):
            p.inc("ld", e.dma_start(
                out=y0[:], in_=io["ypl_dn"][i * 128:(i + 1) * 128, :]), 16)
            p.inc("ld", e.dma_start(
                out=y1[:], in_=io["ypl_dn"][S + i * 128:S + (i + 1) * 128, :]), 16)
            p.inc("ld", e.dma_start(out=rr[:],
                                    in_=io["r2"][i * 128:(i + 1) * 128, :]), 16)
        p.body("sync", ld_fn)

        def cmb_fn(e, par=par):
            e.tensor_add(acc[:], y0[:], y1[:])
            e.drain()
            e.scalar_tensor_tensor(acc[:], acc[:], sclcol[:], biasdn[:],
                                   ALU.mult, ALU.add)
            e.drain()
            return e.tensor_add(ob[par][:], acc[:], rr[:])
        p.body("vector", cmb_fn)

        def st_fn(e, i=i, par=par):
            p.inc("st", e.dma_start(out=io["out"][i * 128:(i + 1) * 128, :],
                                    in_=ob[par][:]), 16)
        p.body("gpsimd", st_fn)


def _moe_seg_program(seg, **kw):
    p = Prog()
    nc = p.nc
    io = {}

    def din(nm, shape, dt=F32):
        io[nm] = nc.dram_tensor(nm, shape, dt, kind="ExternalInput").ap()

    def dout(nm, shape, dt=F32):
        io[nm] = nc.dram_tensor(nm, shape, dt, kind="ExternalOutput").ap()

    if seg == "route_up":
        din("xn1fm", [128, NC, S], BF16)
        din("W_up", [NE, D]); din("A_up", [NE, 64, 32]); din("B_up", [NE, 64, 32])
        dout("slot_up", [S, 2]); dout("park_up", [128, NT * 6])
        dout("btl_up", [TK, 64 * 128], BF16); dout("atl_up", [TK, 64 * 128], BF16)
    elif seg == "sort_up":
        din("slot_d", [S, 2]); din("park_d", [128, NT * 6])
        din("xrow", [S, D], BF16)
        if kw.get("c0", 0) > 0:
            din("packed_in", [TK * 128, 4]); din("xs_in", [TK * 128, D], BF16)
        dout("packed_d", [TK * 128, 4]); dout("xs_d", [TK * 128, D], BF16)
    elif seg == "tiles_up":
        din("packed_d", [TK * 128, 4]); din("xs_d", [TK * 128, D], BF16)
        din("btl_d", [TK, 64 * 128], BF16); din("atl_d", [TK, 64 * 128], BF16)
        if kw.get("j0", 0) > 0:
            din("ypl_in", [2 * S, FF], BF16)
        dout("ypl_d", [2 * S, FF], BF16)
    elif seg == "combine_up":
        din("ypl_up", [2 * S, FF], BF16)
        din("scale_up", [1, 1]); din("bias_up", [1, FF])
        din("W_down", [NE, FF]); din("A_down", [NE, 32, 64]); din("B_down", [NE, 32, 64])
        dout("hrow_d", [S, FF], BF16)
        dout("slot_dn", [S, 2]); dout("park_dn", [128, NT * 6])
        dout("btl_dn", [TK, 64 * 32], BF16); dout("atl_dn", [TK, 64 * 32], BF16)
    elif seg == "sort_dn":
        din("slot_d", [S, 2]); din("park_d", [128, NT * 6])
        din("xrow", [S, FF], BF16)
        if kw.get("c0", 0) > 0:
            din("packed_in", [TK * 128, 4]); din("xs_in", [TK * 128, FF], BF16)
        dout("packed_d", [TK * 128, 4]); dout("xs_d", [TK * 128, FF], BF16)
    elif seg == "tiles_dn":
        din("packed_d", [TK * 128, 4]); din("xs_d", [TK * 128, FF], BF16)
        din("btl_d", [TK, 64 * 32], BF16); din("atl_d", [TK, 64 * 32], BF16)
        if kw.get("j0", 0) > 0:
            din("ypl_in", [2 * S, D], BF16)
        dout("ypl_d", [2 * S, D], BF16)
    elif seg == "final":
        din("ypl_dn", [2 * S, D], BF16); din("r2", [S, D])
        din("scale_down", [1, 1]); din("bias_down", [1, D])
        dout("out", [S, D])

    with p.ctx:
        p.blk = p.ctx.enter_context(nc.Block())
        if seg == "route_up":
            build_moe_route_up(p, io)
        elif seg == "sort_up":
            build_moe_sort(p, io, D, kw["c0"], kw["c1"], kw["c0"] == 0)
        elif seg == "tiles_up":
            build_moe_tiles_seg(p, io, "up", kw["j0"], kw["j1"], kw["j0"] == 0)
        elif seg == "combine_up":
            build_moe_combine_up(p, io)
        elif seg == "sort_dn":
            build_moe_sort(p, io, FF, kw["c0"], kw["c1"], kw["c0"] == 0)
        elif seg == "tiles_dn":
            build_moe_tiles_seg(p, io, "dn", kw["j0"], kw["j1"], kw["j0"] == 0)
        elif seg == "final":
            build_moe_final(p, io)
        p.finish()
    return p


def _spmd(p, maps):
    return run_bass_kernel_spmd(p.nc, maps, core_ids=list(range(8))).results


SORT_SPLITS = ((0, 11), (11, 22), (22, 32))
TILE_SPLITS = ((0, 43), (43, 86), (86, TK))


def run_moe(r2_full, xn1fm_full, xn1row_full, inputs):
    def W(nm):
        v = np.ascontiguousarray(inputs["L1_ffn_" + nm], dtype=np.float32)
        if nm.startswith("scale"):
            v = v.reshape(1, 1)
        elif nm.startswith("bias"):
            v = v.reshape(1, -1)
        return v

    def stage(xrow_full, route_fn, sort_seg, tiles_seg):
        slot_d, park_d, btl, atl = route_fn()
        packed, xs = None, None
        for c0, c1 in SORT_SPLITS:
            p = _moe_seg_program(sort_seg, c0=c0, c1=c1)
            maps = []
            for b in range(8):
                m = {"slot_d": slot_d[b], "park_d": park_d[b],
                     "xrow": np.ascontiguousarray(xrow_full[b])}
                if c0 > 0:
                    m["packed_in"] = packed[b]
                    m["xs_in"] = xs[b]
                maps.append(m)
            res = _spmd(p, maps)
            packed = [r["packed_d"] for r in res]
            xs = [r["xs_d"] for r in res]
        ypl = None
        for j0, j1 in TILE_SPLITS:
            p = _moe_seg_program(tiles_seg, j0=j0, j1=j1)
            maps = []
            for b in range(8):
                m = {"packed_d": packed[b], "xs_d": xs[b],
                     "btl_d": btl[b], "atl_d": atl[b]}
                if j0 > 0:
                    m["ypl_in"] = ypl[b]
                maps.append(m)
            res = _spmd(p, maps)
            ypl = [r["ypl_d"] for r in res]
        return ypl

    def route_up():
        p = _moe_seg_program("route_up")
        res = _spmd(p, [{"xn1fm": np.ascontiguousarray(xn1fm_full[b]),
                         "W_up": W("W_up"), "A_up": W("A_up"),
                         "B_up": W("B_up")} for b in range(8)])
        return ([r["slot_up"] for r in res], [r["park_up"] for r in res],
                [r["btl_up"] for r in res], [r["atl_up"] for r in res])

    ypl_up = stage(xn1row_full, route_up, "sort_up", "tiles_up")

    p = _moe_seg_program("combine_up")
    res = _spmd(p, [{"ypl_up": ypl_up[b], "scale_up": W("scale_up"),
                     "bias_up": W("bias_up"), "W_down": W("W_down"),
                     "A_down": W("A_down"), "B_down": W("B_down")}
                    for b in range(8)])
    hrow = [r["hrow_d"] for r in res]
    dn_state = ([r["slot_dn"] for r in res], [r["park_dn"] for r in res],
                [r["btl_dn"] for r in res], [r["atl_dn"] for r in res])

    ypl_dn = stage(hrow, lambda: dn_state, "sort_dn", "tiles_dn")

    p = _moe_seg_program("final")
    res = _spmd(p, [{"ypl_dn": ypl_dn[b],
                     "r2": np.ascontiguousarray(r2_full[b], dtype=np.float32),
                     "scale_down": W("scale_down"),
                     "bias_down": W("bias_down")} for b in range(8)])
    return np.stack([r["out"] for r in res], axis=0)


def build_l0_program():
    p = Prog()
    nc = p.nc
    io = {}
    io["x"] = nc.dram_tensor("x", [S, D], F32, kind="ExternalInput").ap()
    for nm, shape in (("Wq", [D, D]), ("Wk", [D, D]), ("Wv", [D, D]),
                      ("Wo", [D, D]), ("bq", [1, D]), ("bk", [1, D]), ("bv", [1, D]),
                      ("bo", [1, D]), ("W1", [FF, D]), ("b1", [1, FF]),
                      ("W2", [D, FF]), ("b2", [1, D])):
        io[nm] = nc.dram_tensor(nm, shape, F32, kind="ExternalInput").ap()
    io["r1"] = nc.dram_tensor("r1", [S, D], F32, kind="ExternalOutput").ap()
    io["xnfm"] = nc.dram_tensor("xnfm", [128, NC, S], BF16,
                                kind="ExternalOutput").ap()
    with p.ctx:
        p.blk = p.ctx.enter_context(nc.Block())
        build_l0(p, io)
        p.finish()
    return p


L0_KEYS = (("Wq", "L0_attn_Wq"), ("Wk", "L0_attn_Wk"), ("Wv", "L0_attn_Wv"),
           ("Wo", "L0_attn_Wo"), ("bq", "L0_attn_bq"), ("bk", "L0_attn_bk"),
           ("bv", "L0_attn_bv"), ("bo", "L0_attn_bo"), ("W1", "L0_ffn_W1"),
           ("b1", "L0_ffn_b1"), ("W2", "L0_ffn_W2"), ("b2", "L0_ffn_b2"))


def run_l0(inputs):
    x_full = np.ascontiguousarray(inputs["L0_attn_x"], dtype=np.float32)
    p = build_l0_program()
    in_maps = []
    for b in range(8):
        m = {"x": x_full[b]}
        for nm, key in L0_KEYS:
            v = np.ascontiguousarray(inputs[key], dtype=np.float32)
            if nm.startswith("b"):
                v = v.reshape(1, -1)
            m[nm] = v
        in_maps.append(m)
    res = run_bass_kernel_spmd(p.nc, in_maps, core_ids=list(range(8)))
    r1 = np.stack([r["r1"] for r in res.results], axis=0)
    xnfm = np.stack([r["xnfm"] for r in res.results], axis=0)
    return r1, xnfm


def kernel(**inputs):
    r1, _ = run_l0(inputs)
    r2, xn1fm, xn1row = run_l1a(r1, inputs)
    out = run_moe(r2, xn1fm, xn1row, inputs)
    return out


# version 04 (final)
# speedup vs baseline: 1.1853x; 1.1853x over previous
import numpy as np
from contextlib import ExitStack

import concourse.bass as bass
import concourse.mybir as mybir
from concourse.bass import IndirectOffsetOnAxis
from concourse.bass_utils import run_bass_kernel_spmd

F32 = mybir.dt.float32
BF16 = mybir.dt.bfloat16
I32 = mybir.dt.int32
U32 = mybir.dt.uint32
AF = mybir.ActivationFunctionType
ALU = mybir.AluOpType
AX = mybir.AxisListType

S = 4096
D = 1024
FF = 4096
NT = S // 128
NC = D // 128
NF = FF // 128

ENG_CH = {"sync": "ld", "tensor": "pe", "vector": "ve",
          "scalar": "ac", "gpsimd": "gp"}

_UID = [0]


def _u(base):
    _UID[0] += 1
    return f"{base}_{_UID[0]}"


class Prog:
    def __init__(self):
        self.nc = bass.Bass("TRN2", target_bir_lowering=False)
        self.ctx = ExitStack()
        self.sems = {}
        self.counts = {}
        self.blk = None
        self._waited = {}
        for ch in ("ld", "pe", "ve", "ac", "gp", "st"):
            self.sem(ch)

    def sem(self, name):
        if name not in self.sems:
            self.sems[name] = self.ctx.enter_context(self.nc.semaphore(name))
            self.counts[name] = 0
        return self.sems[name]

    def inc(self, name, instr, k=1):
        instr.then_inc(self.sem(name), k)
        self.counts[name] += k
        return instr

    def body(self, eng, fn):
        ch = ENG_CH[eng]
        seen = self._waited.setdefault(eng, {})
        own = {"tensor": ("pe", "st"), "vector": ("ve",), "scalar": ("ac",),
               "sync": (), "gpsimd": ("pe",)}[eng]
        vals = {}
        for n, v in self.counts.items():
            if v > 0 and n not in own and seen.get(n, 0) < v:
                vals[n] = v
                seen[n] = v

        def run(e):
            for n, v in vals.items():
                e.wait_ge(self.sems[n], v)
            if eng in ("vector", "scalar", "gpsimd"):
                e.drain()
            last = fn(e)
            if last is not None:
                self.inc(ch, last)
        getattr(self.blk, eng)(run)

    def finish(self):
        vals = {n: v for n, v in self.counts.items() if v > 0}

        def run(e):
            for n, v in vals.items():
                e.wait_ge(self.sems[n], v)
        self.blk.sync(run)

    def sb(self, name, shape, dtype, **kw):
        return self.ctx.enter_context(self.nc.sbuf_tensor(_u(name), shape, dtype, **kw))

    def ps(self, name, shape, dtype=F32):
        return self.ctx.enter_context(self.nc.psum_tensor(_u(name), shape, dtype))


def load_wT(p, wT_sb, w_bf, rows, chunks):
    def fn(e):
        for c in range(chunks):
            p.inc("ld", e.dma_start(out=wT_sb[:, c * rows:(c + 1) * rows],
                                    in_=w_bf[:, c * 128:(c + 1) * 128],
                                    transpose=True), 16)
    p.body("sync", fn)


def emit_consts(p):
    ones = p.sb("ones", [1, 512], BF16)
    onesf = p.sb("onesf", [1, 128], F32)
    onecolf = p.sb("onecolf", [128, 1], F32)
    identity = p.sb("identity", [128, 128], BF16)
    constf = p.sb("constf", [128, 4], F32)
    p.constf = constf

    def fn(e):
        e.memset(ones[:], 1.0)
        e.memset(onesf[:], 1.0)
        e.memset(onecolf[:], 1.0)
        e.memset(constf[:, 0:1], 1e-5)
        e.memset(constf[:, 1:2], float(1.0 / np.sqrt(512.0)))
        e.memset(constf[:, 2:3], float(1.0 / np.sqrt(64.0)))
        e.memset(identity[:], 0.0)
        return e.affine_select(out=identity[:], in_=identity[:],
                               compare_op=ALU.not_equal, fill=1.0, base=0,
                               pattern=[[-1, 128]], channel_multiplier=1)
    p.body("gpsimd", fn)
    return ones, onesf, onecolf, identity



def emit_ln_transpose(p, x_d, xnfm_d, identity, xnrow_d=None):
    nc = p.nc
    with ExitStack() as ctx:
        e_ = ctx.enter_context
        xt = [e_(nc.sbuf_tensor(_u(f"ln_x{i}"), [128, D], F32)) for i in range(2)]
        xnb = [e_(nc.sbuf_tensor(_u(f"ln_n{i}"), [128, D], BF16)) for i in range(2)]
        xnT = [e_(nc.sbuf_tensor(_u(f"ln_t{i}"), [128, D], BF16)) for i in range(2)]
        sq = e_(nc.sbuf_tensor(_u("ln_sq"), [128, D], F32))
        st = e_(nc.sbuf_tensor(_u("ln_st"), [128, 8], F32))
        ptT = e_(nc.psum_tensor(_u("ln_pt"), [128, D], BF16))

        for i in range(NT):
            par = i % 2

            def ld_fn(e, i=i, par=par):
                p.inc("ld", e.dma_start(out=xt[par][:],
                                        in_=x_d[i * 128:(i + 1) * 128, :]), 16)
            p.body("sync", ld_fn)

            def sq_fn(e, par=par):
                return e.activation(sq[:], xt[par][:], AF.Square,
                                    accum_out=st[:, 0:1])
            p.body("scalar", sq_fn)

            def ve_fn(e, par=par):
                e.reduce_sum(st[:, 2:3], xt[par][:], axis=AX.X)
                e.drain()
                e.tensor_scalar(st[:, 3:4], st[:, 2:3], 1.0 / D, None, ALU.mult)
                e.tensor_scalar(st[:, 4:5], st[:, 0:1], 1.0 / D, None, ALU.mult)
                e.drain()
                e.tensor_tensor(st[:, 6:7], st[:, 3:4], st[:, 3:4], ALU.mult)
                e.drain()
                return e.tensor_sub(st[:, 4:5], st[:, 4:5], st[:, 6:7])
            p.body("vector", ve_fn)

            def sqrt_fn(e):
                return e.activation(st[:, 5:6], st[:, 4:5], AF.Sqrt,
                                    bias=p.constf[:, 0:1])
            p.body("scalar", sqrt_fn)

            def norm_fn(e, par=par):
                e.reciprocal(st[:, 6:7], st[:, 5:6])
                e.drain()
                return e.tensor_scalar(xnb[par][:], xt[par][:], st[:, 3:4],
                                       st[:, 6:7], ALU.subtract, ALU.mult)
            p.body("vector", norm_fn)

            def tr_fn(e, par=par):
                ins = None
                for c in range(NC):
                    ins = e.transpose(ptT[:, c * 128:(c + 1) * 128],
                                      xnb[par][:, c * 128:(c + 1) * 128],
                                      identity[:])
                return ins
            p.body("tensor", tr_fn)

            def ev_fn(e, par=par):
                return e.activation(xnT[par][:], ptT[:], AF.Copy)
            p.body("scalar", ev_fn)

            def stx_fn(e, i=i, par=par):
                p.inc("st", e.dma_start(
                    out=xnfm_d[:, :, i * 128:(i + 1) * 128],
                    in_=xnT[par][:].rearrange("q (c t) -> q c t", c=NC)), 16)
                if xnrow_d is not None:
                    p.inc("st", e.dma_start(
                        out=xnrow_d[i * 128:(i + 1) * 128, :],
                        in_=xnb[par][:]), 16)
            p.body("gpsimd", stx_fn)


def emit_proj_fm(p, wT_sb, xnfm_sb, dest_d, b_row, ones, ncol, scale=None):
    nc = p.nc
    nd = ncol // 128
    with ExitStack() as ctx:
        e_ = ctx.enter_context
        ot = [e_(nc.sbuf_tensor(_u(f"pj_o{i}"), [128, 512], BF16)) for i in range(2)]
        psu = [e_(nc.psum_tensor(_u(f"pj_p{i}"), [128, 512], F32)) for i in range(2)]
        k = 0
        for dc in range(nd):
            for tp in range(S // 512):
                par = k % 2

                def pe_fn(e, dc=dc, tp=tp, par=par):
                    for c in range(NC):
                        ins = e.matmul(
                            psu[par][:, 0:512],
                            wT_sb[:, c * ncol + dc * 128:c * ncol + (dc + 1) * 128],
                            xnfm_sb[:, c * S + tp * 512:c * S + (tp + 1) * 512],
                            start=(c == 0), stop=(b_row is None and c == NC - 1))
                    if b_row is not None:
                        ins = e.matmul(psu[par][:, 0:512],
                                       b_row[:, dc * 128:(dc + 1) * 128],
                                       ones[:, 0:512], start=False, stop=True)
                    return ins
                p.body("tensor", pe_fn)

                def ac_fn(e, par=par):
                    return e.activation(ot[par][:], psu[par][:, 0:512], AF.Copy,
                                        scale=(scale if scale else 1.0))
                p.body("scalar", ac_fn)

                def st_fn(e, dc=dc, tp=tp, par=par):
                    p.inc("st", e.dma_start(
                        out=dest_d[:, dc, tp * 512:(tp + 1) * 512],
                        in_=ot[par][:]), 16)
                p.body("gpsimd", st_fn)
                k += 1


def emit_proj_tm(p, wT_sb, xnfm_sb, dest_d, b_row, ones, ncol):
    nc = p.nc
    with ExitStack() as ctx:
        e_ = ctx.enter_context
        ot = [e_(nc.sbuf_tensor(_u(f"pv_o{i}"), [128, ncol], BF16)) for i in range(2)]
        psu = [e_(nc.psum_tensor(_u(f"pv_p{i}"), [128, 512], F32)) for i in range(2)]
        for i in range(NT):
            par = i % 2

            def pe_fn(e, i=i, par=par):
                for half in range(ncol // 512):
                    for c in range(NC):
                        ins = e.matmul(
                            psu[half][:, 0:512],
                            xnfm_sb[:, c * S + i * 128:c * S + (i + 1) * 128],
                            wT_sb[:, c * ncol + half * 512:c * ncol + (half + 1) * 512],
                            start=(c == 0), stop=(b_row is None and c == NC - 1))
                    if b_row is not None:
                        ins = e.matmul(psu[half][:, 0:512], ones[0:1, 0:128],
                                       b_row[:, half * 512:(half + 1) * 512],
                                       start=False, stop=True)
                return ins
            p.body("tensor", pe_fn)

            def ac_fn(e, par=par):
                for half in range(ncol // 512):
                    e.drain()
                    ins = e.activation(ot[par][:, half * 512:(half + 1) * 512],
                                       psu[half][:, 0:512], AF.Copy)
                return ins
            p.body("scalar", ac_fn)

            def st_fn(e, i=i, par=par):
                p.inc("st", e.dma_start(out=dest_d[i * 128:(i + 1) * 128, :],
                                        in_=ot[par][:]), 16)
            p.body("gpsimd", st_fn)


def emit_quant(p, w_f32, w_q_bf, rows, cols, onesf, onecolf, tag):
    nc = p.nc
    ntile = rows // 128
    with ExitStack() as ctx:
        e_ = ctx.enter_context
        wt = [e_(nc.sbuf_tensor(_u(f"qz_w{tag}{i}"), [128, cols], F32))
              for i in range(2)]
        wq = e_(nc.sbuf_tensor(_u(f"qz_q{tag}"), [128, cols], BF16))
        q1 = e_(nc.sbuf_tensor(_u(f"qz_1{tag}"), [128, cols], F32))
        q2 = e_(nc.sbuf_tensor(_u(f"qz_2{tag}"), [128, cols], F32))
        absn = e_(nc.sbuf_tensor(_u(f"qz_a{tag}"), [128, ntile], F32))
        gcol = e_(nc.sbuf_tensor(_u(f"qz_g{tag}"), [128, 4], F32))
        gps = e_(nc.psum_tensor(_u(f"qz_p{tag}"), [128, 2], F32))

        for i in range(ntile):
            def ld_fn(e, i=i):
                p.inc("ld", e.dma_start(out=wt[i % 2][:],
                                        in_=w_f32[i * 128:(i + 1) * 128, :]), 16)
            p.body("sync", ld_fn)

            def ve_fn(e, i=i):
                return e.reduce_sum(absn[:, i:i + 1], wt[i % 2][:], axis=AX.X,
                                    apply_absolute_value=True)
            p.body("vector", ve_fn)

        def ve_tot(e):
            return e.reduce_sum(gcol[:, 0:1], absn[:, 0:ntile], axis=AX.X)
        p.body("vector", ve_tot)

        def pe_tot(e):
            return e.matmul(gps[0:1, 0:1], gcol[:, 0:1], onecolf[:],
                            start=True, stop=True)
        p.body("tensor", pe_tot)

        def ve_g(e):
            e.tensor_scalar(gcol[0:1, 1:2], gps[0:1, 0:1], 1.0 / (rows * cols),
                            1e-5, ALU.mult, ALU.add)
            e.drain()
            return e.reciprocal(gcol[0:1, 2:3], gcol[0:1, 1:2])
        p.body("vector", ve_g)

        def pe_b(e):
            return e.matmul(gps[:, 1:2], onesf[:], gcol[0:1, 2:3],
                            start=True, stop=True)
        p.body("tensor", pe_b)

        def ve_bc(e):
            return e.tensor_copy(gcol[:, 3:4], gps[:, 1:2])
        p.body("vector", ve_bc)

        for i in range(ntile):
            def ld_fn(e, i=i):
                p.inc("ld", e.dma_start(out=wt[i % 2][:],
                                        in_=w_f32[i * 128:(i + 1) * 128, :]), 16)
            p.body("sync", ld_fn)

            def ve_fn(e, i=i):
                e.tensor_scalar(q1[:], wt[i % 2][:], gcol[:, 3:4], None,
                                ALU.mult)
                e.drain()
                e.tensor_scalar(wq[:], q1[:], 0.5, None, ALU.is_gt)
                e.tensor_scalar(q2[:], q1[:], -0.5, None, ALU.is_lt)
                e.drain()
                return e.tensor_sub(wq[:], wq[:], q2[:])
            p.body("vector", ve_fn)

            def st_fn(e, i=i):
                p.inc("st", e.dma_start(out=w_q_bf[i * 128:(i + 1) * 128, :],
                                        in_=wq[:]), 16)
            p.body("gpsimd", st_fn)



def build_l0(p, io):
    nc = p.nc
    x = io["x"]
    r1 = io["r1"]
    xnfm_d = io["xnfm"]

    wq_bf = nc.dram_tensor("wq_bf", [D, D], BF16, kind="Internal").ap()
    wk_bf = nc.dram_tensor("wk_bf", [D, D], BF16, kind="Internal").ap()
    wv_bf = nc.dram_tensor("wv_bf", [D, D], BF16, kind="Internal").ap()
    wo_bf = nc.dram_tensor("wo_bf", [D, D], BF16, kind="Internal").ap()
    w1q_bf = nc.dram_tensor("w1q_bf", [FF, D], BF16, kind="Internal").ap()
    w2q_bf = nc.dram_tensor("w2q_bf", [D, FF], BF16, kind="Internal").ap()
    qfm_d = nc.dram_tensor("qfm_d", [128, NC, S], BF16, kind="Internal").ap()
    kfm_d = nc.dram_tensor("kfm_d", [128, NC, S], BF16, kind="Internal").ap()
    vtm_d = nc.dram_tensor("vtm_d", [S, D], BF16, kind="Internal").ap()
    racc_d = nc.dram_tensor("racc_d", [S, D], F32, kind="Internal").ap()

    ones, onesf, onecolf, identity = emit_consts(p)

    def cast_fn(e):
        for dst, src in ((wq_bf, io["Wq"]), (wk_bf, io["Wk"]),
                         (wv_bf, io["Wv"]), (wo_bf, io["Wo"])):
            p.inc("gp", e.dma_start(out=dst, in_=src), 16)
    p.body("gpsimd", cast_fn)

    brow = p.sb("brow", [1, 4 * D + FF + D], BF16)
    bq_r = brow[:, 0:D]
    bk_r = brow[:, D:2 * D]
    bv_r = brow[:, 2 * D:3 * D]
    bo_r = brow[:, 3 * D:4 * D]
    b1_r = brow[:, 4 * D:4 * D + FF]
    b2_r = brow[:, 4 * D + FF:4 * D + FF + D]

    def bias_fn(e):
        for dst, src in ((bq_r, io["bq"]), (bk_r, io["bk"]), (bv_r, io["bv"]),
                         (bo_r, io["bo"]), (b1_r, io["b1"]), (b2_r, io["b2"])):
            p.inc("gp", e.dma_start(out=dst, in_=src), 16)
    p.body("gpsimd", bias_fn)

    emit_ln_transpose(p, x, xnfm_d, identity)
    emit_quant(p, io["W1"], w1q_bf, FF, D, onesf, onecolf, "a")
    emit_quant(p, io["W2"], w2q_bf, D, FF, onesf, onecolf, "b")

    with ExitStack() as ctx:
        e_ = ctx.enter_context
        xnfm = e_(nc.sbuf_tensor(_u("xnfm_sb"), [128, NC * S], BF16))
        wT = e_(nc.sbuf_tensor(_u("wT"), [128, NC * D], BF16))

        def ld_xn(e):
            p.inc("ld", e.dma_start(out=xnfm[:],
                                    in_=xnfm_d.rearrange("q c t -> q (c t)")), 16)
        p.body("sync", ld_xn)

        load_wT(p, wT, wk_bf, D, NC)
        emit_proj_fm(p, wT, xnfm, kfm_d, bk_r, ones, D)
        load_wT(p, wT, wq_bf, D, NC)
        emit_proj_fm(p, wT, xnfm, qfm_d, bq_r, ones, D)
        load_wT(p, wT, wv_bf, D, NC)
        emit_proj_tm(p, wT, xnfm, vtm_d, bv_r, ones, D)

    with ExitStack() as ctx:
        e_ = ctx.enter_context
        kfm = e_(nc.sbuf_tensor(_u("kfm"), [128, NC * S], BF16))
        vtm = e_(nc.sbuf_tensor(_u("vtm"), [128, NT * D], BF16))
        woT = e_(nc.sbuf_tensor(_u("woT"), [128, NC * D], BF16))
        qt = [e_(nc.sbuf_tensor(_u(f"qt{i}"), [128, D], BF16)) for i in range(2)]
        s_sb = e_(nc.sbuf_tensor(_u("s_sb"), [128, S], BF16))
        p_sb = e_(nc.sbuf_tensor(_u("p_sb"), [128, S], BF16))
        pt_sb = [e_(nc.sbuf_tensor(_u(f"pt{i}"), [128, 128], BF16)) for i in range(2)]
        ao_sb = e_(nc.sbuf_tensor(_u("ao_sb"), [128, D], BF16))
        o_sb = [e_(nc.sbuf_tensor(_u(f"o_sb{i}"), [128, D], F32)) for i in range(2)]
        xres = [e_(nc.sbuf_tensor(_u(f"xres{i}"), [128, D], F32)) for i in range(2)]
        cstat = e_(nc.sbuf_tensor(_u("cstat"), [128, 4], F32))
        ps_s = e_(nc.psum_tensor(_u("ps_s"), [128, 512], F32))
        ps_pt = e_(nc.psum_tensor(_u("ps_pt"), [128, 256], BF16))
        ps_ao = e_(nc.psum_tensor(_u("ps_ao"), [128, 1024], F32))
        ps_o = e_(nc.psum_tensor(_u("ps_o"), [128, 1024], F32))

        def ld0(e):
            p.inc("ld", e.dma_start(out=kfm[:],
                                    in_=kfm_d.rearrange("q c t -> q (c t)")), 16)
            p.inc("ld", e.dma_start(
                out=vtm[:].rearrange("q (n d) -> q n d", n=NT),
                in_=vtm_d.rearrange("(n q) d -> q n d", q=128)), 16)
        p.body("sync", ld0)
        load_wT(p, woT, wo_bf, D, NC)

        for i in range(NT):
            par = i % 2

            def ld_fn(e, i=i, par=par):
                p.inc("ld", e.dma_start(
                    out=qt[par][:].rearrange("q (c t) -> q c t", c=NC),
                    in_=qfm_d[:, :, i * 128:(i + 1) * 128]), 16)
                p.inc("ld", e.dma_start(out=xres[par][:],
                                        in_=x[i * 128:(i + 1) * 128, :]), 16)
            p.body("sync", ld_fn)

            for kp in range(NC):
                def pe_s(e, par=par, kp=kp):
                    ins = None
                    for c in range(NC):
                        ins = e.matmul(
                            ps_s[:, 0:512],
                            qt[par][:, c * 128:(c + 1) * 128],
                            kfm[:, c * S + kp * 512:c * S + (kp + 1) * 512],
                            start=(c == 0), stop=(c == NC - 1))
                    return ins
                p.body("tensor", pe_s)

                def ac_s(e, kp=kp):
                    return e.activation(s_sb[:, kp * 512:(kp + 1) * 512],
                                        ps_s[:, 0:512], AF.Copy,
                                        scale=p.constf[:, 1:2])
                p.body("scalar", ac_s)

            def ve_sm(e):
                return e.reduce_max(cstat[:, 0:1], s_sb[:], axis=AX.X,
                                    negate=True)
            p.body("vector", ve_sm)

            def ac_sm(e):
                return e.activation(p_sb[:], s_sb[:], AF.Exp,
                                    bias=cstat[:, 0:1], accum_out=cstat[:, 1:2])
            p.body("scalar", ac_sm)

            def ve_sm2(e):
                e.reciprocal(cstat[:, 2:3], cstat[:, 1:2])
                e.drain()
                return e.tensor_scalar(s_sb[:], p_sb[:], cstat[:, 2:3], None,
                                       ALU.mult)
            p.body("vector", ve_sm2)

            for kc in range(NT):
                ptp = kc % 2

                def pe_pt(e, kc=kc, ptp=ptp):
                    return e.transpose(ps_pt[:, ptp * 128:(ptp + 1) * 128],
                                       s_sb[:, kc * 128:(kc + 1) * 128],
                                       identity[:])
                p.body("tensor", pe_pt)

                def ve_pt(e, ptp=ptp):
                    return e.tensor_copy(pt_sb[ptp][:],
                                         ps_pt[:, ptp * 128:(ptp + 1) * 128])
                p.body("vector", ve_pt)

                def pe_pv(e, kc=kc, ptp=ptp):
                    ins = None
                    for dc in range(NC):
                        ins = e.matmul(
                            ps_ao[:, dc * 128:(dc + 1) * 128],
                            vtm[:, kc * D + dc * 128:kc * D + (dc + 1) * 128],
                            pt_sb[ptp][:],
                            start=(kc == 0 and dc % 4 == 0),
                            stop=(kc == NT - 1))
                    return ins
                p.body("tensor", pe_pv)

            def ac_ao(e):
                return e.activation(ao_sb[:], ps_ao[:], AF.Copy)
            p.body("scalar", ac_ao)

            def pe_o(e):
                ins = None
                for half in range(2):
                    for c in range(NC):
                        e.matmul(ps_o[:, half * 512:(half + 1) * 512],
                                 ao_sb[:, c * 128:(c + 1) * 128],
                                 woT[:, c * D + half * 512:c * D + (half + 1) * 512],
                                 start=(c == 0), stop=False)
                    ins = e.matmul(ps_o[:, half * 512:(half + 1) * 512],
                                   ones[0:1, 0:128],
                                   bo_r[:, half * 512:(half + 1) * 512],
                                   start=False, stop=True)
                return ins
            p.body("tensor", pe_o)

            def ve_r(e, par=par):
                return e.tensor_add(o_sb[par][:], ps_o[:], xres[par][:])
            p.body("vector", ve_r)

            def st_r(e, i=i, par=par):
                p.inc("st", e.dma_start(out=racc_d[i * 128:(i + 1) * 128, :],
                                        in_=o_sb[par][:]), 16)
            p.body("gpsimd", st_r)

    with ExitStack() as ctx:
        e_ = ctx.enter_context
        w1T = e_(nc.sbuf_tensor(_u("w1T"), [128, NC * FF], BF16))
        w2T = e_(nc.sbuf_tensor(_u("w2T"), [128, NF * D], BF16))
        xnt = [e_(nc.sbuf_tensor(_u(f"fx{i}"), [128, D], BF16)) for i in range(2)]
        hfm = e_(nc.sbuf_tensor(_u("hfm"), [128, FF], BF16))
        rres = [e_(nc.sbuf_tensor(_u(f"fr{i}"), [128, D], F32)) for i in range(2)]
        fout = [e_(nc.sbuf_tensor(_u(f"fo{i}"), [128, D], F32)) for i in range(2)]
        ps_h = [e_(nc.psum_tensor(_u(f"ps_h{i}"), [128, 128], F32)) for i in range(2)]
        ps_f = e_(nc.psum_tensor(_u("ps_f"), [128, 1024], F32))

        load_wT(p, w1T, w1q_bf, FF, NC)
        load_wT(p, w2T, w2q_bf, D, NF)

        for i in range(NT):
            par = i % 2

            def ld_fn(e, i=i, par=par):
                p.inc("ld", e.dma_start(
                    out=xnt[par][:].rearrange("q (c t) -> q c t", c=NC),
                    in_=xnfm_d[:, :, i * 128:(i + 1) * 128]), 16)
                p.inc("ld", e.dma_start(out=rres[par][:],
                                        in_=racc_d[i * 128:(i + 1) * 128, :]), 16)
            p.body("sync", ld_fn)

            for fc in range(NF):
                fp = fc % 2

                def pe_h(e, par=par, fc=fc, fp=fp):
                    for c in range(NC):
                        e.matmul(ps_h[fp][:, 0:128],
                                 w1T[:, c * FF + fc * 128:c * FF + (fc + 1) * 128],
                                 xnt[par][:, c * 128:(c + 1) * 128],
                                 start=(c == 0), stop=False)
                    return e.matmul(ps_h[fp][:, 0:128],
                                    b1_r[:, fc * 128:(fc + 1) * 128],
                                    ones[:, 0:128], start=False, stop=True)
                p.body("tensor", pe_h)

                def ac_h(e, fc=fc, fp=fp):
                    return e.activation(hfm[:, fc * 128:(fc + 1) * 128],
                                        ps_h[fp][:, 0:128], AF.Relu)
                p.body("scalar", ac_h)

            def pe_f(e):
                ins = None
                for half in range(2):
                    for fc in range(NF):
                        e.matmul(ps_f[:, half * 512:(half + 1) * 512],
                                 hfm[:, fc * 128:(fc + 1) * 128],
                                 w2T[:, fc * D + half * 512:fc * D + (half + 1) * 512],
                                 start=(fc == 0), stop=False)
                    ins = e.matmul(ps_f[:, half * 512:(half + 1) * 512],
                                   ones[0:1, 0:128],
                                   b2_r[:, half * 512:(half + 1) * 512],
                                   start=False, stop=True)
                return ins
            p.body("tensor", pe_f)

            def ve_f(e, par=par):
                return e.tensor_add(fout[par][:], ps_f[:], rres[par][:])
            p.body("vector", ve_f)

            def st_f(e, i=i, par=par):
                p.inc("st", e.dma_start(out=r1[i * 128:(i + 1) * 128, :],
                                        in_=fout[par][:]), 16)
            p.body("gpsimd", st_f)




NH = 16
HD = 64


def build_l1a(p, io):
    nc = p.nc
    r1 = io["r1"]
    r2 = io["r2"]
    xn1fm_d = io["xn1fm"]
    xn1row_d = io["xn1row"]

    wq_bf = nc.dram_tensor("l1wq_bf", [D, D], BF16, kind="Internal").ap()
    wk_bf = nc.dram_tensor("l1wk_bf", [D, D], BF16, kind="Internal").ap()
    wv_bf = nc.dram_tensor("l1wv_bf", [D, D], BF16, kind="Internal").ap()
    wo_bf = nc.dram_tensor("l1wo_bf", [D, D], BF16, kind="Internal").ap()
    qfm_d = nc.dram_tensor("l1qfm_d", [128, NC, S], BF16, kind="Internal").ap()
    kfm_d = nc.dram_tensor("l1kfm_d", [128, NC, S], BF16, kind="Internal").ap()
    vtm_d = nc.dram_tensor("l1vtm_d", [S, D], BF16, kind="Internal").ap()

    ones, onesf, onecolf, identity = emit_consts(p)

    def cast_fn(e):
        for dst, src in ((wq_bf, io["Wq"]), (wk_bf, io["Wk"]),
                         (wv_bf, io["Wv"]), (wo_bf, io["Wo"])):
            p.inc("gp", e.dma_start(out=dst, in_=src), 16)
    p.body("gpsimd", cast_fn)

    mtriT = p.sb("mtriT", [128, 128], BF16)

    def mask_fn(e):
        e.memset(mtriT[:], 0.0)
        return e.affine_select(out=mtriT[:], in_=mtriT[:],
                               compare_op=ALU.is_ge, fill=-1e30, base=0,
                               pattern=[[1, 128]], channel_multiplier=-1)
    p.body("gpsimd", mask_fn)

    emit_ln_transpose(p, r1, xn1fm_d, identity, xnrow_d=xn1row_d)

    with ExitStack() as ctx:
        e_ = ctx.enter_context
        xnfm = e_(nc.sbuf_tensor(_u("l1xn_sb"), [128, NC * S], BF16))
        wT = e_(nc.sbuf_tensor(_u("l1wT"), [128, NC * D], BF16))

        def ld_xn(e):
            p.inc("ld", e.dma_start(out=xnfm[:],
                                    in_=xn1fm_d.rearrange("q c t -> q (c t)")), 16)
        p.body("sync", ld_xn)

        load_wT(p, wT, wk_bf, D, NC)
        emit_proj_fm(p, wT, xnfm, kfm_d, None, ones, D)
        load_wT(p, wT, wq_bf, D, NC)
        emit_proj_fm(p, wT, xnfm, qfm_d, None, ones, D)
        load_wT(p, wT, wv_bf, D, NC)
        emit_proj_tm(p, wT, xnfm, vtm_d, None, ones, D)

    with ExitStack() as ctx:
        e_ = ctx.enter_context
        kfm = e_(nc.sbuf_tensor(_u("l1kfm"), [128, NC * S], BF16))
        vpl = e_(nc.sbuf_tensor(_u("l1vpl"), [128, NT * D], BF16))
        onecb = e_(nc.sbuf_tensor(_u("l1oneb"), [128, 1], BF16))
        woT = e_(nc.sbuf_tensor(_u("l1woT"), [128, NC * D], BF16))
        qt = [e_(nc.sbuf_tensor(_u("l1qt"), [128, D], BF16)) for i in range(2)]
        ptt = e_(nc.sbuf_tensor(_u("l1ptt"), [128, NH * 128], BF16))
        dmask = e_(nc.sbuf_tensor(_u("l1dmask"), [128, NH * 128], F32))
        sums = e_(nc.sbuf_tensor(_u("l1sums"), [128, NH * 2], F32))
        ao_sb = e_(nc.sbuf_tensor(_u("l1ao"), [128, D], BF16))
        aofm = e_(nc.sbuf_tensor(_u("l1aofm"), [128, D], BF16))
        o_sb = [e_(nc.sbuf_tensor(_u("l1o"), [128, D], F32)) for i in range(2)]
        rres = [e_(nc.sbuf_tensor(_u("l1rr"), [128, D], F32)) for i in range(2)]
        ps_st = e_(nc.psum_tensor(_u("l1ps_st"), [128, NH * 128], F32))
        ps_ao = e_(nc.psum_tensor(_u("l1ps_ao"), [128, NH * 64], F32))
        ps_sum = e_(nc.psum_tensor(_u("l1ps_sum"), [128, 16], F32))
        ps_tr = e_(nc.psum_tensor(_u("l1ps_tr"), [128, D], BF16))

        def ld0(e):
            p.inc("ld", e.dma_start(out=kfm[:],
                                    in_=kfm_d.rearrange("q c t -> q (c t)")), 16)
            p.inc("ld", e.dma_start(
                out=vpl[:].rearrange("q (n d) -> q n d", n=NT),
                in_=vtm_d.rearrange("(n q) d -> q n d", q=128)), 16)
        p.body("sync", ld0)

        def ones_fn(e):
            return e.memset(onecb[:], 1.0)
        p.body("gpsimd", ones_fn)

        load_wT(p, woT, wo_bf, D, NC)

        for i in range(NT):
            par = i % 2

            def ld_fn(e, i=i, par=par):
                p.inc("ld", e.dma_start(
                    out=qt[par][:].rearrange("q (c t) -> q c t", c=NC),
                    in_=qfm_d[:, :, i * 128:(i + 1) * 128]), 16)
                p.inc("ld", e.dma_start(out=rres[par][:],
                                        in_=r1[i * 128:(i + 1) * 128, :]), 16)
            p.body("sync", ld_fn)

            def slot(h):
                return (h % 2) * 8 + h // 2

            for kc in range(i + 1):
                def pe_st(e, par=par, kc=kc):
                    ins = None
                    for h in sorted(range(NH), key=lambda h: h % 2):
                        c, b = h // 2, 64 * (h % 2)
                        ins = e.matmul(
                            ps_st[:, slot(h) * 128:(slot(h) + 1) * 128],
                            kfm[b:b + HD, c * S + kc * 128:c * S + (kc + 1) * 128],
                            qt[par][b:b + HD, c * 128:(c + 1) * 128],
                            start=True, stop=True)
                    return ins
                p.body("tensor", pe_st)

                if kc == i:
                    def m_fn(e, kc=kc):
                        ins = None
                        for h in range(NH):
                            ins = e.tensor_tensor(
                                dmask[:, h * 128:(h + 1) * 128],
                                ps_st[:, slot(h) * 128:(slot(h) + 1) * 128],
                                mtriT[:], ALU.add)
                        return ins
                    p.body("vector", m_fn)

                def ac_exp(e, kc=kc, diag=(kc == i)):
                    ins = None
                    for h in range(NH):
                        src = (dmask[:, h * 128:(h + 1) * 128] if diag else
                               ps_st[:, slot(h) * 128:(slot(h) + 1) * 128])
                        ins = e.activation(ptt[:, h * 128:(h + 1) * 128],
                                           src, AF.Exp, scale=p.constf[:, 2:3])
                    return ins
                p.body("scalar", ac_exp)

                def pe_pv(e, kc=kc, first=(kc == 0), last=(kc == i)):
                    ins = None
                    for h in range(NH):
                        e.matmul(
                            ps_ao[:, h * 64:(h + 1) * 64],
                            ptt[:, h * 128:(h + 1) * 128],
                            vpl[:, kc * D + h * 64:kc * D + (h + 1) * 64],
                            start=(first and h % 8 == 0), stop=last)
                        ins = e.matmul(
                            ps_sum[:, h:h + 1],
                            ptt[:, h * 128:(h + 1) * 128], onecb[:],
                            start=(first and h == 0), stop=last)
                    return ins
                p.body("tensor", pe_pv)

            def ve_ao(e):
                e.tensor_copy(sums[:, 0:NH], ps_sum[:])
                e.drain()
                e.reciprocal(sums[:, NH:2 * NH], sums[:, 0:NH])
                e.drain()
                ins = None
                for h in range(NH):
                    ins = e.tensor_scalar(ao_sb[:, h * HD:(h + 1) * HD],
                                          ps_ao[:, h * 64:(h + 1) * 64],
                                          sums[:, NH + h:NH + h + 1], None,
                                          ALU.mult)
                return ins
            p.body("vector", ve_ao)

            if i == 1 and "dbg_ao" in io:
                def dbg_fn(e):
                    p.inc("st", e.dma_start(out=io["dbg_ao"], in_=ao_sb[:]), 16)
                    p.inc("st", e.dma_start(out=io["dbg_sums"], in_=sums[:]), 16)
                p.body("gpsimd", dbg_fn)

            def pe_tr(e):
                ins = None
                for c in range(NC):
                    ins = e.transpose(ps_tr[:, c * 128:(c + 1) * 128],
                                      ao_sb[:, c * 128:(c + 1) * 128],
                                      identity[:])
                return ins
            p.body("tensor", pe_tr)

            def ac_tr(e):
                return e.activation(aofm[:], ps_tr[:], AF.Copy)
            p.body("scalar", ac_tr)

            def pe_o(e):
                ins = None
                for half in range(2):
                    for c in range(NC):
                        ins = e.matmul(
                            ps_st[:, half * 512:(half + 1) * 512],
                            aofm[:, c * 128:(c + 1) * 128],
                            woT[:, c * D + half * 512:c * D + (half + 1) * 512],
                            start=(c == 0), stop=(c == NC - 1))
                return ins
            p.body("tensor", pe_o)

            def ve_r(e, par=par):
                return e.tensor_add(o_sb[par][:], ps_st[:, 0:1024], rres[par][:])
            p.body("vector", ve_r)

            def st_r(e, i=i, par=par):
                p.inc("st", e.dma_start(out=r2[i * 128:(i + 1) * 128, :],
                                        in_=o_sb[par][:]), 16)
            p.body("gpsimd", st_r)


def build_l1a_program():
    p = Prog()
    nc = p.nc
    io = {}
    io["r1"] = nc.dram_tensor("r1", [S, D], F32, kind="ExternalInput").ap()
    for nm in ("Wq", "Wk", "Wv", "Wo"):
        io[nm] = nc.dram_tensor(nm, [D, D], F32, kind="ExternalInput").ap()
    io["r2"] = nc.dram_tensor("r2", [S, D], F32, kind="ExternalOutput").ap()
    io["xn1fm"] = nc.dram_tensor("xn1fm", [128, NC, S], BF16,
                                 kind="ExternalOutput").ap()
    io["xn1row"] = nc.dram_tensor("xn1row", [S, D], BF16,
                                  kind="ExternalOutput").ap()
    io["dbg_ao"] = nc.dram_tensor("dbg_ao", [128, D], BF16,
                                  kind="ExternalOutput").ap()
    io["dbg_sums"] = nc.dram_tensor("dbg_sums", [128, 32], F32,
                                    kind="ExternalOutput").ap()
    with p.ctx:
        p.blk = p.ctx.enter_context(nc.Block())
        build_l1a(p, io)
        p.finish()
    return p


def run_l1a(r1_full, inputs):
    p = build_l1a_program()
    in_maps = []
    for b in range(8):
        m = {"r1": np.ascontiguousarray(r1_full[b], dtype=np.float32)}
        for nm, key in (("Wq", "L1_attn_Wq"), ("Wk", "L1_attn_Wk"),
                        ("Wv", "L1_attn_Wv"), ("Wo", "L1_attn_Wo")):
            m[nm] = np.ascontiguousarray(inputs[key], dtype=np.float32)
        in_maps.append(m)
    res = run_bass_kernel_spmd(p.nc, in_maps, core_ids=list(range(8)))
    r2 = np.stack([r["r2"] for r in res.results], axis=0)
    xn1fm = np.stack([r["xn1fm"] for r in res.results], axis=0)
    xn1row = np.stack([r["xn1row"] for r in res.results], axis=0)
    return r2, xn1fm, xn1row





NE = 64
TK = 127
I16 = mybir.dt.int16


def moe_dram(p, names):
    nc = p.nc
    shapes = {
        "slot_up": ([S, 2], F32), "slot_dn": ([S, 2], F32),
        "park_up": ([128, NT * 6], F32), "park_dn": ([128, NT * 6], F32),
        "btl_up": ([TK, 64 * 128], BF16), "atl_up": ([TK, 64 * 128], BF16),
        "btl_dn": ([TK, 64 * 32], BF16), "atl_dn": ([TK, 64 * 32], BF16),
        "packed_up": ([TK * 128, 4], F32), "packed_dn": ([TK * 128, 4], F32),
        "xs_up": ([TK * 128, D], BF16), "xs_dn": ([TK * 128, FF], BF16),
        "ypl_up": ([2 * S, FF], BF16), "ypl_dn": ([2 * S, D], BF16),
        "hrow_d": ([S, FF], BF16),
    }
    out = {}
    for nm, kind in names.items():
        shp, dt = shapes[nm]
        out[nm] = p.nc.dram_tensor(nm, shp, dt, kind=kind).ap()
    return out


def emit_routing(p, io, consts, stage, slot_d, park_d, wexp):
    nc = p.nc
    ones, onesf, onecolf, identity = consts
    iota64 = p.sb("iota64", [128, NE], F32)
    iotaT = p.sb("iotaT", [64, TK], F32)
    u128 = p.sb("u128", [128, 128], F32)
    u64 = p.sb("u64", [64, NE], F32)
    onec64 = p.sb("onec64", [64, 1], F32)
    park = p.sb("park", [128, NT * 6], F32)
    lg_sb = p.sb("lg_sb", [128, NE], F32)
    m8 = p.sb("m8", [128, 8], F32)
    mi8 = p.sb("mi8", [128, 8], U32)
    efp = p.sb("efp", [128, 8], F32)
    mc = p.sb("mc", [128, 3 * NE], F32)
    offrow = p.sb("offrow", [1, NE], F32)
    rtmp = p.sb("rtmp", [128, NE], F32)
    grow = p.sb("grow", [64, 8], F32)
    thr64 = p.sb("thr64", [64, NE], F32)
    thrt = p.sb("thrt", [64, NE], F32)
    brow_t = p.sb("brow_t", [1, NE], F32)
    bb128 = p.sb("bb128", [128, NE], F32)
    tidw = p.sb("tidw", [128, 2 * TK], F32)
    wgi = p.sb("wgi", [TK, 1], I32)
    slotc = p.sb("slotc", [128, 2], F32)
    ps_r = p.ps("ps_r", [128, 512], F32)
    ps_c = p.ps("ps_c", [128, 512], F32)
    ps_c2 = p.ps("ps_c2", [128, 512], F32)

    def cst_fn(e):
        e.iota(iota64[:], pattern=[[1, NE]], base=0, channel_multiplier=0,
               allow_small_or_imprecise_dtypes=True)
        e.iota(iotaT[:], pattern=[[1, TK]], base=0, channel_multiplier=0,
               allow_small_or_imprecise_dtypes=True)
        e.memset(onec64[:], 1.0)
        e.memset(u128[:], 1.0)
        e.memset(u64[:], 1.0)
        e.iota(thr64[:], pattern=[[128, NE]], base=0, channel_multiplier=0,
               allow_small_or_imprecise_dtypes=True)
        e.drain()
        e.affine_select(out=u128[:], in_=u128[:], compare_op=ALU.is_gt,
                        fill=0.0, base=0, pattern=[[1, 128]],
                        channel_multiplier=-1)
        return e.affine_select(out=u64[:], in_=u64[:], compare_op=ALU.is_gt,
                               fill=0.0, base=0, pattern=[[1, NE]],
                               channel_multiplier=-1)
    p.body("gpsimd", cst_fn)

    def zoff_fn(e):
        return e.memset(offrow[:], 0.0)
    p.body("vector", zoff_fn)

    mk_logits = io["mk_logits"]
    for i in range(NT):
        mk_logits(i, ps_r, lg_sb)

        def top2_fn(e, i=i):
            e.max(out=m8[:], in_=lg_sb[:])
            e.drain()
            e.max_index(out=mi8[:], in_max=m8[:], in_values=lg_sb[:])
            e.drain()
            e.tensor_copy(efp[:, 0:2], mi8[:, 0:2])
            e.tensor_sub(efp[:, 2:3], m8[:, 0:1], m8[:, 1:2])
            e.drain()
            return e.tensor_scalar(efp[:, 3:4], efp[:, 2:3], -1.0, None,
                                   ALU.mult)
        p.body("vector", top2_fn)

        def sig_fn(e):
            return e.activation(efp[:, 4:6], efp[:, 2:4], AF.Sigmoid)
        p.body("scalar", sig_fn)

        def mc_fn(e, i=i):
            e.tensor_scalar(mc[:, 0:NE], iota64[:], efp[:, 0:1], None,
                            ALU.is_equal)
            e.tensor_scalar(mc[:, NE:2 * NE], iota64[:], efp[:, 1:2], None,
                            ALU.is_equal)
            e.drain()
            e.tensor_add(mc[:, 2 * NE:3 * NE], mc[:, 0:NE], mc[:, NE:2 * NE])
            e.drain()
            e.tensor_copy(park[:, i * 6:i * 6 + 2], efp[:, 0:2])
            return e.tensor_copy(park[:, i * 6 + 4:i * 6 + 6], efp[:, 4:6])
        p.body("vector", mc_fn)

        def rank_fn(e, i=i):
            e.matmul(ps_r[:, 64:128], u128[:], mc[:, 2 * NE:3 * NE],
                     start=True, stop=False)
            e.matmul(ps_r[:, 64:128], onesf[:], offrow[:], start=False,
                     stop=True)
            e.matmul(ps_c[0:1, 0:64], onecolf[:], mc[:, 2 * NE:3 * NE],
                     start=(i == 0), stop=(i == NT - 1))
            return e.matmul(ps_c2[0:64, 0:1], mc[:, 2 * NE:3 * NE],
                            onecolf[:], start=(i == 0), stop=(i == NT - 1))
        p.body("tensor", rank_fn)

        def ext_fn(e, i=i):
            e.tensor_copy(offrow[:], ps_c[0:1, 0:64])
            e.tensor_mul(rtmp[:], ps_r[:, 64:128], mc[:, 0:NE])
            e.drain()
            e.reduce_sum(park[:, i * 6 + 2:i * 6 + 3], rtmp[:], axis=AX.X)
            e.drain()
            e.tensor_mul(rtmp[:], ps_r[:, 64:128], mc[:, NE:2 * NE])
            e.drain()
            return e.reduce_sum(park[:, i * 6 + 3:i * 6 + 4], rtmp[:],
                                axis=AX.X)
        p.body("vector", ext_fn)

    def tt_fn(e):
        e.tensor_copy(grow[:, 0:1], ps_c2[0:64, 0:1])
        e.drain()
        e.tensor_scalar(thrt[:], thr64[:], grow[:, 0:1], None, ALU.is_lt)
        e.drain()
        return e.reduce_sum(grow[:, 4:5], thrt[:], axis=AX.X)
    p.body("vector", tt_fn)

    def bt_fn(e):
        e.matmul(ps_c[0:1, 128:192], grow[0:64, 4:5], u64[:], start=True,
                 stop=True)
        return e.matmul(ps_c[0:64, 192:193], u64[:], grow[0:64, 4:5],
                        start=True, stop=True)
    p.body("tensor", bt_fn)

    def tid_fn(e):
        e.tensor_copy(brow_t[:], ps_c[0:1, 128:192])
        e.tensor_copy(grow[:, 5:6], ps_c[0:64, 192:193])
        e.drain()
        return e.tensor_scalar(tidw[0:64, 0:TK], iotaT[:], grow[:, 5:6],
                               None, ALU.is_ge)
    p.body("vector", tid_fn)

    def tid2_fn(e):
        return e.matmul(ps_c[0:TK, 256:257], tidw[0:64, 0:TK], onec64[:],
                        start=True, stop=True)
    p.body("tensor", tid2_fn)

    def tid3_fn(e):
        e.tensor_scalar(tidw[0:TK, TK:TK + 1], ps_c[0:TK, 256:257], 1.0,
                        None, ALU.subtract)
        e.drain()
        return e.tensor_copy(wgi[:], tidw[0:TK, TK:TK + 1])
    p.body("vector", tid3_fn)

    wexp_sb = p.sb("wexp_sb", [128, 64 * 128], BF16)
    for (wsrc, wdst) in wexp:
        rowlen = wsrc.shape[1] * wsrc.shape[2]

        def wexp_fn(e, wsrc=wsrc, rowlen=rowlen):
            p.inc("gp", e.indirect_dma_start(
                out=wexp_sb[0:TK, 0:rowlen], out_offset=None,
                in_=wsrc.rearrange("e r c -> e (r c)"),
                in_offset=IndirectOffsetOnAxis(wgi[:], 0),
                bounds_check=NE - 1, oob_is_err=False), 16)
        p.body("gpsimd", wexp_fn)

        def wst_fn(e, wdst=wdst, rowlen=rowlen):
            p.inc("st", e.dma_start(out=wdst, in_=wexp_sb[0:TK, 0:rowlen]), 16)
        p.body("gpsimd", wst_fn)

    def bb_fn(e):
        return e.matmul(ps_c[:, 384:448], onesf[:], brow_t[:], start=True,
                        stop=True)
    p.body("tensor", bb_fn)

    def bb_ev(e):
        return e.tensor_copy(bb128[:], ps_c[:, 384:448])
    p.body("vector", bb_ev)

    for i in range(NT):
        def sl_fn(e, i=i):
            ins = None
            for k in range(2):
                e.tensor_scalar(mc[:, 0:NE], iota64[:],
                                park[:, i * 6 + k:i * 6 + k + 1], None,
                                ALU.is_equal)
                e.drain()
                e.tensor_mul(rtmp[:], bb128[:], mc[:, 0:NE])
                e.drain()
                e.reduce_sum(grow[0:128, 6:7] if False else efp[:, 6:7],
                             rtmp[:], axis=AX.X)
                e.drain()
                ins = e.tensor_scalar(slotc[:, k:k + 1], efp[:, 6:7], 128.0,
                                      park[:, i * 6 + 2 + k:i * 6 + 3 + k],
                                      ALU.mult, ALU.add)
            return ins
        p.body("vector", sl_fn)

        def sst_fn(e, i=i):
            p.inc("st", e.dma_start(out=slot_d[i * 128:(i + 1) * 128, :],
                                    in_=slotc[:]), 16)
        p.body("gpsimd", sst_fn)

    def pst_fn(e):
        p.inc("st", e.dma_start(out=park_d, in_=park[:]), 16)
    p.body("gpsimd", pst_fn)


def emit_wprep_up(p, io, consts):
    nc = p.nc
    ones, onesf, onecolf, identity = consts
    aup_bf = nc.dram_tensor("aup_bf", [NE, 64, 32], BF16, kind="Internal").ap()
    bup_bf = nc.dram_tensor("bup_bf", [NE, 64, 32], BF16, kind="Internal").ap()
    btz_d = nc.dram_tensor("btz_d", [NE, 64, 128], BF16, kind="Internal").ap()
    atz_d = nc.dram_tensor("atz_d", [NE, 64, 128], BF16, kind="Internal").ap()

    def cast_fn(e):
        p.inc("gp", e.dma_start(out=aup_bf, in_=io["A_up"]), 16)
        p.inc("gp", e.dma_start(out=bup_bf, in_=io["B_up"]), 16)
    p.body("gpsimd", cast_fn)

    with ExitStack() as ctx:
        e_ = ctx.enter_context
        wld = e_(nc.sbuf_tensor(_u("mw_ld"), [128, 64], BF16))
        flat32 = e_(nc.sbuf_tensor(_u("mw_f32"), [32, NE * 64], BF16))
        zz = e_(nc.sbuf_tensor(_u("mw_zz"), [128, 128], BF16))
        ps_w = e_(nc.psum_tensor(_u("mw_ps"), [128, 1024], BF16))

        def z_fn(e):
            return e.memset(zz[:], 0.0)
        p.body("gpsimd", z_fn)

        def zfill_fn(e):
            for tgt in (btz_d, atz_d):
                flat = tgt.rearrange("e r c -> (e r) c")
                for r0 in range(0, NE * 64, 128):
                    p.inc("st", e.dma_start(out=flat[r0:r0 + 128, :],
                                            in_=zz[:]), 16)
        p.body("gpsimd", zfill_fn)

        for src, tgt in ((bup_bf, btz_d), (aup_bf, atz_d)):
            for pair in range(32):
                def ld_fn(e, src=src, pair=pair):
                    p.inc("ld", e.dma_start(
                        out=wld[:, 0:32],
                        in_=src[2 * pair:2 * pair + 2]
                            .rearrange("e r c -> (e r) c")), 16)
                p.body("sync", ld_fn)

                def tr_fn(e, pair=pair):
                    return e.transpose(
                        ps_w[0:32, (pair % 8) * 128:(pair % 8 + 1) * 128],
                        wld[:, 0:32], identity[:])
                p.body("tensor", tr_fn)

                def ev_fn(e, pair=pair):
                    return e.activation(
                        flat32[0:32, pair * 128:(pair + 1) * 128],
                        ps_w[0:32, (pair % 8) * 128:(pair % 8 + 1) * 128],
                        AF.Copy)
                p.body("scalar", ev_fn)

            def st_fn(e, tgt=tgt):
                p.inc("st", e.dma_start(
                    out=tgt[:, 0:32, 0:64].rearrange("e j q -> j e q"),
                    in_=flat32[0:32, :].rearrange("j (e q) -> j e q", e=NE)), 16)
                p.inc("st", e.dma_start(
                    out=tgt[:, 32:64, 64:128].rearrange("e j q -> j e q"),
                    in_=flat32[0:32, :].rearrange("j (e q) -> j e q", e=NE)), 16)
            p.body("gpsimd", st_fn)
    return btz_d, atz_d


def emit_wprep_dn(p, io, consts):
    nc = p.nc
    ones, onesf, onecolf, identity = consts
    adn_bf = nc.dram_tensor("adn_bf", [NE, 32, 64], BF16, kind="Internal").ap()
    bdn_bf = nc.dram_tensor("bdn_bf", [NE, 32, 64], BF16, kind="Internal").ap()
    bdt_d = nc.dram_tensor("bdt_d", [NE, 64, 32], BF16, kind="Internal").ap()
    adt_d = nc.dram_tensor("adt_d", [NE, 64, 32], BF16, kind="Internal").ap()

    def cast_fn(e):
        p.inc("gp", e.dma_start(out=adn_bf, in_=io["A_down"]), 16)
        p.inc("gp", e.dma_start(out=bdn_bf, in_=io["B_down"]), 16)
    p.body("gpsimd", cast_fn)

    with ExitStack() as ctx:
        e_ = ctx.enter_context
        wld = e_(nc.sbuf_tensor(_u("mw_ld"), [128, 64], BF16))
        flat64 = e_(nc.sbuf_tensor(_u("mw_f64"), [64, NE * 32], BF16))
        ps_w = e_(nc.psum_tensor(_u("mw_ps"), [128, 1024], BF16))
        for src, tgt in ((bdn_bf, bdt_d), (adn_bf, adt_d)):
            for quad in range(16):
                def ld_fn(e, src=src, quad=quad):
                    p.inc("ld", e.dma_start(
                        out=wld[:, 0:64],
                        in_=src[4 * quad:4 * quad + 4]
                            .rearrange("e r c -> (e r) c")), 16)
                p.body("sync", ld_fn)

                def tr_fn(e, quad=quad):
                    return e.transpose(
                        ps_w[0:64, (quad % 8) * 128:(quad % 8 + 1) * 128],
                        wld[:, 0:64], identity[:])
                p.body("tensor", tr_fn)

                def ev_fn(e, quad=quad):
                    return e.activation(
                        flat64[0:64, quad * 128:(quad + 1) * 128],
                        ps_w[0:64, (quad % 8) * 128:(quad % 8 + 1) * 128],
                        AF.Copy)
                p.body("scalar", ev_fn)

            def st_fn(e, tgt=tgt):
                p.inc("st", e.dma_start(
                    out=tgt.rearrange("e j q -> j e q"),
                    in_=flat64[0:64, :].rearrange("j (e q) -> j e q", e=NE)), 16)
            p.body("gpsimd", st_fn)
    return bdt_d, adt_d


def build_moe_route_up(p, io):
    consts = emit_consts(p)
    nc = p.nc
    wup_bf = nc.dram_tensor("wup_bf", [NE, D], BF16, kind="Internal").ap()

    def cast_fn(e):
        p.inc("gp", e.dma_start(out=wup_bf, in_=io["W_up"]), 16)
    p.body("gpsimd", cast_fn)

    wupT = p.sb("wupT", [128, NC * NE], BF16)
    load_wT(p, wupT, wup_bf, NE, NC)
    btz_d, atz_d = emit_wprep_up(p, io, consts)

    xnt = p.sb("mr_x", [128, D], BF16)

    def mk_logits(i, ps_r, lg_sb):
        def ld_fn(e, i=i):
            p.inc("ld", e.dma_start(
                out=xnt[:].rearrange("q (c t) -> q c t", c=NC),
                in_=io["xn1fm"][:, :, i * 128:(i + 1) * 128]), 16)
        p.body("sync", ld_fn)

        def lg_fn(e):
            for c in range(NC):
                ins = e.matmul(ps_r[:, 0:64], xnt[:, c * 128:(c + 1) * 128],
                               wupT[:, c * NE:(c + 1) * NE],
                               start=(c == 0), stop=(c == NC - 1))
            return ins
        p.body("tensor", lg_fn)

        def lg_ev(e):
            return e.tensor_copy(lg_sb[:], ps_r[:, 0:64])
        p.body("vector", lg_ev)

    io2 = dict(io)
    io2["mk_logits"] = mk_logits
    emit_routing(p, io2, consts, "up", io["slot_up"], io["park_up"],
                 [(btz_d, io["btl_up"]), (atz_d, io["atl_up"])])


def build_moe_sort(p, io, xw, c0, c1, fill_packed):
    nc = p.nc
    slot_d = io["slot_d"]
    park_d = io["park_d"]
    xrow = io["xrow"]
    xs_d = io["xs_d"]
    packed_d = io["packed_d"]
    if not fill_packed:
        def pt_fn(e):
            p.inc("st", e.dma_start(out=packed_d, in_=io["packed_in"]), 16)
            nrow = TK * 128
            step = nrow // 4
            for r0 in range(0, nrow, step):
                p.inc("st", e.dma_start(out=xs_d[r0:r0 + step, :],
                                        in_=io["xs_in"][r0:r0 + step, :]), 16)
        p.body("gpsimd", pt_fn)
    park = p.sb("park", [128, NT * 6], F32)
    tokio = p.sb("tokio", [128, 1], F32)
    pkq = p.sb("pkq", [128, 4], F32)
    pki = p.sb("pki", [128, 1], I32)
    slotc = p.sb("slotc", [128, 2], F32)
    xtile = p.sb("sx_x", [128, xw], BF16)
    xptile = p.sb("sx_xp", [128, xw], BF16)
    zf = p.sb("zf", [128, 508], F32)

    def cst_fn(e):
        e.iota(tokio[:], pattern=[[0, 1]], base=0, channel_multiplier=1,
               allow_small_or_imprecise_dtypes=True)
        if fill_packed:
            return e.memset(zf[:], 1e9)
        return e.memset(zf[0:1, 0:1], 0.0)
    p.body("gpsimd", cst_fn)

    if fill_packed:
        def fp_fn(e):
            p.inc("st", e.dma_start(
                out=packed_d.rearrange("(q r) c -> q (r c)", q=128),
                in_=zf[:]), 16)
        p.body("gpsimd", fp_fn)

    def ldp_fn(e):
        p.inc("ld", e.dma_start(out=park[:], in_=park_d), 16)
    p.body("sync", ldp_fn)

    for i in range(c0, c1):
        def ldx_fn(e, i=i):
            p.inc("ld", e.dma_start(out=xtile[:],
                                    in_=xrow[i * 128:(i + 1) * 128, :]), 16)
            p.inc("ld", e.dma_start(out=slotc[:],
                                    in_=slot_d[i * 128:(i + 1) * 128, :]), 16)
        p.body("sync", ldx_fn)

        for k in range(2):
            def pk_fn(e, i=i, k=k):
                e.tensor_scalar(pkq[:, 0:1], tokio[:], float(i * 128), None,
                                ALU.add)
                e.tensor_scalar(pkq[:, 1:2], tokio[:],
                                float(i * 128 + k * S), None, ALU.add)
                e.tensor_copy(pkq[:, 2:3],
                              park[:, i * 6 + 4 + k:i * 6 + 5 + k])
                e.tensor_scalar(pkq[:, 3:4], tokio[:], 0.0, None, ALU.mult)
                e.tensor_copy(pki[:], slotc[:, k:k + 1])
                e.drain()
                return e.tensor_scalar(
                    xptile[:], xtile[:],
                    park[:, i * 6 + 4 + k:i * 6 + 5 + k], None, ALU.mult)
            p.body("vector", pk_fn)

            def sc_fn(e, i=i, k=k):
                p.inc("gp", e.indirect_dma_start(
                    out=packed_d, out_offset=IndirectOffsetOnAxis(pki[:], 0),
                    in_=pkq[:], in_offset=None,
                    bounds_check=TK * 128 - 1, oob_is_err=False), 16)
                p.inc("gp", e.indirect_dma_start(
                    out=xs_d, out_offset=IndirectOffsetOnAxis(pki[:], 0),
                    in_=xptile[:], in_offset=None,
                    bounds_check=TK * 128 - 1, oob_is_err=False), 16)
            p.body("gpsimd", sc_fn)


def build_moe_tiles_seg(p, io, stage, j0, j1, prezero):
    nc = p.nc
    consts = emit_consts(p)
    ones, onesf, onecolf, identity = consts
    up = (stage == "up")
    XW = D if up else FF
    YW = FF if up else D
    packed_d = io["packed_d"]
    xs_d = io["xs_d"]
    btl_d = io["btl_d"]
    atl_d = io["atl_d"]
    ypl_d = io["ypl_d"]
    zzb = p.sb("zzb", [128, 8192], BF16)

    if not prezero:
        def pt_fn(e):
            nrow = 2 * S
            step = nrow // 4
            for r0 in range(0, nrow, step):
                p.inc("st", e.dma_start(out=ypl_d[r0:r0 + step, :],
                                        in_=io["ypl_in"][r0:r0 + step, :]), 16)
        p.body("gpsimd", pt_fn)

    if prezero:
        def z0_fn(e):
            return e.memset(zzb[:], 0.0)
        p.body("gpsimd", z0_fn)
        def zp_fn(e):
            step = 8192 // YW * 128
            for r0 in range(0, 2 * S, 128):
                p.inc("st", e.dma_start(out=ypl_d[r0:r0 + 128, :],
                                        in_=zzb[:, 0:YW]), 16)
        p.body("gpsimd", zp_fn)

    with ExitStack() as ctx:
        e_ = ctx.enter_context
        pk = e_(nc.sbuf_tensor(_u("mt_pk"), [128, 4], F32))
        pki = e_(nc.sbuf_tensor(_u("mt_pki"), [128, 1], I32))
        xg = e_(nc.sbuf_tensor(_u("mt_xg"), [128, XW], BF16))
        xt = e_(nc.sbuf_tensor(_u("mt_xt"), [128, 2 * XW if up else XW], BF16))
        wb = e_(nc.sbuf_tensor(_u("mt_wb"), [128, 128], BF16))
        wa = e_(nc.sbuf_tensor(_u("mt_wa"), [128, 128], BF16))
        tsb = e_(nc.sbuf_tensor(_u("mt_tsb"), [128, 2048], BF16))
        ttb = e_(nc.sbuf_tensor(_u("mt_ttb"), [128, 2048], BF16))
        yrow = e_(nc.sbuf_tensor(_u("mt_yr"), [128, YW], BF16))
        ps_t1 = e_(nc.psum_tensor(_u("mt_p1"), [128, 1024], BF16))
        ps_ta = e_(nc.psum_tensor(_u("mt_pa"), [128, 512], F32))
        ps_tb = e_(nc.psum_tensor(_u("mt_pb"), [128, 512], F32))
        ps_t2 = e_(nc.psum_tensor(_u("mt_p2"), [128, 1024], BF16))
        ps_ya = e_(nc.psum_tensor(_u("mt_ya"), [128, 512], F32))
        ps_yb = e_(nc.psum_tensor(_u("mt_yb"), [128, 512], F32))

        for j in range(j0, j1):
            def ld_fn(e, j=j):
                p.inc("ld", e.dma_start(
                    out=pk[:], in_=packed_d[j * 128:(j + 1) * 128, :]), 16)
                p.inc("ld", e.dma_start(
                    out=xg[:], in_=xs_d[j * 128:(j + 1) * 128, :]), 16)
                if up:
                    p.inc("ld", e.dma_start(
                        out=wb[0:64, :],
                        in_=btl_d[j].rearrange("(r c) -> r c", r=64)), 16)
                    for hb in range(2):
                        p.inc("ld", e.dma_start(
                            out=wa[hb * 64:(hb + 1) * 64, :],
                            in_=atl_d[j].rearrange("(r c) -> r c", r=64)), 16)
                else:
                    for hb in range(2):
                        p.inc("ld", e.dma_start(
                            out=wb[hb * 64:(hb + 1) * 64, 0:32],
                            in_=btl_d[j].rearrange("(r c) -> r c", r=64)), 16)
                        p.inc("ld", e.dma_start(
                            out=wa[hb * 64:(hb + 1) * 64, 0:32],
                            in_=atl_d[j].rearrange("(r c) -> r c", r=64)), 16)
            p.body("sync", ld_fn)

            def cvt_fn(e):
                return e.tensor_copy(pki[:], pk[:, 1:2])
            p.body("vector", cvt_fn)

            if up:
                for w in range(2):
                    def t1_fn(e, w=w):
                        ins = None
                        for q in range(8):
                            i2 = w * 8 + q
                            ins = e.transpose(
                                ps_t1[0:64, q * 128:(q + 1) * 128],
                                xg[:, i2 * 64:(i2 + 1) * 64], identity[:])
                        return ins
                    p.body("tensor", t1_fn)

                    def t1_ev(e, w=w):
                        return e.activation(xt[0:64, w * 1024:(w + 1) * 1024],
                                            ps_t1[0:64, :], AF.Copy)
                    p.body("scalar", t1_ev)
            else:
                for w in range(4):
                    def t1_fn(e, w=w):
                        ins = None
                        for q in range(8):
                            c = w * 8 + q
                            ins = e.transpose(
                                ps_t1[:, q * 128:(q + 1) * 128],
                                xg[:, c * 128:(c + 1) * 128], identity[:])
                        return ins
                    p.body("tensor", t1_fn)

                    def t1_ev(e, w=w):
                        return e.activation(xt[:, w * 1024:(w + 1) * 1024],
                                            ps_t1[:], AF.Copy)
                    p.body("scalar", t1_ev)

            if up:
                for w in range(4):
                    def s1_fn(e, w=w):
                        ins = None
                        for q in range(8):
                            ii = w * 8 + q
                            i2, v = ii // 2, ii % 2
                            ins = e.matmul(
                                ps_ta[:, q * 64:(q + 1) * 64],
                                xt[0:64, i2 * 128:(i2 + 1) * 128],
                                wb[0:64, v * 64:(v + 1) * 64],
                                start=True, stop=True)
                        return ins
                    p.body("tensor", s1_fn)

                    def s1_ev(e, w=w):
                        return e.tensor_copy(
                            tsb[:].rearrange("q (o i) -> q o i", i=32)
                                  [:, :, w * 8:(w + 1) * 8],
                            ps_ta[:, 0:512].rearrange("q (i o) -> q o i", i=8))
                    p.body("vector", s1_ev)
            else:
                for w in range(4):
                    par2 = w % 2
                    psu = ps_ta if par2 == 0 else ps_tb
                    base = 64 * par2

                    def s1_fn(e, w=w, psu=psu, base=base):
                        ins = None
                        for q in range(16):
                            c = (w // 2) * 16 + q
                            ins = e.matmul(
                                psu[:, q * 32:(q + 1) * 32],
                                xt[base:base + 64, c * 128:(c + 1) * 128],
                                wb[base:base + 64, 0:32],
                                start=True, stop=True)
                        return ins
                    p.body("tensor", s1_fn)

                    def s1_ev(e, w=w, psu=psu):
                        return e.tensor_copy(
                            tsb[:].rearrange("q (o i2 r) -> q o i2 r",
                                             i2=32, r=2)
                                  [:, :, (w // 2) * 16:(w // 2) * 16 + 16,
                                   (w % 2):(w % 2) + 1]
                                  .rearrange("q o i2 r -> q o (i2 r)"),
                            psu[:, 0:512].rearrange("q (i o) -> q o i", i=16))
                    p.body("vector", s1_ev)

            for w in range(2):
                def t2_fn(e, w=w):
                    ins = None
                    for q in range(8):
                        c = w * 8 + q
                        ins = e.transpose(ps_t2[:, q * 128:(q + 1) * 128],
                                          tsb[:, c * 128:(c + 1) * 128],
                                          identity[:])
                    return ins
                p.body("tensor", t2_fn)

                def t2_ev(e, w=w):
                    return e.activation(ttb[:, w * 1024:(w + 1) * 1024],
                                        ps_t2[:], AF.Copy)
                p.body("scalar", t2_ev)

            if up:
                for g in range(8):
                    def s2_fn(e, g=g):
                        ins = None
                        for b2 in range(2):
                            blk = g * 2 + b2
                            for v in range(2):
                                sl = b2 * 2 + v
                                ins = e.matmul(
                                    ps_ya[:, sl * 64:(sl + 1) * 64],
                                    ttb[0:64, blk * 128:(blk + 1) * 128],
                                    wa[0:64, v * 64:(v + 1) * 64],
                                    start=True, stop=True)
                        for b2 in range(2):
                            blk = g * 2 + b2
                            for v in range(2):
                                sl = b2 * 2 + v
                                ins = e.matmul(
                                    ps_yb[:, sl * 64:(sl + 1) * 64],
                                    ttb[64:128, blk * 128:(blk + 1) * 128],
                                    wa[64:128, v * 64:(v + 1) * 64],
                                    start=True, stop=True)
                        return ins
                    p.body("tensor", s2_fn)

                    def s2_ev(e, g=g):
                        e.tensor_copy(
                            yrow[:].rearrange("q (o pp) -> q o pp", pp=64)
                                   [:, :, 8 * g:8 * g + 8]
                                   .rearrange("q o (b2 vv) -> q o b2 vv", b2=2)
                                   [:, :, :, 0:2],
                            ps_ya[:, 0:256]
                                .rearrange("q (b2 v o) -> q o b2 v", b2=2, v=2))
                        e.drain()
                        return e.tensor_copy(
                            yrow[:].rearrange("q (o pp) -> q o pp", pp=64)
                                   [:, :, 8 * g:8 * g + 8]
                                   .rearrange("q o (b2 vv) -> q o b2 vv", b2=2)
                                   [:, :, :, 2:4],
                            ps_yb[:, 0:256]
                                .rearrange("q (b2 v o) -> q o b2 v", b2=2, v=2))
                    p.body("vector", s2_ev)
            else:
                for w in range(2):
                    psu = ps_ya if w == 0 else ps_yb
                    base = 64 * w

                    def s2_fn(e, w=w, psu=psu, base=base):
                        ins = None
                        for q in range(16):
                            ins = e.matmul(
                                psu[:, q * 32:(q + 1) * 32],
                                ttb[base:base + 64, q * 128:(q + 1) * 128],
                                wa[base:base + 64, 0:32],
                                start=True, stop=True)
                        return ins
                    p.body("tensor", s2_fn)

                    def s2_ev(e, w=w, psu=psu):
                        return e.tensor_copy(
                            yrow[:].rearrange("q (o p2 r) -> q o p2 r",
                                              p2=16, r=2)
                                   [:, :, :, w:w + 1]
                                   .rearrange("q o p2 r -> q o (p2 r)"),
                            psu[:, 0:512].rearrange("q (i o) -> q o i", i=16))
                    p.body("vector", s2_ev)

            def sc_fn(e, j=j):
                p.inc("gp", e.indirect_dma_start(
                    out=ypl_d, out_offset=IndirectOffsetOnAxis(pki[:], 0),
                    in_=yrow[:], in_offset=None,
                    bounds_check=2 * S - 1, oob_is_err=False), 16)
            p.body("gpsimd", sc_fn)


def build_moe_combine_up(p, io):
    nc = p.nc
    consts = emit_consts(p)
    ones, onesf, onecolf, identity = consts
    wdn_bf = nc.dram_tensor("wdn_bf", [NE, FF], BF16, kind="Internal").ap()

    def cast_fn(e):
        p.inc("gp", e.dma_start(out=wdn_bf, in_=io["W_down"]), 16)
    p.body("gpsimd", cast_fn)

    wdnT = p.sb("wdnT", [128, NF * NE], BF16)
    load_wT(p, wdnT, wdn_bf, NE, NF)
    bdt_d, adt_d = emit_wprep_dn(p, io, consts)

    sclcol = p.sb("sclcol", [128, 1], F32)
    biasup = p.sb("biasup", [128, FF], F32)
    with ExitStack() as ctx:
        e_ = ctx.enter_context
        srow = e_(nc.sbuf_tensor(_u("mb_s"), [1, 1], F32))
        burow = e_(nc.sbuf_tensor(_u("mb_u"), [1, FF], F32))
        ps_bc = e_(nc.psum_tensor(_u("mb_ps"), [128, 512], F32))

        def ld_sb(e):
            p.inc("ld", e.dma_start(out=srow[:], in_=io["scale_up"]), 16)
            p.inc("ld", e.dma_start(out=burow[:], in_=io["bias_up"]), 16)
        p.body("sync", ld_sb)

        def bc_fn(e):
            return e.matmul(ps_bc[:, 0:1], onesf[:], srow[:], start=True,
                            stop=True)
        p.body("tensor", bc_fn)

        def bc_ev(e):
            return e.tensor_copy(sclcol[:], ps_bc[:, 0:1])
        p.body("vector", bc_ev)

        for j in range(FF // 512):
            def bb_fn(e, j=j):
                return e.matmul(ps_bc[:, 0:512], onesf[:],
                                burow[:, j * 512:(j + 1) * 512],
                                start=True, stop=True)
            p.body("tensor", bb_fn)

            def bb_ev(e, j=j):
                return e.tensor_copy(biasup[:, j * 512:(j + 1) * 512],
                                     ps_bc[:, 0:512])
            p.body("vector", bb_ev)

    y0 = p.sb("cu_y0", [128, FF], BF16)
    y1 = p.sb("cu_y1", [128, FF], BF16)
    hsum = p.sb("cu_s", [128, FF], F32)
    hbf = p.sb("cu_h", [128, FF], BF16)
    hfm_t = p.sb("cu_hf", [128, FF], BF16)
    ps_tr2 = p.ps("cu_ps", [128, 1024], BF16)

    def mk_logits(i, ps_r, lg_sb):
        def ld_fn(e, i=i):
            p.inc("ld", e.dma_start(
                out=y0[:], in_=io["ypl_up"][i * 128:(i + 1) * 128, :]), 16)
            p.inc("ld", e.dma_start(
                out=y1[:], in_=io["ypl_up"][S + i * 128:S + (i + 1) * 128, :]), 16)
        p.body("sync", ld_fn)

        def cmb_fn(e):
            e.tensor_add(hsum[:], y0[:], y1[:])
            e.drain()
            return e.scalar_tensor_tensor(hsum[:], hsum[:], sclcol[:],
                                          biasup[:], ALU.mult, ALU.add)
        p.body("vector", cmb_fn)

        def gel_fn(e):
            return e.activation(hbf[:], hsum[:], AF.Gelu)
        p.body("scalar", gel_fn)

        def sth_fn(e, i=i):
            p.inc("st", e.dma_start(out=io["hrow_d"][i * 128:(i + 1) * 128, :],
                                    in_=hbf[:]), 16)
        p.body("gpsimd", sth_fn)

        for half in range(4):
            def tr_fn(e, half=half):
                ins = None
                for c in range(8):
                    cc = half * 8 + c
                    ins = e.transpose(ps_tr2[:, c * 128:(c + 1) * 128],
                                      hbf[:, cc * 128:(cc + 1) * 128],
                                      identity[:])
                return ins
            p.body("tensor", tr_fn)

            def tr_ev(e, half=half):
                return e.activation(hfm_t[:, half * 1024:(half + 1) * 1024],
                                    ps_tr2[:], AF.Copy)
            p.body("scalar", tr_ev)

        def lg_fn(e):
            for c in range(NF):
                ins = e.matmul(ps_r[:, 0:64], hfm_t[:, c * 128:(c + 1) * 128],
                               wdnT[:, c * NE:(c + 1) * NE],
                               start=(c == 0), stop=(c == NF - 1))
            return ins
        p.body("tensor", lg_fn)

        def lg_ev(e):
            return e.tensor_copy(lg_sb[:], ps_r[:, 0:64])
        p.body("vector", lg_ev)

    io2 = dict(io)
    io2["mk_logits"] = mk_logits
    emit_routing(p, io2, consts, "dn", io["slot_dn"], io["park_dn"],
                 [(bdt_d, io["btl_dn"]), (adt_d, io["atl_dn"])])


def build_moe_final(p, io):
    nc = p.nc
    consts = emit_consts(p)
    ones, onesf, onecolf, identity = consts
    sclcol = p.sb("sclcol", [128, 1], F32)
    biasdn = p.sb("biasdn", [128, D], F32)
    with ExitStack() as ctx:
        e_ = ctx.enter_context
        srow = e_(nc.sbuf_tensor(_u("mb_s"), [1, 1], F32))
        bdrow = e_(nc.sbuf_tensor(_u("mb_d"), [1, D], F32))
        ps_bc = e_(nc.psum_tensor(_u("mb_ps"), [128, 512], F32))

        def ld_sb(e):
            p.inc("ld", e.dma_start(out=srow[:], in_=io["scale_down"]), 16)
            p.inc("ld", e.dma_start(out=bdrow[:], in_=io["bias_down"]), 16)
        p.body("sync", ld_sb)

        def bc_fn(e):
            return e.matmul(ps_bc[:, 0:1], onesf[:], srow[:], start=True,
                            stop=True)
        p.body("tensor", bc_fn)

        def bc_ev(e):
            return e.tensor_copy(sclcol[:], ps_bc[:, 0:1])
        p.body("vector", bc_ev)

        for j in range(D // 512):
            def bb_fn(e, j=j):
                return e.matmul(ps_bc[:, 0:512], onesf[:],
                                bdrow[:, j * 512:(j + 1) * 512],
                                start=True, stop=True)
            p.body("tensor", bb_fn)

            def bb_ev(e, j=j):
                return e.tensor_copy(biasdn[:, j * 512:(j + 1) * 512],
                                     ps_bc[:, 0:512])
            p.body("vector", bb_ev)

    y0 = p.sb("cf_y0", [128, D], BF16)
    y1 = p.sb("cf_y1", [128, D], BF16)
    rr = p.sb("cf_r", [128, D], F32)
    acc = p.sb("cf_a", [128, D], F32)
    ob = [p.sb("cf_o", [128, D], F32) for _ in range(2)]
    for i in range(NT):
        par = i % 2

        def ld_fn(e, i=i):
            p.inc("ld", e.dma_start(
                out=y0[:], in_=io["ypl_dn"][i * 128:(i + 1) * 128, :]), 16)
            p.inc("ld", e.dma_start(
                out=y1[:], in_=io["ypl_dn"][S + i * 128:S + (i + 1) * 128, :]), 16)
            p.inc("ld", e.dma_start(out=rr[:],
                                    in_=io["r2"][i * 128:(i + 1) * 128, :]), 16)
        p.body("sync", ld_fn)

        def cmb_fn(e, par=par):
            e.tensor_add(acc[:], y0[:], y1[:])
            e.drain()
            e.scalar_tensor_tensor(acc[:], acc[:], sclcol[:], biasdn[:],
                                   ALU.mult, ALU.add)
            e.drain()
            return e.tensor_add(ob[par][:], acc[:], rr[:])
        p.body("vector", cmb_fn)

        def st_fn(e, i=i, par=par):
            p.inc("st", e.dma_start(out=io["out"][i * 128:(i + 1) * 128, :],
                                    in_=ob[par][:]), 16)
        p.body("gpsimd", st_fn)


def _moe_seg_program(seg, **kw):
    p = Prog()
    nc = p.nc
    io = {}

    def din(nm, shape, dt=F32):
        io[nm] = nc.dram_tensor(nm, shape, dt, kind="ExternalInput").ap()

    def dout(nm, shape, dt=F32):
        io[nm] = nc.dram_tensor(nm, shape, dt, kind="ExternalOutput").ap()

    if seg == "route_up":
        din("xn1fm", [128, NC, S], BF16)
        din("W_up", [NE, D]); din("A_up", [NE, 64, 32]); din("B_up", [NE, 64, 32])
        dout("slot_up", [S, 2]); dout("park_up", [128, NT * 6])
        dout("btl_up", [TK, 64 * 128], BF16); dout("atl_up", [TK, 64 * 128], BF16)
    elif seg == "sort_up":
        din("slot_d", [S, 2]); din("park_d", [128, NT * 6])
        din("xrow", [S, D], BF16)
        if kw.get("c0", 0) > 0:
            din("packed_in", [TK * 128, 4]); din("xs_in", [TK * 128, D], BF16)
        dout("packed_d", [TK * 128, 4]); dout("xs_d", [TK * 128, D], BF16)
    elif seg == "tiles_up":
        din("packed_d", [TK * 128, 4]); din("xs_d", [TK * 128, D], BF16)
        din("btl_d", [TK, 64 * 128], BF16); din("atl_d", [TK, 64 * 128], BF16)
        if kw.get("j0", 0) > 0:
            din("ypl_in", [2 * S, FF], BF16)
        dout("ypl_d", [2 * S, FF], BF16)
    elif seg == "combine_up":
        din("ypl_up", [2 * S, FF], BF16)
        din("scale_up", [1, 1]); din("bias_up", [1, FF])
        din("W_down", [NE, FF]); din("A_down", [NE, 32, 64]); din("B_down", [NE, 32, 64])
        dout("hrow_d", [S, FF], BF16)
        dout("slot_dn", [S, 2]); dout("park_dn", [128, NT * 6])
        dout("btl_dn", [TK, 64 * 32], BF16); dout("atl_dn", [TK, 64 * 32], BF16)
    elif seg == "sort_dn":
        din("slot_d", [S, 2]); din("park_d", [128, NT * 6])
        din("xrow", [S, FF], BF16)
        if kw.get("c0", 0) > 0:
            din("packed_in", [TK * 128, 4]); din("xs_in", [TK * 128, FF], BF16)
        dout("packed_d", [TK * 128, 4]); dout("xs_d", [TK * 128, FF], BF16)
    elif seg == "tiles_dn":
        din("packed_d", [TK * 128, 4]); din("xs_d", [TK * 128, FF], BF16)
        din("btl_d", [TK, 64 * 32], BF16); din("atl_d", [TK, 64 * 32], BF16)
        if kw.get("j0", 0) > 0:
            din("ypl_in", [2 * S, D], BF16)
        dout("ypl_d", [2 * S, D], BF16)
    elif seg == "final":
        din("ypl_dn", [2 * S, D], BF16); din("r2", [S, D])
        din("scale_down", [1, 1]); din("bias_down", [1, D])
        dout("out", [S, D])

    with p.ctx:
        p.blk = p.ctx.enter_context(nc.Block())
        if seg == "route_up":
            build_moe_route_up(p, io)
        elif seg == "sort_up":
            build_moe_sort(p, io, D, kw["c0"], kw["c1"], kw["c0"] == 0)
        elif seg == "tiles_up":
            build_moe_tiles_seg(p, io, "up", kw["j0"], kw["j1"], kw["j0"] == 0)
        elif seg == "combine_up":
            build_moe_combine_up(p, io)
        elif seg == "sort_dn":
            build_moe_sort(p, io, FF, kw["c0"], kw["c1"], kw["c0"] == 0)
        elif seg == "tiles_dn":
            build_moe_tiles_seg(p, io, "dn", kw["j0"], kw["j1"], kw["j0"] == 0)
        elif seg == "final":
            build_moe_final(p, io)
        p.finish()
    return p


def _spmd(p, maps):
    return run_bass_kernel_spmd(p.nc, maps, core_ids=list(range(8))).results


SORT_SPLITS = ((0, 11), (11, 22), (22, 32))
TILE_SPLITS = ((0, 43), (43, 86), (86, TK))


def run_moe(r2_full, xn1fm_full, xn1row_full, inputs):
    def W(nm):
        v = np.ascontiguousarray(inputs["L1_ffn_" + nm], dtype=np.float32)
        if nm.startswith("scale"):
            v = v.reshape(1, 1)
        elif nm.startswith("bias"):
            v = v.reshape(1, -1)
        return v

    def stage(xrow_full, route_fn, sort_seg, tiles_seg):
        slot_d, park_d, btl, atl = route_fn()
        packed, xs = None, None
        for c0, c1 in SORT_SPLITS:
            p = _moe_seg_program(sort_seg, c0=c0, c1=c1)
            maps = []
            for b in range(8):
                m = {"slot_d": slot_d[b], "park_d": park_d[b],
                     "xrow": np.ascontiguousarray(xrow_full[b])}
                if c0 > 0:
                    m["packed_in"] = packed[b]
                    m["xs_in"] = xs[b]
                maps.append(m)
            res = _spmd(p, maps)
            packed = [r["packed_d"] for r in res]
            xs = [r["xs_d"] for r in res]
        ypl = None
        for j0, j1 in TILE_SPLITS:
            p = _moe_seg_program(tiles_seg, j0=j0, j1=j1)
            maps = []
            for b in range(8):
                m = {"packed_d": packed[b], "xs_d": xs[b],
                     "btl_d": btl[b], "atl_d": atl[b]}
                if j0 > 0:
                    m["ypl_in"] = ypl[b]
                maps.append(m)
            res = _spmd(p, maps)
            ypl = [r["ypl_d"] for r in res]
        return ypl

    def route_up():
        p = _moe_seg_program("route_up")
        res = _spmd(p, [{"xn1fm": np.ascontiguousarray(xn1fm_full[b]),
                         "W_up": W("W_up"), "A_up": W("A_up"),
                         "B_up": W("B_up")} for b in range(8)])
        return ([r["slot_up"] for r in res], [r["park_up"] for r in res],
                [r["btl_up"] for r in res], [r["atl_up"] for r in res])

    ypl_up = stage(xn1row_full, route_up, "sort_up", "tiles_up")

    p = _moe_seg_program("combine_up")
    res = _spmd(p, [{"ypl_up": ypl_up[b], "scale_up": W("scale_up"),
                     "bias_up": W("bias_up"), "W_down": W("W_down"),
                     "A_down": W("A_down"), "B_down": W("B_down")}
                    for b in range(8)])
    hrow = [r["hrow_d"] for r in res]
    dn_state = ([r["slot_dn"] for r in res], [r["park_dn"] for r in res],
                [r["btl_dn"] for r in res], [r["atl_dn"] for r in res])

    ypl_dn = stage(hrow, lambda: dn_state, "sort_dn", "tiles_dn")

    p = _moe_seg_program("final")
    res = _spmd(p, [{"ypl_dn": ypl_dn[b],
                     "r2": np.ascontiguousarray(r2_full[b], dtype=np.float32),
                     "scale_down": W("scale_down"),
                     "bias_down": W("bias_down")} for b in range(8)])
    return np.stack([r["out"] for r in res], axis=0)


def build_l0_program():
    p = Prog()
    nc = p.nc
    io = {}
    io["x"] = nc.dram_tensor("x", [S, D], F32, kind="ExternalInput").ap()
    for nm, shape in (("Wq", [D, D]), ("Wk", [D, D]), ("Wv", [D, D]),
                      ("Wo", [D, D]), ("bq", [1, D]), ("bk", [1, D]), ("bv", [1, D]),
                      ("bo", [1, D]), ("W1", [FF, D]), ("b1", [1, FF]),
                      ("W2", [D, FF]), ("b2", [1, D])):
        io[nm] = nc.dram_tensor(nm, shape, F32, kind="ExternalInput").ap()
    io["r1"] = nc.dram_tensor("r1", [S, D], F32, kind="ExternalOutput").ap()
    io["xnfm"] = nc.dram_tensor("xnfm", [128, NC, S], BF16,
                                kind="ExternalOutput").ap()
    with p.ctx:
        p.blk = p.ctx.enter_context(nc.Block())
        build_l0(p, io)
        p.finish()
    return p


L0_KEYS = (("Wq", "L0_attn_Wq"), ("Wk", "L0_attn_Wk"), ("Wv", "L0_attn_Wv"),
           ("Wo", "L0_attn_Wo"), ("bq", "L0_attn_bq"), ("bk", "L0_attn_bk"),
           ("bv", "L0_attn_bv"), ("bo", "L0_attn_bo"), ("W1", "L0_ffn_W1"),
           ("b1", "L0_ffn_b1"), ("W2", "L0_ffn_W2"), ("b2", "L0_ffn_b2"))


def run_l0(inputs):
    x_full = np.ascontiguousarray(inputs["L0_attn_x"], dtype=np.float32)
    p = build_l0_program()
    in_maps = []
    for b in range(8):
        m = {"x": x_full[b]}
        for nm, key in L0_KEYS:
            v = np.ascontiguousarray(inputs[key], dtype=np.float32)
            if nm.startswith("b"):
                v = v.reshape(1, -1)
            m[nm] = v
        in_maps.append(m)
    res = run_bass_kernel_spmd(p.nc, in_maps, core_ids=list(range(8)))
    r1 = np.stack([r["r1"] for r in res.results], axis=0)
    xnfm = np.stack([r["xnfm"] for r in res.results], axis=0)
    return r1, xnfm


def kernel(**inputs):
    r1, _ = run_l0(inputs)
    r2, xn1fm, xn1row = run_l1a(r1, inputs)
    out = run_moe(r2, xn1fm, xn1row, inputs)
    return out


# version 05
# speedup vs baseline: 1.2053x; 1.0168x over previous
import numpy as np
from contextlib import ExitStack

import concourse.bass as bass
import concourse.mybir as mybir
from concourse.bass import IndirectOffsetOnAxis
from concourse.bass_utils import run_bass_kernel_spmd

F32 = mybir.dt.float32
BF16 = mybir.dt.bfloat16
I32 = mybir.dt.int32
U32 = mybir.dt.uint32
AF = mybir.ActivationFunctionType
ALU = mybir.AluOpType
AX = mybir.AxisListType

S = 4096
D = 1024
FF = 4096
NT = S // 128
NC = D // 128
NF = FF // 128

ENG_CH = {"sync": "ld", "tensor": "pe", "vector": "ve",
          "scalar": "ac", "gpsimd": "gp"}

_UID = [0]


def _u(base):
    _UID[0] += 1
    return f"{base}_{_UID[0]}"


class Prog:
    def __init__(self):
        self.nc = bass.Bass("TRN2", target_bir_lowering=False)
        self.ctx = ExitStack()
        self.sems = {}
        self.counts = {}
        self.blk = None
        self._waited = {}
        for ch in ("ld", "pe", "ve", "ac", "gp", "st"):
            self.sem(ch)

    def sem(self, name):
        if name not in self.sems:
            self.sems[name] = self.ctx.enter_context(self.nc.semaphore(name))
            self.counts[name] = 0
        return self.sems[name]

    def inc(self, name, instr, k=1):
        instr.then_inc(self.sem(name), k)
        self.counts[name] += k
        return instr

    def body(self, eng, fn):
        ch = ENG_CH[eng]
        seen = self._waited.setdefault(eng, {})
        own = {"tensor": ("pe", "st"), "vector": ("ve",), "scalar": ("ac",),
               "sync": (), "gpsimd": ("pe",)}[eng]
        vals = {}
        for n, v in self.counts.items():
            if v > 0 and n not in own and seen.get(n, 0) < v:
                vals[n] = v
                seen[n] = v

        def run(e):
            for n, v in vals.items():
                e.wait_ge(self.sems[n], v)
            if eng in ("vector", "scalar", "gpsimd"):
                e.drain()
            last = fn(e)
            if last is not None:
                self.inc(ch, last)
        getattr(self.blk, eng)(run)

    def finish(self):
        vals = {n: v for n, v in self.counts.items() if v > 0}

        def run(e):
            for n, v in vals.items():
                e.wait_ge(self.sems[n], v)
        self.blk.sync(run)

    def sb(self, name, shape, dtype, **kw):
        return self.ctx.enter_context(self.nc.sbuf_tensor(_u(name), shape, dtype, **kw))

    def ps(self, name, shape, dtype=F32):
        return self.ctx.enter_context(self.nc.psum_tensor(_u(name), shape, dtype))


def load_wT(p, wT_sb, w_bf, rows, chunks):
    def fn(e):
        for c in range(chunks):
            p.inc("ld", e.dma_start(out=wT_sb[:, c * rows:(c + 1) * rows],
                                    in_=w_bf[:, c * 128:(c + 1) * 128],
                                    transpose=True), 16)
    p.body("sync", fn)


def emit_consts(p):
    ones = p.sb("ones", [1, 512], BF16)
    onesf = p.sb("onesf", [1, 128], F32)
    onecolf = p.sb("onecolf", [128, 1], F32)
    identity = p.sb("identity", [128, 128], BF16)
    constf = p.sb("constf", [128, 4], F32)
    p.constf = constf

    def fn(e):
        e.memset(ones[:], 1.0)
        e.memset(onesf[:], 1.0)
        e.memset(onecolf[:], 1.0)
        e.memset(constf[:, 0:1], 1e-5)
        e.memset(constf[:, 1:2], float(1.0 / np.sqrt(512.0)))
        e.memset(constf[:, 2:3], float(1.0 / np.sqrt(64.0)))
        e.memset(identity[:], 0.0)
        return e.affine_select(out=identity[:], in_=identity[:],
                               compare_op=ALU.not_equal, fill=1.0, base=0,
                               pattern=[[-1, 128]], channel_multiplier=1)
    p.body("gpsimd", fn)
    return ones, onesf, onecolf, identity



def emit_ln_transpose(p, x_d, xnfm_d, identity, xnrow_d=None):
    nc = p.nc
    with ExitStack() as ctx:
        e_ = ctx.enter_context
        xt = [e_(nc.sbuf_tensor(_u(f"ln_x{i}"), [128, D], F32)) for i in range(2)]
        xnb = [e_(nc.sbuf_tensor(_u(f"ln_n{i}"), [128, D], BF16)) for i in range(2)]
        xnT = [e_(nc.sbuf_tensor(_u(f"ln_t{i}"), [128, D], BF16)) for i in range(2)]
        sq = e_(nc.sbuf_tensor(_u("ln_sq"), [128, D], F32))
        st = e_(nc.sbuf_tensor(_u("ln_st"), [128, 8], F32))
        ptT = e_(nc.psum_tensor(_u("ln_pt"), [128, D], BF16))

        for i in range(NT):
            par = i % 2

            def ld_fn(e, i=i, par=par):
                p.inc("ld", e.dma_start(out=xt[par][:],
                                        in_=x_d[i * 128:(i + 1) * 128, :]), 16)
            p.body("sync", ld_fn)

            def sq_fn(e, par=par):
                return e.activation(sq[:], xt[par][:], AF.Square,
                                    accum_out=st[:, 0:1])
            p.body("scalar", sq_fn)

            def ve_fn(e, par=par):
                e.reduce_sum(st[:, 2:3], xt[par][:], axis=AX.X)
                e.drain()
                e.tensor_scalar(st[:, 3:4], st[:, 2:3], 1.0 / D, None, ALU.mult)
                e.tensor_scalar(st[:, 4:5], st[:, 0:1], 1.0 / D, None, ALU.mult)
                e.drain()
                e.tensor_tensor(st[:, 6:7], st[:, 3:4], st[:, 3:4], ALU.mult)
                e.drain()
                return e.tensor_sub(st[:, 4:5], st[:, 4:5], st[:, 6:7])
            p.body("vector", ve_fn)

            def sqrt_fn(e):
                return e.activation(st[:, 5:6], st[:, 4:5], AF.Sqrt,
                                    bias=p.constf[:, 0:1])
            p.body("scalar", sqrt_fn)

            def norm_fn(e, par=par):
                e.reciprocal(st[:, 6:7], st[:, 5:6])
                e.drain()
                return e.tensor_scalar(xnb[par][:], xt[par][:], st[:, 3:4],
                                       st[:, 6:7], ALU.subtract, ALU.mult)
            p.body("vector", norm_fn)

            def tr_fn(e, par=par):
                ins = None
                for c in range(NC):
                    ins = e.transpose(ptT[:, c * 128:(c + 1) * 128],
                                      xnb[par][:, c * 128:(c + 1) * 128],
                                      identity[:])
                return ins
            p.body("tensor", tr_fn)

            def ev_fn(e, par=par):
                return e.activation(xnT[par][:], ptT[:], AF.Copy)
            p.body("scalar", ev_fn)

            def stx_fn(e, i=i, par=par):
                p.inc("st", e.dma_start(
                    out=xnfm_d[:, :, i * 128:(i + 1) * 128],
                    in_=xnT[par][:].rearrange("q (c t) -> q c t", c=NC)), 16)
                if xnrow_d is not None:
                    p.inc("st", e.dma_start(
                        out=xnrow_d[i * 128:(i + 1) * 128, :],
                        in_=xnb[par][:]), 16)
            p.body("gpsimd", stx_fn)


def emit_proj_fm(p, wT_sb, xnfm_sb, dest_d, b_row, ones, ncol, scale=None):
    nc = p.nc
    nd = ncol // 128
    with ExitStack() as ctx:
        e_ = ctx.enter_context
        ot = [e_(nc.sbuf_tensor(_u(f"pj_o{i}"), [128, 512], BF16)) for i in range(2)]
        psu = [e_(nc.psum_tensor(_u(f"pj_p{i}"), [128, 512], F32)) for i in range(2)]
        k = 0
        for dc in range(nd):
            for tp in range(S // 512):
                par = k % 2

                def pe_fn(e, dc=dc, tp=tp, par=par):
                    for c in range(NC):
                        ins = e.matmul(
                            psu[par][:, 0:512],
                            wT_sb[:, c * ncol + dc * 128:c * ncol + (dc + 1) * 128],
                            xnfm_sb[:, c * S + tp * 512:c * S + (tp + 1) * 512],
                            start=(c == 0), stop=(b_row is None and c == NC - 1))
                    if b_row is not None:
                        ins = e.matmul(psu[par][:, 0:512],
                                       b_row[:, dc * 128:(dc + 1) * 128],
                                       ones[:, 0:512], start=False, stop=True)
                    return ins
                p.body("tensor", pe_fn)

                def ac_fn(e, par=par):
                    return e.activation(ot[par][:], psu[par][:, 0:512], AF.Copy,
                                        scale=(scale if scale else 1.0))
                p.body("scalar", ac_fn)

                def st_fn(e, dc=dc, tp=tp, par=par):
                    p.inc("st", e.dma_start(
                        out=dest_d[:, dc, tp * 512:(tp + 1) * 512],
                        in_=ot[par][:]), 16)
                p.body("gpsimd", st_fn)
                k += 1


def emit_proj_tm(p, wT_sb, xnfm_sb, dest_d, b_row, ones, ncol):
    nc = p.nc
    with ExitStack() as ctx:
        e_ = ctx.enter_context
        ot = [e_(nc.sbuf_tensor(_u(f"pv_o{i}"), [128, ncol], BF16)) for i in range(2)]
        psu = [e_(nc.psum_tensor(_u(f"pv_p{i}"), [128, 512], F32)) for i in range(2)]
        for i in range(NT):
            par = i % 2

            def pe_fn(e, i=i, par=par):
                for half in range(ncol // 512):
                    for c in range(NC):
                        ins = e.matmul(
                            psu[half][:, 0:512],
                            xnfm_sb[:, c * S + i * 128:c * S + (i + 1) * 128],
                            wT_sb[:, c * ncol + half * 512:c * ncol + (half + 1) * 512],
                            start=(c == 0), stop=(b_row is None and c == NC - 1))
                    if b_row is not None:
                        ins = e.matmul(psu[half][:, 0:512], ones[0:1, 0:128],
                                       b_row[:, half * 512:(half + 1) * 512],
                                       start=False, stop=True)
                return ins
            p.body("tensor", pe_fn)

            def ac_fn(e, par=par):
                for half in range(ncol // 512):
                    e.drain()
                    ins = e.activation(ot[par][:, half * 512:(half + 1) * 512],
                                       psu[half][:, 0:512], AF.Copy)
                return ins
            p.body("scalar", ac_fn)

            def st_fn(e, i=i, par=par):
                p.inc("st", e.dma_start(out=dest_d[i * 128:(i + 1) * 128, :],
                                        in_=ot[par][:]), 16)
            p.body("gpsimd", st_fn)


def emit_quant(p, w_f32, w_q_bf, rows, cols, onesf, onecolf, tag):
    nc = p.nc
    ntile = rows // 128
    with ExitStack() as ctx:
        e_ = ctx.enter_context
        wt = [e_(nc.sbuf_tensor(_u(f"qz_w{tag}{i}"), [128, cols], F32))
              for i in range(2)]
        wq = e_(nc.sbuf_tensor(_u(f"qz_q{tag}"), [128, cols], BF16))
        q1 = e_(nc.sbuf_tensor(_u(f"qz_1{tag}"), [128, cols], F32))
        q2 = e_(nc.sbuf_tensor(_u(f"qz_2{tag}"), [128, cols], F32))
        absn = e_(nc.sbuf_tensor(_u(f"qz_a{tag}"), [128, ntile], F32))
        gcol = e_(nc.sbuf_tensor(_u(f"qz_g{tag}"), [128, 4], F32))
        gps = e_(nc.psum_tensor(_u(f"qz_p{tag}"), [128, 2], F32))

        for i in range(ntile):
            def ld_fn(e, i=i):
                p.inc("ld", e.dma_start(out=wt[i % 2][:],
                                        in_=w_f32[i * 128:(i + 1) * 128, :]), 16)
            p.body("sync", ld_fn)

            def ve_fn(e, i=i):
                return e.reduce_sum(absn[:, i:i + 1], wt[i % 2][:], axis=AX.X,
                                    apply_absolute_value=True)
            p.body("vector", ve_fn)

        def ve_tot(e):
            return e.reduce_sum(gcol[:, 0:1], absn[:, 0:ntile], axis=AX.X)
        p.body("vector", ve_tot)

        def pe_tot(e):
            return e.matmul(gps[0:1, 0:1], gcol[:, 0:1], onecolf[:],
                            start=True, stop=True)
        p.body("tensor", pe_tot)

        def ve_g(e):
            e.tensor_scalar(gcol[0:1, 1:2], gps[0:1, 0:1], 1.0 / (rows * cols),
                            1e-5, ALU.mult, ALU.add)
            e.drain()
            return e.reciprocal(gcol[0:1, 2:3], gcol[0:1, 1:2])
        p.body("vector", ve_g)

        def pe_b(e):
            return e.matmul(gps[:, 1:2], onesf[:], gcol[0:1, 2:3],
                            start=True, stop=True)
        p.body("tensor", pe_b)

        def ve_bc(e):
            return e.tensor_copy(gcol[:, 3:4], gps[:, 1:2])
        p.body("vector", ve_bc)

        for i in range(ntile):
            def ld_fn(e, i=i):
                p.inc("ld", e.dma_start(out=wt[i % 2][:],
                                        in_=w_f32[i * 128:(i + 1) * 128, :]), 16)
            p.body("sync", ld_fn)

            def ve_fn(e, i=i):
                e.tensor_scalar(q1[:], wt[i % 2][:], gcol[:, 3:4], None,
                                ALU.mult)
                e.drain()
                e.tensor_scalar(wq[:], q1[:], 0.5, None, ALU.is_gt)
                e.tensor_scalar(q2[:], q1[:], -0.5, None, ALU.is_lt)
                e.drain()
                return e.tensor_sub(wq[:], wq[:], q2[:])
            p.body("vector", ve_fn)

            def st_fn(e, i=i):
                p.inc("st", e.dma_start(out=w_q_bf[i * 128:(i + 1) * 128, :],
                                        in_=wq[:]), 16)
            p.body("gpsimd", st_fn)



def build_l0(p, io):
    nc = p.nc
    x = io["x"]
    r1 = io["r1"]
    xnfm_d = io["xnfm"]

    wq_bf = nc.dram_tensor("wq_bf", [D, D], BF16, kind="Internal").ap()
    wk_bf = nc.dram_tensor("wk_bf", [D, D], BF16, kind="Internal").ap()
    wv_bf = nc.dram_tensor("wv_bf", [D, D], BF16, kind="Internal").ap()
    wo_bf = nc.dram_tensor("wo_bf", [D, D], BF16, kind="Internal").ap()
    w1q_bf = nc.dram_tensor("w1q_bf", [FF, D], BF16, kind="Internal").ap()
    w2q_bf = nc.dram_tensor("w2q_bf", [D, FF], BF16, kind="Internal").ap()
    qfm_d = nc.dram_tensor("qfm_d", [128, NC, S], BF16, kind="Internal").ap()
    kfm_d = nc.dram_tensor("kfm_d", [128, NC, S], BF16, kind="Internal").ap()
    vtm_d = nc.dram_tensor("vtm_d", [S, D], BF16, kind="Internal").ap()
    racc_d = nc.dram_tensor("racc_d", [S, D], F32, kind="Internal").ap()

    ones, onesf, onecolf, identity = emit_consts(p)

    def cast_fn(e):
        for dst, src in ((wq_bf, io["Wq"]), (wk_bf, io["Wk"]),
                         (wv_bf, io["Wv"]), (wo_bf, io["Wo"])):
            p.inc("gp", e.dma_start(out=dst, in_=src), 16)
    p.body("gpsimd", cast_fn)

    brow = p.sb("brow", [1, 4 * D + FF + D], BF16)
    bq_r = brow[:, 0:D]
    bk_r = brow[:, D:2 * D]
    bv_r = brow[:, 2 * D:3 * D]
    bo_r = brow[:, 3 * D:4 * D]
    b1_r = brow[:, 4 * D:4 * D + FF]
    b2_r = brow[:, 4 * D + FF:4 * D + FF + D]

    def bias_fn(e):
        for dst, src in ((bq_r, io["bq"]), (bk_r, io["bk"]), (bv_r, io["bv"]),
                         (bo_r, io["bo"]), (b1_r, io["b1"]), (b2_r, io["b2"])):
            p.inc("gp", e.dma_start(out=dst, in_=src), 16)
    p.body("gpsimd", bias_fn)

    emit_ln_transpose(p, x, xnfm_d, identity)
    emit_quant(p, io["W1"], w1q_bf, FF, D, onesf, onecolf, "a")
    emit_quant(p, io["W2"], w2q_bf, D, FF, onesf, onecolf, "b")

    with ExitStack() as ctx:
        e_ = ctx.enter_context
        xnfm = e_(nc.sbuf_tensor(_u("xnfm_sb"), [128, NC * S], BF16))
        wT = e_(nc.sbuf_tensor(_u("wT"), [128, NC * D], BF16))

        def ld_xn(e):
            p.inc("ld", e.dma_start(out=xnfm[:],
                                    in_=xnfm_d.rearrange("q c t -> q (c t)")), 16)
        p.body("sync", ld_xn)

        load_wT(p, wT, wk_bf, D, NC)
        emit_proj_fm(p, wT, xnfm, kfm_d, bk_r, ones, D)
        load_wT(p, wT, wq_bf, D, NC)
        emit_proj_fm(p, wT, xnfm, qfm_d, bq_r, ones, D)
        load_wT(p, wT, wv_bf, D, NC)
        emit_proj_tm(p, wT, xnfm, vtm_d, bv_r, ones, D)

    with ExitStack() as ctx:
        e_ = ctx.enter_context
        kfm = e_(nc.sbuf_tensor(_u("kfm"), [128, NC * S], BF16))
        vtm = e_(nc.sbuf_tensor(_u("vtm"), [128, NT * D], BF16))
        woT = e_(nc.sbuf_tensor(_u("woT"), [128, NC * D], BF16))
        qt = [e_(nc.sbuf_tensor(_u(f"qt{i}"), [128, D], BF16)) for i in range(2)]
        s_sb = e_(nc.sbuf_tensor(_u("s_sb"), [128, S], BF16))
        p_sb = e_(nc.sbuf_tensor(_u("p_sb"), [128, S], BF16))
        pt_sb = e_(nc.sbuf_tensor(_u("pt"), [128, 512], BF16))
        ao_sb = e_(nc.sbuf_tensor(_u("ao_sb"), [128, D], BF16))
        o_sb = [e_(nc.sbuf_tensor(_u(f"o_sb{i}"), [128, D], F32)) for i in range(2)]
        xres = [e_(nc.sbuf_tensor(_u(f"xres{i}"), [128, D], F32)) for i in range(2)]
        cstat = e_(nc.sbuf_tensor(_u("cstat"), [128, 4], F32))
        ps_s = e_(nc.psum_tensor(_u("ps_s"), [128, 512], F32))
        ps_pt = e_(nc.psum_tensor(_u("ps_pt"), [128, 512], BF16))
        ps_ao = e_(nc.psum_tensor(_u("ps_ao"), [128, 1024], F32))
        ps_o = e_(nc.psum_tensor(_u("ps_o"), [128, 1024], F32))

        def ld0(e):
            p.inc("ld", e.dma_start(out=kfm[:],
                                    in_=kfm_d.rearrange("q c t -> q (c t)")), 16)
            p.inc("ld", e.dma_start(
                out=vtm[:].rearrange("q (n d) -> q n d", n=NT),
                in_=vtm_d.rearrange("(n q) d -> q n d", q=128)), 16)
        p.body("sync", ld0)
        load_wT(p, woT, wo_bf, D, NC)

        for i in range(NT):
            par = i % 2

            def ld_fn(e, i=i, par=par):
                p.inc("ld", e.dma_start(
                    out=qt[par][:].rearrange("q (c t) -> q c t", c=NC),
                    in_=qfm_d[:, :, i * 128:(i + 1) * 128]), 16)
                p.inc("ld", e.dma_start(out=xres[par][:],
                                        in_=x[i * 128:(i + 1) * 128, :]), 16)
            p.body("sync", ld_fn)

            for kp in range(NC):
                def pe_s(e, par=par, kp=kp):
                    ins = None
                    for c in range(NC):
                        ins = e.matmul(
                            ps_s[:, 0:512],
                            qt[par][:, c * 128:(c + 1) * 128],
                            kfm[:, c * S + kp * 512:c * S + (kp + 1) * 512],
                            start=(c == 0), stop=(c == NC - 1))
                    return ins
                p.body("tensor", pe_s)

                def ac_s(e, kp=kp):
                    return e.activation(s_sb[:, kp * 512:(kp + 1) * 512],
                                        ps_s[:, 0:512], AF.Copy,
                                        scale=p.constf[:, 1:2])
                p.body("scalar", ac_s)

            def ve_sm(e):
                return e.reduce_max(cstat[:, 0:1], s_sb[:], axis=AX.X,
                                    negate=True)
            p.body("vector", ve_sm)

            def ac_sm(e):
                return e.activation(p_sb[:], s_sb[:], AF.Exp,
                                    bias=cstat[:, 0:1], accum_out=cstat[:, 1:2])
            p.body("scalar", ac_sm)

            def ve_sm2(e):
                e.reciprocal(cstat[:, 2:3], cstat[:, 1:2])
                e.drain()
                return e.tensor_scalar(s_sb[:], p_sb[:], cstat[:, 2:3], None,
                                       ALU.mult)
            p.body("vector", ve_sm2)

            for kg in range(NT // 4):
                def pe_pt(e, kg=kg):
                    ins = None
                    for q in range(4):
                        kc = kg * 4 + q
                        ins = e.transpose(ps_pt[:, q * 128:(q + 1) * 128],
                                          s_sb[:, kc * 128:(kc + 1) * 128],
                                          identity[:])
                    return ins
                p.body("tensor", pe_pt)

                def ve_pt(e):
                    return e.tensor_copy(pt_sb[:], ps_pt[:, 0:512])
                p.body("vector", ve_pt)

                def pe_pv(e, kg=kg):
                    ins = None
                    for q in range(4):
                        kc = kg * 4 + q
                        for dc in range(NC):
                            ins = e.matmul(
                                ps_ao[:, dc * 128:(dc + 1) * 128],
                                vtm[:, kc * D + dc * 128:kc * D + (dc + 1) * 128],
                                pt_sb[:, q * 128:(q + 1) * 128],
                                start=(kc == 0 and dc % 4 == 0),
                                stop=(kc == NT - 1))
                    return ins
                p.body("tensor", pe_pv)

            def ac_ao(e):
                return e.activation(ao_sb[:], ps_ao[:], AF.Copy)
            p.body("scalar", ac_ao)

            def pe_o(e):
                ins = None
                for half in range(2):
                    for c in range(NC):
                        e.matmul(ps_o[:, half * 512:(half + 1) * 512],
                                 ao_sb[:, c * 128:(c + 1) * 128],
                                 woT[:, c * D + half * 512:c * D + (half + 1) * 512],
                                 start=(c == 0), stop=False)
                    ins = e.matmul(ps_o[:, half * 512:(half + 1) * 512],
                                   ones[0:1, 0:128],
                                   bo_r[:, half * 512:(half + 1) * 512],
                                   start=False, stop=True)
                return ins
            p.body("tensor", pe_o)

            def ve_r(e, par=par):
                return e.tensor_add(o_sb[par][:], ps_o[:], xres[par][:])
            p.body("vector", ve_r)

            def st_r(e, i=i, par=par):
                p.inc("st", e.dma_start(out=racc_d[i * 128:(i + 1) * 128, :],
                                        in_=o_sb[par][:]), 16)
            p.body("gpsimd", st_r)

    with ExitStack() as ctx:
        e_ = ctx.enter_context
        w1T = e_(nc.sbuf_tensor(_u("w1T"), [128, NC * FF], BF16))
        w2T = e_(nc.sbuf_tensor(_u("w2T"), [128, NF * D], BF16))
        xnt = [e_(nc.sbuf_tensor(_u(f"fx{i}"), [128, D], BF16)) for i in range(2)]
        hfm = e_(nc.sbuf_tensor(_u("hfm"), [128, FF], BF16))
        rres = [e_(nc.sbuf_tensor(_u(f"fr{i}"), [128, D], F32)) for i in range(2)]
        fout = [e_(nc.sbuf_tensor(_u(f"fo{i}"), [128, D], F32)) for i in range(2)]
        ps_h = [e_(nc.psum_tensor(_u(f"ps_h{i}"), [128, 128], F32)) for i in range(2)]
        ps_f = e_(nc.psum_tensor(_u("ps_f"), [128, 1024], F32))

        load_wT(p, w1T, w1q_bf, FF, NC)
        load_wT(p, w2T, w2q_bf, D, NF)

        for i in range(NT):
            par = i % 2

            def ld_fn(e, i=i, par=par):
                p.inc("ld", e.dma_start(
                    out=xnt[par][:].rearrange("q (c t) -> q c t", c=NC),
                    in_=xnfm_d[:, :, i * 128:(i + 1) * 128]), 16)
                p.inc("ld", e.dma_start(out=rres[par][:],
                                        in_=racc_d[i * 128:(i + 1) * 128, :]), 16)
            p.body("sync", ld_fn)

            for fc in range(NF):
                fp = fc % 2

                def pe_h(e, par=par, fc=fc, fp=fp):
                    for c in range(NC):
                        e.matmul(ps_h[fp][:, 0:128],
                                 w1T[:, c * FF + fc * 128:c * FF + (fc + 1) * 128],
                                 xnt[par][:, c * 128:(c + 1) * 128],
                                 start=(c == 0), stop=False)
                    return e.matmul(ps_h[fp][:, 0:128],
                                    b1_r[:, fc * 128:(fc + 1) * 128],
                                    ones[:, 0:128], start=False, stop=True)
                p.body("tensor", pe_h)

                def ac_h(e, fc=fc, fp=fp):
                    return e.activation(hfm[:, fc * 128:(fc + 1) * 128],
                                        ps_h[fp][:, 0:128], AF.Relu)
                p.body("scalar", ac_h)

            def pe_f(e):
                ins = None
                for half in range(2):
                    for fc in range(NF):
                        e.matmul(ps_f[:, half * 512:(half + 1) * 512],
                                 hfm[:, fc * 128:(fc + 1) * 128],
                                 w2T[:, fc * D + half * 512:fc * D + (half + 1) * 512],
                                 start=(fc == 0), stop=False)
                    ins = e.matmul(ps_f[:, half * 512:(half + 1) * 512],
                                   ones[0:1, 0:128],
                                   b2_r[:, half * 512:(half + 1) * 512],
                                   start=False, stop=True)
                return ins
            p.body("tensor", pe_f)

            def ve_f(e, par=par):
                return e.tensor_add(fout[par][:], ps_f[:], rres[par][:])
            p.body("vector", ve_f)

            def st_f(e, i=i, par=par):
                p.inc("st", e.dma_start(out=r1[i * 128:(i + 1) * 128, :],
                                        in_=fout[par][:]), 16)
            p.body("gpsimd", st_f)




NH = 16
HD = 64


def build_l1a(p, io):
    nc = p.nc
    r1 = io["r1"]
    r2 = io["r2"]
    xn1fm_d = io["xn1fm"]
    xn1row_d = io["xn1row"]

    wq_bf = nc.dram_tensor("l1wq_bf", [D, D], BF16, kind="Internal").ap()
    wk_bf = nc.dram_tensor("l1wk_bf", [D, D], BF16, kind="Internal").ap()
    wv_bf = nc.dram_tensor("l1wv_bf", [D, D], BF16, kind="Internal").ap()
    wo_bf = nc.dram_tensor("l1wo_bf", [D, D], BF16, kind="Internal").ap()
    qfm_d = nc.dram_tensor("l1qfm_d", [128, NC, S], BF16, kind="Internal").ap()
    kfm_d = nc.dram_tensor("l1kfm_d", [128, NC, S], BF16, kind="Internal").ap()
    vtm_d = nc.dram_tensor("l1vtm_d", [S, D], BF16, kind="Internal").ap()

    ones, onesf, onecolf, identity = emit_consts(p)

    def cast_fn(e):
        for dst, src in ((wq_bf, io["Wq"]), (wk_bf, io["Wk"]),
                         (wv_bf, io["Wv"]), (wo_bf, io["Wo"])):
            p.inc("gp", e.dma_start(out=dst, in_=src), 16)
    p.body("gpsimd", cast_fn)

    mtriT = p.sb("mtriT", [128, 128], BF16)

    def mask_fn(e):
        e.memset(mtriT[:], 0.0)
        return e.affine_select(out=mtriT[:], in_=mtriT[:],
                               compare_op=ALU.is_ge, fill=-1e30, base=0,
                               pattern=[[1, 128]], channel_multiplier=-1)
    p.body("gpsimd", mask_fn)

    emit_ln_transpose(p, r1, xn1fm_d, identity, xnrow_d=xn1row_d)

    with ExitStack() as ctx:
        e_ = ctx.enter_context
        xnfm = e_(nc.sbuf_tensor(_u("l1xn_sb"), [128, NC * S], BF16))
        wT = e_(nc.sbuf_tensor(_u("l1wT"), [128, NC * D], BF16))

        def ld_xn(e):
            p.inc("ld", e.dma_start(out=xnfm[:],
                                    in_=xn1fm_d.rearrange("q c t -> q (c t)")), 16)
        p.body("sync", ld_xn)

        load_wT(p, wT, wk_bf, D, NC)
        emit_proj_fm(p, wT, xnfm, kfm_d, None, ones, D)
        load_wT(p, wT, wq_bf, D, NC)
        emit_proj_fm(p, wT, xnfm, qfm_d, None, ones, D)
        load_wT(p, wT, wv_bf, D, NC)
        emit_proj_tm(p, wT, xnfm, vtm_d, None, ones, D)

    with ExitStack() as ctx:
        e_ = ctx.enter_context
        kfm = e_(nc.sbuf_tensor(_u("l1kfm"), [128, NC * S], BF16))
        vpl = e_(nc.sbuf_tensor(_u("l1vpl"), [128, NT * D], BF16))
        onecb = e_(nc.sbuf_tensor(_u("l1oneb"), [128, 1], BF16))
        woT = e_(nc.sbuf_tensor(_u("l1woT"), [128, NC * D], BF16))
        qt = [e_(nc.sbuf_tensor(_u("l1qt"), [128, D], BF16)) for i in range(2)]
        ptt = e_(nc.sbuf_tensor(_u("l1ptt"), [128, NH * 128], BF16))
        dmask = e_(nc.sbuf_tensor(_u("l1dmask"), [128, NH * 128], F32))
        sums = e_(nc.sbuf_tensor(_u("l1sums"), [128, NH * 2], F32))
        ao_sb = e_(nc.sbuf_tensor(_u("l1ao"), [128, D], BF16))
        aofm = e_(nc.sbuf_tensor(_u("l1aofm"), [128, D], BF16))
        o_sb = [e_(nc.sbuf_tensor(_u("l1o"), [128, D], F32)) for i in range(2)]
        rres = [e_(nc.sbuf_tensor(_u("l1rr"), [128, D], F32)) for i in range(2)]
        ps_st = e_(nc.psum_tensor(_u("l1ps_st"), [128, NH * 128], F32))
        ps_ao = e_(nc.psum_tensor(_u("l1ps_ao"), [128, NH * 64], F32))
        ps_sum = e_(nc.psum_tensor(_u("l1ps_sum"), [128, 16], F32))
        ps_tr = e_(nc.psum_tensor(_u("l1ps_tr"), [128, D], BF16))

        def ld0(e):
            p.inc("ld", e.dma_start(out=kfm[:],
                                    in_=kfm_d.rearrange("q c t -> q (c t)")), 16)
            p.inc("ld", e.dma_start(
                out=vpl[:].rearrange("q (n d) -> q n d", n=NT),
                in_=vtm_d.rearrange("(n q) d -> q n d", q=128)), 16)
        p.body("sync", ld0)

        def ones_fn(e):
            return e.memset(onecb[:], 1.0)
        p.body("gpsimd", ones_fn)

        load_wT(p, woT, wo_bf, D, NC)

        for i in range(NT):
            par = i % 2

            def ld_fn(e, i=i, par=par):
                p.inc("ld", e.dma_start(
                    out=qt[par][:].rearrange("q (c t) -> q c t", c=NC),
                    in_=qfm_d[:, :, i * 128:(i + 1) * 128]), 16)
                p.inc("ld", e.dma_start(out=rres[par][:],
                                        in_=r1[i * 128:(i + 1) * 128, :]), 16)
            p.body("sync", ld_fn)

            def slot(h):
                return (h % 2) * 8 + h // 2

            for kc in range(i + 1):
                def pe_st(e, par=par, kc=kc):
                    ins = None
                    for h in sorted(range(NH), key=lambda h: h % 2):
                        c, b = h // 2, 64 * (h % 2)
                        ins = e.matmul(
                            ps_st[:, slot(h) * 128:(slot(h) + 1) * 128],
                            kfm[b:b + HD, c * S + kc * 128:c * S + (kc + 1) * 128],
                            qt[par][b:b + HD, c * 128:(c + 1) * 128],
                            start=True, stop=True)
                    return ins
                p.body("tensor", pe_st)

                if kc == i:
                    def m_fn(e, kc=kc):
                        ins = None
                        for h in range(NH):
                            ins = e.tensor_tensor(
                                dmask[:, h * 128:(h + 1) * 128],
                                ps_st[:, slot(h) * 128:(slot(h) + 1) * 128],
                                mtriT[:], ALU.add)
                        return ins
                    p.body("vector", m_fn)

                def ac_exp(e, kc=kc, diag=(kc == i)):
                    ins = None
                    for h in range(NH):
                        src = (dmask[:, h * 128:(h + 1) * 128] if diag else
                               ps_st[:, slot(h) * 128:(slot(h) + 1) * 128])
                        ins = e.activation(ptt[:, h * 128:(h + 1) * 128],
                                           src, AF.Exp, scale=p.constf[:, 2:3])
                    return ins
                p.body("scalar", ac_exp)

                def pe_pv(e, kc=kc, first=(kc == 0), last=(kc == i)):
                    ins = None
                    for h in range(NH):
                        e.matmul(
                            ps_ao[:, h * 64:(h + 1) * 64],
                            ptt[:, h * 128:(h + 1) * 128],
                            vpl[:, kc * D + h * 64:kc * D + (h + 1) * 64],
                            start=(first and h % 8 == 0), stop=last)
                        ins = e.matmul(
                            ps_sum[:, h:h + 1],
                            ptt[:, h * 128:(h + 1) * 128], onecb[:],
                            start=(first and h == 0), stop=last)
                    return ins
                p.body("tensor", pe_pv)

            def ve_ao(e):
                e.tensor_copy(sums[:, 0:NH], ps_sum[:])
                e.drain()
                e.reciprocal(sums[:, NH:2 * NH], sums[:, 0:NH])
                e.drain()
                ins = None
                for h in range(NH):
                    ins = e.tensor_scalar(ao_sb[:, h * HD:(h + 1) * HD],
                                          ps_ao[:, h * 64:(h + 1) * 64],
                                          sums[:, NH + h:NH + h + 1], None,
                                          ALU.mult)
                return ins
            p.body("vector", ve_ao)

            if i == 1 and "dbg_ao" in io:
                def dbg_fn(e):
                    p.inc("st", e.dma_start(out=io["dbg_ao"], in_=ao_sb[:]), 16)
                    p.inc("st", e.dma_start(out=io["dbg_sums"], in_=sums[:]), 16)
                p.body("gpsimd", dbg_fn)

            def pe_tr(e):
                ins = None
                for c in range(NC):
                    ins = e.transpose(ps_tr[:, c * 128:(c + 1) * 128],
                                      ao_sb[:, c * 128:(c + 1) * 128],
                                      identity[:])
                return ins
            p.body("tensor", pe_tr)

            def ac_tr(e):
                return e.activation(aofm[:], ps_tr[:], AF.Copy)
            p.body("scalar", ac_tr)

            def pe_o(e):
                ins = None
                for half in range(2):
                    for c in range(NC):
                        ins = e.matmul(
                            ps_st[:, half * 512:(half + 1) * 512],
                            aofm[:, c * 128:(c + 1) * 128],
                            woT[:, c * D + half * 512:c * D + (half + 1) * 512],
                            start=(c == 0), stop=(c == NC - 1))
                return ins
            p.body("tensor", pe_o)

            def ve_r(e, par=par):
                return e.tensor_add(o_sb[par][:], ps_st[:, 0:1024], rres[par][:])
            p.body("vector", ve_r)

            def st_r(e, i=i, par=par):
                p.inc("st", e.dma_start(out=r2[i * 128:(i + 1) * 128, :],
                                        in_=o_sb[par][:]), 16)
            p.body("gpsimd", st_r)


def build_l1a_program():
    p = Prog()
    nc = p.nc
    io = {}
    io["r1"] = nc.dram_tensor("r1", [S, D], F32, kind="ExternalInput").ap()
    for nm in ("Wq", "Wk", "Wv", "Wo"):
        io[nm] = nc.dram_tensor(nm, [D, D], F32, kind="ExternalInput").ap()
    io["r2"] = nc.dram_tensor("r2", [S, D], F32, kind="ExternalOutput").ap()
    io["xn1fm"] = nc.dram_tensor("xn1fm", [128, NC, S], BF16,
                                 kind="ExternalOutput").ap()
    io["xn1row"] = nc.dram_tensor("xn1row", [S, D], BF16,
                                  kind="ExternalOutput").ap()
    io["dbg_ao"] = nc.dram_tensor("dbg_ao", [128, D], BF16,
                                  kind="ExternalOutput").ap()
    io["dbg_sums"] = nc.dram_tensor("dbg_sums", [128, 32], F32,
                                    kind="ExternalOutput").ap()
    with p.ctx:
        p.blk = p.ctx.enter_context(nc.Block())
        build_l1a(p, io)
        p.finish()
    return p


def run_l1a(r1_full, inputs):
    p = build_l1a_program()
    in_maps = []
    for b in range(8):
        m = {"r1": np.ascontiguousarray(r1_full[b], dtype=np.float32)}
        for nm, key in (("Wq", "L1_attn_Wq"), ("Wk", "L1_attn_Wk"),
                        ("Wv", "L1_attn_Wv"), ("Wo", "L1_attn_Wo")):
            m[nm] = np.ascontiguousarray(inputs[key], dtype=np.float32)
        in_maps.append(m)
    res = run_bass_kernel_spmd(p.nc, in_maps, core_ids=list(range(8)))
    r2 = np.stack([r["r2"] for r in res.results], axis=0)
    xn1fm = np.stack([r["xn1fm"] for r in res.results], axis=0)
    xn1row = np.stack([r["xn1row"] for r in res.results], axis=0)
    return r2, xn1fm, xn1row





NE = 64
TK = 127
I16 = mybir.dt.int16


def moe_dram(p, names):
    nc = p.nc
    shapes = {
        "slot_up": ([S, 2], F32), "slot_dn": ([S, 2], F32),
        "park_up": ([128, NT * 6], F32), "park_dn": ([128, NT * 6], F32),
        "btl_up": ([TK, 64 * 128], BF16), "atl_up": ([TK, 64 * 128], BF16),
        "btl_dn": ([TK, 64 * 32], BF16), "atl_dn": ([TK, 64 * 32], BF16),
        "packed_up": ([TK * 128, 4], F32), "packed_dn": ([TK * 128, 4], F32),
        "xs_up": ([TK * 128, D], BF16), "xs_dn": ([TK * 128, FF], BF16),
        "ypl_up": ([2 * S, FF], BF16), "ypl_dn": ([2 * S, D], BF16),
        "hrow_d": ([S, FF], BF16),
    }
    out = {}
    for nm, kind in names.items():
        shp, dt = shapes[nm]
        out[nm] = p.nc.dram_tensor(nm, shp, dt, kind=kind).ap()
    return out


def emit_routing(p, io, consts, stage, slot_d, park_d, wexp):
    nc = p.nc
    ones, onesf, onecolf, identity = consts
    iota64 = p.sb("iota64", [128, NE], F32)
    iotaT = p.sb("iotaT", [64, TK], F32)
    u128 = p.sb("u128", [128, 128], F32)
    u64 = p.sb("u64", [64, NE], F32)
    onec64 = p.sb("onec64", [64, 1], F32)
    park = p.sb("park", [128, NT * 6], F32)
    lg_sb = p.sb("lg_sb", [128, NE], F32)
    m8 = p.sb("m8", [128, 8], F32)
    mi8 = p.sb("mi8", [128, 8], U32)
    efp = p.sb("efp", [128, 8], F32)
    mc = p.sb("mc", [128, 3 * NE], F32)
    offrow = p.sb("offrow", [1, NE], F32)
    rtmp = p.sb("rtmp", [128, NE], F32)
    grow = p.sb("grow", [64, 8], F32)
    thr64 = p.sb("thr64", [64, NE], F32)
    thrt = p.sb("thrt", [64, NE], F32)
    brow_t = p.sb("brow_t", [1, NE], F32)
    bb128 = p.sb("bb128", [128, NE], F32)
    tidw = p.sb("tidw", [128, 2 * TK], F32)
    wgi = p.sb("wgi", [TK, 1], I32)
    slotc = p.sb("slotc", [128, 2], F32)
    ps_r = p.ps("ps_r", [128, 512], F32)
    ps_c = p.ps("ps_c", [128, 512], F32)
    ps_c2 = p.ps("ps_c2", [128, 512], F32)

    def cst_fn(e):
        e.iota(iota64[:], pattern=[[1, NE]], base=0, channel_multiplier=0,
               allow_small_or_imprecise_dtypes=True)
        e.iota(iotaT[:], pattern=[[1, TK]], base=0, channel_multiplier=0,
               allow_small_or_imprecise_dtypes=True)
        e.memset(onec64[:], 1.0)
        e.memset(u128[:], 1.0)
        e.memset(u64[:], 1.0)
        e.iota(thr64[:], pattern=[[128, NE]], base=0, channel_multiplier=0,
               allow_small_or_imprecise_dtypes=True)
        e.drain()
        e.affine_select(out=u128[:], in_=u128[:], compare_op=ALU.is_gt,
                        fill=0.0, base=0, pattern=[[1, 128]],
                        channel_multiplier=-1)
        return e.affine_select(out=u64[:], in_=u64[:], compare_op=ALU.is_gt,
                               fill=0.0, base=0, pattern=[[1, NE]],
                               channel_multiplier=-1)
    p.body("gpsimd", cst_fn)

    def zoff_fn(e):
        return e.memset(offrow[:], 0.0)
    p.body("vector", zoff_fn)

    mk_logits = io["mk_logits"]
    for i in range(NT):
        mk_logits(i, ps_r, lg_sb)

        def top2_fn(e, i=i):
            e.max(out=m8[:], in_=lg_sb[:])
            e.drain()
            e.max_index(out=mi8[:], in_max=m8[:], in_values=lg_sb[:])
            e.drain()
            e.tensor_copy(efp[:, 0:2], mi8[:, 0:2])
            e.tensor_sub(efp[:, 2:3], m8[:, 0:1], m8[:, 1:2])
            e.drain()
            return e.tensor_scalar(efp[:, 3:4], efp[:, 2:3], -1.0, None,
                                   ALU.mult)
        p.body("vector", top2_fn)

        def sig_fn(e):
            return e.activation(efp[:, 4:6], efp[:, 2:4], AF.Sigmoid)
        p.body("scalar", sig_fn)

        def mc_fn(e, i=i):
            e.tensor_scalar(mc[:, 0:NE], iota64[:], efp[:, 0:1], None,
                            ALU.is_equal)
            e.tensor_scalar(mc[:, NE:2 * NE], iota64[:], efp[:, 1:2], None,
                            ALU.is_equal)
            e.drain()
            e.tensor_add(mc[:, 2 * NE:3 * NE], mc[:, 0:NE], mc[:, NE:2 * NE])
            e.drain()
            e.tensor_copy(park[:, i * 6:i * 6 + 2], efp[:, 0:2])
            return e.tensor_copy(park[:, i * 6 + 4:i * 6 + 6], efp[:, 4:6])
        p.body("vector", mc_fn)

        def rank_fn(e, i=i):
            e.matmul(ps_r[:, 64:128], u128[:], mc[:, 2 * NE:3 * NE],
                     start=True, stop=False)
            e.matmul(ps_r[:, 64:128], onesf[:], offrow[:], start=False,
                     stop=True)
            e.matmul(ps_c[0:1, 0:64], onecolf[:], mc[:, 2 * NE:3 * NE],
                     start=(i == 0), stop=(i == NT - 1))
            return e.matmul(ps_c2[0:64, 0:1], mc[:, 2 * NE:3 * NE],
                            onecolf[:], start=(i == 0), stop=(i == NT - 1))
        p.body("tensor", rank_fn)

        def ext_fn(e, i=i):
            e.tensor_copy(offrow[:], ps_c[0:1, 0:64])
            e.tensor_mul(rtmp[:], ps_r[:, 64:128], mc[:, 0:NE])
            e.drain()
            e.reduce_sum(park[:, i * 6 + 2:i * 6 + 3], rtmp[:], axis=AX.X)
            e.drain()
            e.tensor_mul(rtmp[:], ps_r[:, 64:128], mc[:, NE:2 * NE])
            e.drain()
            return e.reduce_sum(park[:, i * 6 + 3:i * 6 + 4], rtmp[:],
                                axis=AX.X)
        p.body("vector", ext_fn)

    def tt_fn(e):
        e.tensor_copy(grow[:, 0:1], ps_c2[0:64, 0:1])
        e.drain()
        e.tensor_scalar(thrt[:], thr64[:], grow[:, 0:1], None, ALU.is_lt)
        e.drain()
        return e.reduce_sum(grow[:, 4:5], thrt[:], axis=AX.X)
    p.body("vector", tt_fn)

    def bt_fn(e):
        e.matmul(ps_c[0:1, 128:192], grow[0:64, 4:5], u64[:], start=True,
                 stop=True)
        return e.matmul(ps_c[0:64, 192:193], u64[:], grow[0:64, 4:5],
                        start=True, stop=True)
    p.body("tensor", bt_fn)

    def tid_fn(e):
        e.tensor_copy(brow_t[:], ps_c[0:1, 128:192])
        e.tensor_copy(grow[:, 5:6], ps_c[0:64, 192:193])
        e.drain()
        return e.tensor_scalar(tidw[0:64, 0:TK], iotaT[:], grow[:, 5:6],
                               None, ALU.is_ge)
    p.body("vector", tid_fn)

    def tid2_fn(e):
        return e.matmul(ps_c[0:TK, 256:257], tidw[0:64, 0:TK], onec64[:],
                        start=True, stop=True)
    p.body("tensor", tid2_fn)

    def tid3_fn(e):
        e.tensor_scalar(tidw[0:TK, TK:TK + 1], ps_c[0:TK, 256:257], 1.0,
                        None, ALU.subtract)
        e.drain()
        return e.tensor_copy(wgi[:], tidw[0:TK, TK:TK + 1])
    p.body("vector", tid3_fn)

    wexp_sb = p.sb("wexp_sb", [128, 64 * 128], BF16)
    for (wsrc, wdst) in wexp:
        rowlen = wsrc.shape[1] * wsrc.shape[2]

        def wexp_fn(e, wsrc=wsrc, rowlen=rowlen):
            p.inc("gp", e.indirect_dma_start(
                out=wexp_sb[0:TK, 0:rowlen], out_offset=None,
                in_=wsrc.rearrange("e r c -> e (r c)"),
                in_offset=IndirectOffsetOnAxis(wgi[:], 0),
                bounds_check=NE - 1, oob_is_err=False), 16)
        p.body("gpsimd", wexp_fn)

        def wst_fn(e, wdst=wdst, rowlen=rowlen):
            p.inc("st", e.dma_start(out=wdst, in_=wexp_sb[0:TK, 0:rowlen]), 16)
        p.body("gpsimd", wst_fn)

    def bb_fn(e):
        return e.matmul(ps_c[:, 384:448], onesf[:], brow_t[:], start=True,
                        stop=True)
    p.body("tensor", bb_fn)

    def bb_ev(e):
        return e.tensor_copy(bb128[:], ps_c[:, 384:448])
    p.body("vector", bb_ev)

    for i in range(NT):
        def sl_fn(e, i=i):
            ins = None
            for k in range(2):
                e.tensor_scalar(mc[:, 0:NE], iota64[:],
                                park[:, i * 6 + k:i * 6 + k + 1], None,
                                ALU.is_equal)
                e.drain()
                e.tensor_mul(rtmp[:], bb128[:], mc[:, 0:NE])
                e.drain()
                e.reduce_sum(grow[0:128, 6:7] if False else efp[:, 6:7],
                             rtmp[:], axis=AX.X)
                e.drain()
                ins = e.tensor_scalar(slotc[:, k:k + 1], efp[:, 6:7], 128.0,
                                      park[:, i * 6 + 2 + k:i * 6 + 3 + k],
                                      ALU.mult, ALU.add)
            return ins
        p.body("vector", sl_fn)

        def sst_fn(e, i=i):
            p.inc("st", e.dma_start(out=slot_d[i * 128:(i + 1) * 128, :],
                                    in_=slotc[:]), 16)
        p.body("gpsimd", sst_fn)

    def pst_fn(e):
        p.inc("st", e.dma_start(out=park_d, in_=park[:]), 16)
    p.body("gpsimd", pst_fn)


def emit_wprep_up(p, io, consts):
    nc = p.nc
    ones, onesf, onecolf, identity = consts
    aup_bf = nc.dram_tensor("aup_bf", [NE, 64, 32], BF16, kind="Internal").ap()
    bup_bf = nc.dram_tensor("bup_bf", [NE, 64, 32], BF16, kind="Internal").ap()
    btz_d = nc.dram_tensor("btz_d", [NE, 64, 128], BF16, kind="Internal").ap()
    atz_d = nc.dram_tensor("atz_d", [NE, 64, 128], BF16, kind="Internal").ap()

    def cast_fn(e):
        p.inc("gp", e.dma_start(out=aup_bf, in_=io["A_up"]), 16)
        p.inc("gp", e.dma_start(out=bup_bf, in_=io["B_up"]), 16)
    p.body("gpsimd", cast_fn)

    with ExitStack() as ctx:
        e_ = ctx.enter_context
        wld = e_(nc.sbuf_tensor(_u("mw_ld"), [128, 64], BF16))
        flat32 = e_(nc.sbuf_tensor(_u("mw_f32"), [32, NE * 64], BF16))
        zz = e_(nc.sbuf_tensor(_u("mw_zz"), [128, 128], BF16))
        ps_w = e_(nc.psum_tensor(_u("mw_ps"), [128, 1024], BF16))

        def z_fn(e):
            return e.memset(zz[:], 0.0)
        p.body("gpsimd", z_fn)

        def zfill_fn(e):
            for tgt in (btz_d, atz_d):
                flat = tgt.rearrange("e r c -> (e r) c")
                for r0 in range(0, NE * 64, 128):
                    p.inc("st", e.dma_start(out=flat[r0:r0 + 128, :],
                                            in_=zz[:]), 16)
        p.body("gpsimd", zfill_fn)

        for src, tgt in ((bup_bf, btz_d), (aup_bf, atz_d)):
            for pair in range(32):
                def ld_fn(e, src=src, pair=pair):
                    p.inc("ld", e.dma_start(
                        out=wld[:, 0:32],
                        in_=src[2 * pair:2 * pair + 2]
                            .rearrange("e r c -> (e r) c")), 16)
                p.body("sync", ld_fn)

                def tr_fn(e, pair=pair):
                    return e.transpose(
                        ps_w[0:32, (pair % 8) * 128:(pair % 8 + 1) * 128],
                        wld[:, 0:32], identity[:])
                p.body("tensor", tr_fn)

                def ev_fn(e, pair=pair):
                    return e.activation(
                        flat32[0:32, pair * 128:(pair + 1) * 128],
                        ps_w[0:32, (pair % 8) * 128:(pair % 8 + 1) * 128],
                        AF.Copy)
                p.body("scalar", ev_fn)

            def st_fn(e, tgt=tgt):
                p.inc("st", e.dma_start(
                    out=tgt[:, 0:32, 0:64].rearrange("e j q -> j e q"),
                    in_=flat32[0:32, :].rearrange("j (e q) -> j e q", e=NE)), 16)
                p.inc("st", e.dma_start(
                    out=tgt[:, 32:64, 64:128].rearrange("e j q -> j e q"),
                    in_=flat32[0:32, :].rearrange("j (e q) -> j e q", e=NE)), 16)
            p.body("gpsimd", st_fn)
    return btz_d, atz_d


def emit_wprep_dn(p, io, consts):
    nc = p.nc
    ones, onesf, onecolf, identity = consts
    adn_bf = nc.dram_tensor("adn_bf", [NE, 32, 64], BF16, kind="Internal").ap()
    bdn_bf = nc.dram_tensor("bdn_bf", [NE, 32, 64], BF16, kind="Internal").ap()
    bdt_d = nc.dram_tensor("bdt_d", [NE, 64, 32], BF16, kind="Internal").ap()
    adt_d = nc.dram_tensor("adt_d", [NE, 64, 32], BF16, kind="Internal").ap()

    def cast_fn(e):
        p.inc("gp", e.dma_start(out=adn_bf, in_=io["A_down"]), 16)
        p.inc("gp", e.dma_start(out=bdn_bf, in_=io["B_down"]), 16)
    p.body("gpsimd", cast_fn)

    with ExitStack() as ctx:
        e_ = ctx.enter_context
        wld = e_(nc.sbuf_tensor(_u("mw_ld"), [128, 64], BF16))
        flat64 = e_(nc.sbuf_tensor(_u("mw_f64"), [64, NE * 32], BF16))
        ps_w = e_(nc.psum_tensor(_u("mw_ps"), [128, 1024], BF16))
        for src, tgt in ((bdn_bf, bdt_d), (adn_bf, adt_d)):
            for quad in range(16):
                def ld_fn(e, src=src, quad=quad):
                    p.inc("ld", e.dma_start(
                        out=wld[:, 0:64],
                        in_=src[4 * quad:4 * quad + 4]
                            .rearrange("e r c -> (e r) c")), 16)
                p.body("sync", ld_fn)

                def tr_fn(e, quad=quad):
                    return e.transpose(
                        ps_w[0:64, (quad % 8) * 128:(quad % 8 + 1) * 128],
                        wld[:, 0:64], identity[:])
                p.body("tensor", tr_fn)

                def ev_fn(e, quad=quad):
                    return e.activation(
                        flat64[0:64, quad * 128:(quad + 1) * 128],
                        ps_w[0:64, (quad % 8) * 128:(quad % 8 + 1) * 128],
                        AF.Copy)
                p.body("scalar", ev_fn)

            def st_fn(e, tgt=tgt):
                p.inc("st", e.dma_start(
                    out=tgt.rearrange("e j q -> j e q"),
                    in_=flat64[0:64, :].rearrange("j (e q) -> j e q", e=NE)), 16)
            p.body("gpsimd", st_fn)
    return bdt_d, adt_d


def build_moe_route_up(p, io):
    consts = emit_consts(p)
    nc = p.nc
    wup_bf = nc.dram_tensor("wup_bf", [NE, D], BF16, kind="Internal").ap()

    def cast_fn(e):
        p.inc("gp", e.dma_start(out=wup_bf, in_=io["W_up"]), 16)
    p.body("gpsimd", cast_fn)

    wupT = p.sb("wupT", [128, NC * NE], BF16)
    load_wT(p, wupT, wup_bf, NE, NC)
    btz_d, atz_d = emit_wprep_up(p, io, consts)

    xnt = p.sb("mr_x", [128, D], BF16)

    def mk_logits(i, ps_r, lg_sb):
        def ld_fn(e, i=i):
            p.inc("ld", e.dma_start(
                out=xnt[:].rearrange("q (c t) -> q c t", c=NC),
                in_=io["xn1fm"][:, :, i * 128:(i + 1) * 128]), 16)
        p.body("sync", ld_fn)

        def lg_fn(e):
            for c in range(NC):
                ins = e.matmul(ps_r[:, 0:64], xnt[:, c * 128:(c + 1) * 128],
                               wupT[:, c * NE:(c + 1) * NE],
                               start=(c == 0), stop=(c == NC - 1))
            return ins
        p.body("tensor", lg_fn)

        def lg_ev(e):
            return e.tensor_copy(lg_sb[:], ps_r[:, 0:64])
        p.body("vector", lg_ev)

    io2 = dict(io)
    io2["mk_logits"] = mk_logits
    emit_routing(p, io2, consts, "up", io["slot_up"], io["park_up"],
                 [(btz_d, io["btl_up"]), (atz_d, io["atl_up"])])


def build_moe_sort(p, io, xw, c0, c1, fill_packed):
    nc = p.nc
    slot_d = io["slot_d"]
    park_d = io["park_d"]
    xrow = io["xrow"]
    xs_d = io["xs_d"]
    packed_d = io["packed_d"]
    if not fill_packed:
        def pt_fn(e):
            p.inc("st", e.dma_start(out=packed_d, in_=io["packed_in"]), 16)
            nrow = TK * 128
            step = nrow // 4
            for r0 in range(0, nrow, step):
                p.inc("st", e.dma_start(out=xs_d[r0:r0 + step, :],
                                        in_=io["xs_in"][r0:r0 + step, :]), 16)
        p.body("gpsimd", pt_fn)
    park = p.sb("park", [128, NT * 6], F32)
    tokio = p.sb("tokio", [128, 1], F32)
    pkq = p.sb("pkq", [128, 4], F32)
    pki = p.sb("pki", [128, 1], I32)
    slotc = p.sb("slotc", [128, 2], F32)
    xtile = p.sb("sx_x", [128, xw], BF16)
    xptile = p.sb("sx_xp", [128, xw], BF16)
    zf = p.sb("zf", [128, 508], F32)

    def cst_fn(e):
        e.iota(tokio[:], pattern=[[0, 1]], base=0, channel_multiplier=1,
               allow_small_or_imprecise_dtypes=True)
        if fill_packed:
            return e.memset(zf[:], 1e9)
        return e.memset(zf[0:1, 0:1], 0.0)
    p.body("gpsimd", cst_fn)

    if fill_packed:
        def fp_fn(e):
            p.inc("st", e.dma_start(
                out=packed_d.rearrange("(q r) c -> q (r c)", q=128),
                in_=zf[:]), 16)
        p.body("gpsimd", fp_fn)

    def ldp_fn(e):
        p.inc("ld", e.dma_start(out=park[:], in_=park_d), 16)
    p.body("sync", ldp_fn)

    for i in range(c0, c1):
        def ldx_fn(e, i=i):
            p.inc("ld", e.dma_start(out=xtile[:],
                                    in_=xrow[i * 128:(i + 1) * 128, :]), 16)
            p.inc("ld", e.dma_start(out=slotc[:],
                                    in_=slot_d[i * 128:(i + 1) * 128, :]), 16)
        p.body("sync", ldx_fn)

        for k in range(2):
            def pk_fn(e, i=i, k=k):
                e.tensor_scalar(pkq[:, 0:1], tokio[:], float(i * 128), None,
                                ALU.add)
                e.tensor_scalar(pkq[:, 1:2], tokio[:],
                                float(i * 128 + k * S), None, ALU.add)
                e.tensor_copy(pkq[:, 2:3],
                              park[:, i * 6 + 4 + k:i * 6 + 5 + k])
                e.tensor_scalar(pkq[:, 3:4], tokio[:], 0.0, None, ALU.mult)
                e.tensor_copy(pki[:], slotc[:, k:k + 1])
                e.drain()
                return e.tensor_scalar(
                    xptile[:], xtile[:],
                    park[:, i * 6 + 4 + k:i * 6 + 5 + k], None, ALU.mult)
            p.body("vector", pk_fn)

            def sc_fn(e, i=i, k=k):
                p.inc("gp", e.indirect_dma_start(
                    out=packed_d, out_offset=IndirectOffsetOnAxis(pki[:], 0),
                    in_=pkq[:], in_offset=None,
                    bounds_check=TK * 128 - 1, oob_is_err=False), 16)
                p.inc("gp", e.indirect_dma_start(
                    out=xs_d, out_offset=IndirectOffsetOnAxis(pki[:], 0),
                    in_=xptile[:], in_offset=None,
                    bounds_check=TK * 128 - 1, oob_is_err=False), 16)
            p.body("gpsimd", sc_fn)


def build_moe_tiles_seg(p, io, stage, j0, j1, prezero):
    nc = p.nc
    consts = emit_consts(p)
    ones, onesf, onecolf, identity = consts
    up = (stage == "up")
    XW = D if up else FF
    YW = FF if up else D
    packed_d = io["packed_d"]
    xs_d = io["xs_d"]
    btl_d = io["btl_d"]
    atl_d = io["atl_d"]
    ypl_d = io["ypl_d"]
    zzb = p.sb("zzb", [128, 8192], BF16)

    if not prezero:
        def pt_fn(e):
            nrow = 2 * S
            step = nrow // 4
            for r0 in range(0, nrow, step):
                p.inc("st", e.dma_start(out=ypl_d[r0:r0 + step, :],
                                        in_=io["ypl_in"][r0:r0 + step, :]), 16)
        p.body("gpsimd", pt_fn)

    if prezero:
        def z0_fn(e):
            return e.memset(zzb[:], 0.0)
        p.body("gpsimd", z0_fn)
        def zp_fn(e):
            step = 8192 // YW * 128
            for r0 in range(0, 2 * S, 128):
                p.inc("st", e.dma_start(out=ypl_d[r0:r0 + 128, :],
                                        in_=zzb[:, 0:YW]), 16)
        p.body("gpsimd", zp_fn)

    with ExitStack() as ctx:
        e_ = ctx.enter_context
        pk = e_(nc.sbuf_tensor(_u("mt_pk"), [128, 4], F32))
        pki = e_(nc.sbuf_tensor(_u("mt_pki"), [128, 1], I32))
        xg = e_(nc.sbuf_tensor(_u("mt_xg"), [128, XW], BF16))
        xt = e_(nc.sbuf_tensor(_u("mt_xt"), [128, 2 * XW if up else XW], BF16))
        wb = e_(nc.sbuf_tensor(_u("mt_wb"), [128, 128], BF16))
        wa = e_(nc.sbuf_tensor(_u("mt_wa"), [128, 128], BF16))
        tsb = e_(nc.sbuf_tensor(_u("mt_tsb"), [128, 2048], BF16))
        ttb = e_(nc.sbuf_tensor(_u("mt_ttb"), [128, 2048], BF16))
        yrow = e_(nc.sbuf_tensor(_u("mt_yr"), [128, YW], BF16))
        ps_t1 = e_(nc.psum_tensor(_u("mt_p1"), [128, 1024], BF16))
        ps_ta = e_(nc.psum_tensor(_u("mt_pa"), [128, 512], F32))
        ps_tb = e_(nc.psum_tensor(_u("mt_pb"), [128, 512], F32))
        ps_t2 = e_(nc.psum_tensor(_u("mt_p2"), [128, 1024], BF16))
        ps_ya = e_(nc.psum_tensor(_u("mt_ya"), [128, 512], F32))
        ps_yb = e_(nc.psum_tensor(_u("mt_yb"), [128, 512], F32))

        for j in range(j0, j1):
            def ld_fn(e, j=j):
                p.inc("ld", e.dma_start(
                    out=pk[:], in_=packed_d[j * 128:(j + 1) * 128, :]), 16)
                p.inc("ld", e.dma_start(
                    out=xg[:], in_=xs_d[j * 128:(j + 1) * 128, :]), 16)
                if up:
                    p.inc("ld", e.dma_start(
                        out=wb[0:64, :],
                        in_=btl_d[j].rearrange("(r c) -> r c", r=64)), 16)
                    for hb in range(2):
                        p.inc("ld", e.dma_start(
                            out=wa[hb * 64:(hb + 1) * 64, :],
                            in_=atl_d[j].rearrange("(r c) -> r c", r=64)), 16)
                else:
                    for hb in range(2):
                        p.inc("ld", e.dma_start(
                            out=wb[hb * 64:(hb + 1) * 64, 0:32],
                            in_=btl_d[j].rearrange("(r c) -> r c", r=64)), 16)
                        p.inc("ld", e.dma_start(
                            out=wa[hb * 64:(hb + 1) * 64, 0:32],
                            in_=atl_d[j].rearrange("(r c) -> r c", r=64)), 16)
            p.body("sync", ld_fn)

            def cvt_fn(e):
                return e.tensor_copy(pki[:], pk[:, 1:2])
            p.body("vector", cvt_fn)

            if up:
                for w in range(2):
                    def t1_fn(e, w=w):
                        ins = None
                        for q in range(8):
                            i2 = w * 8 + q
                            ins = e.transpose(
                                ps_t1[0:64, q * 128:(q + 1) * 128],
                                xg[:, i2 * 64:(i2 + 1) * 64], identity[:])
                        return ins
                    p.body("tensor", t1_fn)

                    def t1_ev(e, w=w):
                        return e.activation(xt[0:64, w * 1024:(w + 1) * 1024],
                                            ps_t1[0:64, :], AF.Copy)
                    p.body("scalar", t1_ev)
            else:
                for w in range(4):
                    def t1_fn(e, w=w):
                        ins = None
                        for q in range(8):
                            c = w * 8 + q
                            ins = e.transpose(
                                ps_t1[:, q * 128:(q + 1) * 128],
                                xg[:, c * 128:(c + 1) * 128], identity[:])
                        return ins
                    p.body("tensor", t1_fn)

                    def t1_ev(e, w=w):
                        return e.activation(xt[:, w * 1024:(w + 1) * 1024],
                                            ps_t1[:], AF.Copy)
                    p.body("scalar", t1_ev)

            if up:
                for w in range(4):
                    def s1_fn(e, w=w):
                        ins = None
                        for q in range(8):
                            ii = w * 8 + q
                            i2, v = ii // 2, ii % 2
                            ins = e.matmul(
                                ps_ta[:, q * 64:(q + 1) * 64],
                                xt[0:64, i2 * 128:(i2 + 1) * 128],
                                wb[0:64, v * 64:(v + 1) * 64],
                                start=True, stop=True)
                        return ins
                    p.body("tensor", s1_fn)

                    def s1_ev(e, w=w):
                        return e.tensor_copy(
                            tsb[:].rearrange("q (o i) -> q o i", i=32)
                                  [:, :, w * 8:(w + 1) * 8],
                            ps_ta[:, 0:512].rearrange("q (i o) -> q o i", i=8))
                    p.body("vector", s1_ev)
            else:
                for w in range(4):
                    par2 = w % 2
                    psu = ps_ta if par2 == 0 else ps_tb
                    base = 64 * par2

                    def s1_fn(e, w=w, psu=psu, base=base):
                        ins = None
                        for q in range(16):
                            c = (w // 2) * 16 + q
                            ins = e.matmul(
                                psu[:, q * 32:(q + 1) * 32],
                                xt[base:base + 64, c * 128:(c + 1) * 128],
                                wb[base:base + 64, 0:32],
                                start=True, stop=True)
                        return ins
                    p.body("tensor", s1_fn)

                    def s1_ev(e, w=w, psu=psu):
                        return e.tensor_copy(
                            tsb[:].rearrange("q (o i2 r) -> q o i2 r",
                                             i2=32, r=2)
                                  [:, :, (w // 2) * 16:(w // 2) * 16 + 16,
                                   (w % 2):(w % 2) + 1]
                                  .rearrange("q o i2 r -> q o (i2 r)"),
                            psu[:, 0:512].rearrange("q (i o) -> q o i", i=16))
                    p.body("vector", s1_ev)

            for w in range(2):
                def t2_fn(e, w=w):
                    ins = None
                    for q in range(8):
                        c = w * 8 + q
                        ins = e.transpose(ps_t2[:, q * 128:(q + 1) * 128],
                                          tsb[:, c * 128:(c + 1) * 128],
                                          identity[:])
                    return ins
                p.body("tensor", t2_fn)

                def t2_ev(e, w=w):
                    return e.activation(ttb[:, w * 1024:(w + 1) * 1024],
                                        ps_t2[:], AF.Copy)
                p.body("scalar", t2_ev)

            if up:
                for g in range(8):
                    def s2_fn(e, g=g):
                        ins = None
                        for b2 in range(2):
                            blk = g * 2 + b2
                            for v in range(2):
                                sl = b2 * 2 + v
                                ins = e.matmul(
                                    ps_ya[:, sl * 64:(sl + 1) * 64],
                                    ttb[0:64, blk * 128:(blk + 1) * 128],
                                    wa[0:64, v * 64:(v + 1) * 64],
                                    start=True, stop=True)
                        for b2 in range(2):
                            blk = g * 2 + b2
                            for v in range(2):
                                sl = b2 * 2 + v
                                ins = e.matmul(
                                    ps_yb[:, sl * 64:(sl + 1) * 64],
                                    ttb[64:128, blk * 128:(blk + 1) * 128],
                                    wa[64:128, v * 64:(v + 1) * 64],
                                    start=True, stop=True)
                        return ins
                    p.body("tensor", s2_fn)

                    def s2_ev(e, g=g):
                        e.tensor_copy(
                            yrow[:].rearrange("q (o pp) -> q o pp", pp=64)
                                   [:, :, 8 * g:8 * g + 8]
                                   .rearrange("q o (b2 vv) -> q o b2 vv", b2=2)
                                   [:, :, :, 0:2],
                            ps_ya[:, 0:256]
                                .rearrange("q (b2 v o) -> q o b2 v", b2=2, v=2))
                        e.drain()
                        return e.tensor_copy(
                            yrow[:].rearrange("q (o pp) -> q o pp", pp=64)
                                   [:, :, 8 * g:8 * g + 8]
                                   .rearrange("q o (b2 vv) -> q o b2 vv", b2=2)
                                   [:, :, :, 2:4],
                            ps_yb[:, 0:256]
                                .rearrange("q (b2 v o) -> q o b2 v", b2=2, v=2))
                    p.body("vector", s2_ev)
            else:
                for w in range(2):
                    psu = ps_ya if w == 0 else ps_yb
                    base = 64 * w

                    def s2_fn(e, w=w, psu=psu, base=base):
                        ins = None
                        for q in range(16):
                            ins = e.matmul(
                                psu[:, q * 32:(q + 1) * 32],
                                ttb[base:base + 64, q * 128:(q + 1) * 128],
                                wa[base:base + 64, 0:32],
                                start=True, stop=True)
                        return ins
                    p.body("tensor", s2_fn)

                    def s2_ev(e, w=w, psu=psu):
                        return e.tensor_copy(
                            yrow[:].rearrange("q (o p2 r) -> q o p2 r",
                                              p2=16, r=2)
                                   [:, :, :, w:w + 1]
                                   .rearrange("q o p2 r -> q o (p2 r)"),
                            psu[:, 0:512].rearrange("q (i o) -> q o i", i=16))
                    p.body("vector", s2_ev)

            def sc_fn(e, j=j):
                p.inc("gp", e.indirect_dma_start(
                    out=ypl_d, out_offset=IndirectOffsetOnAxis(pki[:], 0),
                    in_=yrow[:], in_offset=None,
                    bounds_check=2 * S - 1, oob_is_err=False), 16)
            p.body("gpsimd", sc_fn)


def build_moe_combine_up(p, io):
    nc = p.nc
    consts = emit_consts(p)
    ones, onesf, onecolf, identity = consts
    wdn_bf = nc.dram_tensor("wdn_bf", [NE, FF], BF16, kind="Internal").ap()

    def cast_fn(e):
        p.inc("gp", e.dma_start(out=wdn_bf, in_=io["W_down"]), 16)
    p.body("gpsimd", cast_fn)

    wdnT = p.sb("wdnT", [128, NF * NE], BF16)
    load_wT(p, wdnT, wdn_bf, NE, NF)
    bdt_d, adt_d = emit_wprep_dn(p, io, consts)

    sclcol = p.sb("sclcol", [128, 1], F32)
    biasup = p.sb("biasup", [128, FF], F32)
    with ExitStack() as ctx:
        e_ = ctx.enter_context
        srow = e_(nc.sbuf_tensor(_u("mb_s"), [1, 1], F32))
        burow = e_(nc.sbuf_tensor(_u("mb_u"), [1, FF], F32))
        ps_bc = e_(nc.psum_tensor(_u("mb_ps"), [128, 512], F32))

        def ld_sb(e):
            p.inc("ld", e.dma_start(out=srow[:], in_=io["scale_up"]), 16)
            p.inc("ld", e.dma_start(out=burow[:], in_=io["bias_up"]), 16)
        p.body("sync", ld_sb)

        def bc_fn(e):
            return e.matmul(ps_bc[:, 0:1], onesf[:], srow[:], start=True,
                            stop=True)
        p.body("tensor", bc_fn)

        def bc_ev(e):
            return e.tensor_copy(sclcol[:], ps_bc[:, 0:1])
        p.body("vector", bc_ev)

        for j in range(FF // 512):
            def bb_fn(e, j=j):
                return e.matmul(ps_bc[:, 0:512], onesf[:],
                                burow[:, j * 512:(j + 1) * 512],
                                start=True, stop=True)
            p.body("tensor", bb_fn)

            def bb_ev(e, j=j):
                return e.tensor_copy(biasup[:, j * 512:(j + 1) * 512],
                                     ps_bc[:, 0:512])
            p.body("vector", bb_ev)

    y0 = p.sb("cu_y0", [128, FF], BF16)
    y1 = p.sb("cu_y1", [128, FF], BF16)
    hsum = p.sb("cu_s", [128, FF], F32)
    hbf = p.sb("cu_h", [128, FF], BF16)
    hfm_t = p.sb("cu_hf", [128, FF], BF16)
    ps_tr2 = p.ps("cu_ps", [128, 1024], BF16)

    def mk_logits(i, ps_r, lg_sb):
        def ld_fn(e, i=i):
            p.inc("ld", e.dma_start(
                out=y0[:], in_=io["ypl_up"][i * 128:(i + 1) * 128, :]), 16)
            p.inc("ld", e.dma_start(
                out=y1[:], in_=io["ypl_up"][S + i * 128:S + (i + 1) * 128, :]), 16)
        p.body("sync", ld_fn)

        def cmb_fn(e):
            e.tensor_add(hsum[:], y0[:], y1[:])
            e.drain()
            return e.scalar_tensor_tensor(hsum[:], hsum[:], sclcol[:],
                                          biasup[:], ALU.mult, ALU.add)
        p.body("vector", cmb_fn)

        def gel_fn(e):
            return e.activation(hbf[:], hsum[:], AF.Gelu)
        p.body("scalar", gel_fn)

        def sth_fn(e, i=i):
            p.inc("st", e.dma_start(out=io["hrow_d"][i * 128:(i + 1) * 128, :],
                                    in_=hbf[:]), 16)
        p.body("gpsimd", sth_fn)

        for half in range(4):
            def tr_fn(e, half=half):
                ins = None
                for c in range(8):
                    cc = half * 8 + c
                    ins = e.transpose(ps_tr2[:, c * 128:(c + 1) * 128],
                                      hbf[:, cc * 128:(cc + 1) * 128],
                                      identity[:])
                return ins
            p.body("tensor", tr_fn)

            def tr_ev(e, half=half):
                return e.activation(hfm_t[:, half * 1024:(half + 1) * 1024],
                                    ps_tr2[:], AF.Copy)
            p.body("scalar", tr_ev)

        def lg_fn(e):
            for c in range(NF):
                ins = e.matmul(ps_r[:, 0:64], hfm_t[:, c * 128:(c + 1) * 128],
                               wdnT[:, c * NE:(c + 1) * NE],
                               start=(c == 0), stop=(c == NF - 1))
            return ins
        p.body("tensor", lg_fn)

        def lg_ev(e):
            return e.tensor_copy(lg_sb[:], ps_r[:, 0:64])
        p.body("vector", lg_ev)

    io2 = dict(io)
    io2["mk_logits"] = mk_logits
    emit_routing(p, io2, consts, "dn", io["slot_dn"], io["park_dn"],
                 [(bdt_d, io["btl_dn"]), (adt_d, io["atl_dn"])])


def build_moe_final(p, io):
    nc = p.nc
    consts = emit_consts(p)
    ones, onesf, onecolf, identity = consts
    sclcol = p.sb("sclcol", [128, 1], F32)
    biasdn = p.sb("biasdn", [128, D], F32)
    with ExitStack() as ctx:
        e_ = ctx.enter_context
        srow = e_(nc.sbuf_tensor(_u("mb_s"), [1, 1], F32))
        bdrow = e_(nc.sbuf_tensor(_u("mb_d"), [1, D], F32))
        ps_bc = e_(nc.psum_tensor(_u("mb_ps"), [128, 512], F32))

        def ld_sb(e):
            p.inc("ld", e.dma_start(out=srow[:], in_=io["scale_down"]), 16)
            p.inc("ld", e.dma_start(out=bdrow[:], in_=io["bias_down"]), 16)
        p.body("sync", ld_sb)

        def bc_fn(e):
            return e.matmul(ps_bc[:, 0:1], onesf[:], srow[:], start=True,
                            stop=True)
        p.body("tensor", bc_fn)

        def bc_ev(e):
            return e.tensor_copy(sclcol[:], ps_bc[:, 0:1])
        p.body("vector", bc_ev)

        for j in range(D // 512):
            def bb_fn(e, j=j):
                return e.matmul(ps_bc[:, 0:512], onesf[:],
                                bdrow[:, j * 512:(j + 1) * 512],
                                start=True, stop=True)
            p.body("tensor", bb_fn)

            def bb_ev(e, j=j):
                return e.tensor_copy(biasdn[:, j * 512:(j + 1) * 512],
                                     ps_bc[:, 0:512])
            p.body("vector", bb_ev)

    y0 = p.sb("cf_y0", [128, D], BF16)
    y1 = p.sb("cf_y1", [128, D], BF16)
    rr = p.sb("cf_r", [128, D], F32)
    acc = p.sb("cf_a", [128, D], F32)
    ob = [p.sb("cf_o", [128, D], F32) for _ in range(2)]
    for i in range(NT):
        par = i % 2

        def ld_fn(e, i=i):
            p.inc("ld", e.dma_start(
                out=y0[:], in_=io["ypl_dn"][i * 128:(i + 1) * 128, :]), 16)
            p.inc("ld", e.dma_start(
                out=y1[:], in_=io["ypl_dn"][S + i * 128:S + (i + 1) * 128, :]), 16)
            p.inc("ld", e.dma_start(out=rr[:],
                                    in_=io["r2"][i * 128:(i + 1) * 128, :]), 16)
        p.body("sync", ld_fn)

        def cmb_fn(e, par=par):
            e.tensor_add(acc[:], y0[:], y1[:])
            e.drain()
            e.scalar_tensor_tensor(acc[:], acc[:], sclcol[:], biasdn[:],
                                   ALU.mult, ALU.add)
            e.drain()
            return e.tensor_add(ob[par][:], acc[:], rr[:])
        p.body("vector", cmb_fn)

        def st_fn(e, i=i, par=par):
            p.inc("st", e.dma_start(out=io["out"][i * 128:(i + 1) * 128, :],
                                    in_=ob[par][:]), 16)
        p.body("gpsimd", st_fn)


def _moe_seg_program(seg, **kw):
    p = Prog()
    nc = p.nc
    io = {}

    def din(nm, shape, dt=F32):
        io[nm] = nc.dram_tensor(nm, shape, dt, kind="ExternalInput").ap()

    def dout(nm, shape, dt=F32):
        io[nm] = nc.dram_tensor(nm, shape, dt, kind="ExternalOutput").ap()

    if seg == "route_up":
        din("xn1fm", [128, NC, S], BF16)
        din("W_up", [NE, D]); din("A_up", [NE, 64, 32]); din("B_up", [NE, 64, 32])
        dout("slot_up", [S, 2]); dout("park_up", [128, NT * 6])
        dout("btl_up", [TK, 64 * 128], BF16); dout("atl_up", [TK, 64 * 128], BF16)
    elif seg == "sort_up":
        din("slot_d", [S, 2]); din("park_d", [128, NT * 6])
        din("xrow", [S, D], BF16)
        if kw.get("c0", 0) > 0:
            din("packed_in", [TK * 128, 4]); din("xs_in", [TK * 128, D], BF16)
        dout("packed_d", [TK * 128, 4]); dout("xs_d", [TK * 128, D], BF16)
    elif seg == "tiles_up":
        din("packed_d", [TK * 128, 4]); din("xs_d", [TK * 128, D], BF16)
        din("btl_d", [TK, 64 * 128], BF16); din("atl_d", [TK, 64 * 128], BF16)
        if kw.get("j0", 0) > 0:
            din("ypl_in", [2 * S, FF], BF16)
        dout("ypl_d", [2 * S, FF], BF16)
    elif seg == "combine_up":
        din("ypl_up", [2 * S, FF], BF16)
        din("scale_up", [1, 1]); din("bias_up", [1, FF])
        din("W_down", [NE, FF]); din("A_down", [NE, 32, 64]); din("B_down", [NE, 32, 64])
        dout("hrow_d", [S, FF], BF16)
        dout("slot_dn", [S, 2]); dout("park_dn", [128, NT * 6])
        dout("btl_dn", [TK, 64 * 32], BF16); dout("atl_dn", [TK, 64 * 32], BF16)
    elif seg == "sort_dn":
        din("slot_d", [S, 2]); din("park_d", [128, NT * 6])
        din("xrow", [S, FF], BF16)
        if kw.get("c0", 0) > 0:
            din("packed_in", [TK * 128, 4]); din("xs_in", [TK * 128, FF], BF16)
        dout("packed_d", [TK * 128, 4]); dout("xs_d", [TK * 128, FF], BF16)
    elif seg == "tiles_dn":
        din("packed_d", [TK * 128, 4]); din("xs_d", [TK * 128, FF], BF16)
        din("btl_d", [TK, 64 * 32], BF16); din("atl_d", [TK, 64 * 32], BF16)
        if kw.get("j0", 0) > 0:
            din("ypl_in", [2 * S, D], BF16)
        dout("ypl_d", [2 * S, D], BF16)
    elif seg == "final":
        din("ypl_dn", [2 * S, D], BF16); din("r2", [S, D])
        din("scale_down", [1, 1]); din("bias_down", [1, D])
        dout("out", [S, D])

    with p.ctx:
        p.blk = p.ctx.enter_context(nc.Block())
        if seg == "route_up":
            build_moe_route_up(p, io)
        elif seg == "sort_up":
            build_moe_sort(p, io, D, kw["c0"], kw["c1"], kw["c0"] == 0)
        elif seg == "tiles_up":
            build_moe_tiles_seg(p, io, "up", kw["j0"], kw["j1"], kw["j0"] == 0)
        elif seg == "combine_up":
            build_moe_combine_up(p, io)
        elif seg == "sort_dn":
            build_moe_sort(p, io, FF, kw["c0"], kw["c1"], kw["c0"] == 0)
        elif seg == "tiles_dn":
            build_moe_tiles_seg(p, io, "dn", kw["j0"], kw["j1"], kw["j0"] == 0)
        elif seg == "final":
            build_moe_final(p, io)
        p.finish()
    return p


def _spmd(p, maps):
    return run_bass_kernel_spmd(p.nc, maps, core_ids=list(range(8))).results


SORT_SPLITS = ((0, 11), (11, 22), (22, 32))
TILE_SPLITS = ((0, 43), (43, 86), (86, TK))


def run_moe(r2_full, xn1fm_full, xn1row_full, inputs):
    def W(nm):
        v = np.ascontiguousarray(inputs["L1_ffn_" + nm], dtype=np.float32)
        if nm.startswith("scale"):
            v = v.reshape(1, 1)
        elif nm.startswith("bias"):
            v = v.reshape(1, -1)
        return v

    def stage(xrow_full, route_fn, sort_seg, tiles_seg):
        slot_d, park_d, btl, atl = route_fn()
        packed, xs = None, None
        for c0, c1 in SORT_SPLITS:
            p = _moe_seg_program(sort_seg, c0=c0, c1=c1)
            maps = []
            for b in range(8):
                m = {"slot_d": slot_d[b], "park_d": park_d[b],
                     "xrow": np.ascontiguousarray(xrow_full[b])}
                if c0 > 0:
                    m["packed_in"] = packed[b]
                    m["xs_in"] = xs[b]
                maps.append(m)
            res = _spmd(p, maps)
            packed = [r["packed_d"] for r in res]
            xs = [r["xs_d"] for r in res]
        ypl = None
        for j0, j1 in TILE_SPLITS:
            p = _moe_seg_program(tiles_seg, j0=j0, j1=j1)
            maps = []
            for b in range(8):
                m = {"packed_d": packed[b], "xs_d": xs[b],
                     "btl_d": btl[b], "atl_d": atl[b]}
                if j0 > 0:
                    m["ypl_in"] = ypl[b]
                maps.append(m)
            res = _spmd(p, maps)
            ypl = [r["ypl_d"] for r in res]
        return ypl

    def route_up():
        p = _moe_seg_program("route_up")
        res = _spmd(p, [{"xn1fm": np.ascontiguousarray(xn1fm_full[b]),
                         "W_up": W("W_up"), "A_up": W("A_up"),
                         "B_up": W("B_up")} for b in range(8)])
        return ([r["slot_up"] for r in res], [r["park_up"] for r in res],
                [r["btl_up"] for r in res], [r["atl_up"] for r in res])

    ypl_up = stage(xn1row_full, route_up, "sort_up", "tiles_up")

    p = _moe_seg_program("combine_up")
    res = _spmd(p, [{"ypl_up": ypl_up[b], "scale_up": W("scale_up"),
                     "bias_up": W("bias_up"), "W_down": W("W_down"),
                     "A_down": W("A_down"), "B_down": W("B_down")}
                    for b in range(8)])
    hrow = [r["hrow_d"] for r in res]
    dn_state = ([r["slot_dn"] for r in res], [r["park_dn"] for r in res],
                [r["btl_dn"] for r in res], [r["atl_dn"] for r in res])

    ypl_dn = stage(hrow, lambda: dn_state, "sort_dn", "tiles_dn")

    p = _moe_seg_program("final")
    res = _spmd(p, [{"ypl_dn": ypl_dn[b],
                     "r2": np.ascontiguousarray(r2_full[b], dtype=np.float32),
                     "scale_down": W("scale_down"),
                     "bias_down": W("bias_down")} for b in range(8)])
    return np.stack([r["out"] for r in res], axis=0)


def build_l0_program():
    p = Prog()
    nc = p.nc
    io = {}
    io["x"] = nc.dram_tensor("x", [S, D], F32, kind="ExternalInput").ap()
    for nm, shape in (("Wq", [D, D]), ("Wk", [D, D]), ("Wv", [D, D]),
                      ("Wo", [D, D]), ("bq", [1, D]), ("bk", [1, D]), ("bv", [1, D]),
                      ("bo", [1, D]), ("W1", [FF, D]), ("b1", [1, FF]),
                      ("W2", [D, FF]), ("b2", [1, D])):
        io[nm] = nc.dram_tensor(nm, shape, F32, kind="ExternalInput").ap()
    io["r1"] = nc.dram_tensor("r1", [S, D], F32, kind="ExternalOutput").ap()
    io["xnfm"] = nc.dram_tensor("xnfm", [128, NC, S], BF16,
                                kind="ExternalOutput").ap()
    with p.ctx:
        p.blk = p.ctx.enter_context(nc.Block())
        build_l0(p, io)
        p.finish()
    return p


L0_KEYS = (("Wq", "L0_attn_Wq"), ("Wk", "L0_attn_Wk"), ("Wv", "L0_attn_Wv"),
           ("Wo", "L0_attn_Wo"), ("bq", "L0_attn_bq"), ("bk", "L0_attn_bk"),
           ("bv", "L0_attn_bv"), ("bo", "L0_attn_bo"), ("W1", "L0_ffn_W1"),
           ("b1", "L0_ffn_b1"), ("W2", "L0_ffn_W2"), ("b2", "L0_ffn_b2"))


def run_l0(inputs):
    x_full = np.ascontiguousarray(inputs["L0_attn_x"], dtype=np.float32)
    p = build_l0_program()
    in_maps = []
    for b in range(8):
        m = {"x": x_full[b]}
        for nm, key in L0_KEYS:
            v = np.ascontiguousarray(inputs[key], dtype=np.float32)
            if nm.startswith("b"):
                v = v.reshape(1, -1)
            m[nm] = v
        in_maps.append(m)
    res = run_bass_kernel_spmd(p.nc, in_maps, core_ids=list(range(8)))
    r1 = np.stack([r["r1"] for r in res.results], axis=0)
    xnfm = np.stack([r["xnfm"] for r in res.results], axis=0)
    return r1, xnfm


def kernel(**inputs):
    r1, _ = run_l0(inputs)
    r2, xn1fm, xn1row = run_l1a(r1, inputs)
    out = run_moe(r2, xn1fm, xn1row, inputs)
    return out
